# version 01
# baseline (speedup 1.0000x reference)
import math
from contextlib import ExitStack

import numpy as np
import ml_dtypes

import concourse.bass as bass
import concourse.mybir as mybir
from concourse.bass_utils import run_bass_kernel_spmd

F32 = mybir.dt.float32
BF16 = mybir.dt.bfloat16
I32 = mybir.dt.int32
AF = mybir.ActivationFunctionType
ALU = mybir.AluOpType
AX = mybir.AxisListType

P = 128
D = 1024
S = 4096
B = 4
HT = 2048
NT = HT // P
NTS = S // P
EXPB = 25.0

BF16NP = ml_dtypes.bfloat16


class Prog:

    ENGINES = ("sync", "tensor", "vector", "scalar", "gpsimd")

    def __init__(self, nc):
        self.nc = nc
        self.ctx = ExitStack()
        self.q = {e: [] for e in self.ENGINES}
        self.cnt = {}

    def sb(self, name, shape, dtype):
        return self.ctx.enter_context(self.nc.sbuf_tensor(name, shape, dtype)).ap()

    def ps(self, name, shape, dtype=F32):
        return self.ctx.enter_context(self.nc.psum_tensor(name, shape, dtype)).ap()

    def sem(self, name):
        h = self.ctx.enter_context(self.nc.semaphore(name))
        self.cnt[name] = 0
        self._sems = getattr(self, "_sems", {})
        self._sems[name] = h
        return h

    def _h(self, sem):
        return self._sems[sem]

    def on(self, eng, fn):
        self.q[eng].append(fn)

    def inc(self, sem, n=1):
        self.cnt[sem] += n
        return self.cnt[sem]

    def wait(self, eng, sem, val=None):
        v = self.cnt[sem] if val is None else val
        h = self._h(sem)
        self.on(eng, lambda e, h=h, v=v: e.wait_ge(h, v))

    def op(self, eng, fn, sem=None, n=1):
        if sem is None:
            self.on(eng, fn)
            return None
        h = self._h(sem)
        v = self.inc(sem, n)
        self.on(eng, lambda e, fn=fn, h=h, n=n: fn(e).then_inc(h, n))
        return v

    def dma(self, eng, out, in_, sem, **kw):
        return self.op(eng, lambda e, o=out, i=in_: e.dma_start(out=o, in_=i, **kw),
                       sem=sem, n=16)

    def build(self, guard_store_sem=None):
        with self.ctx:
            with self.nc.Block() as block:
                for eng in self.ENGINES:
                    cmds = self.q[eng]
                    if not cmds:
                        continue
                    deco = getattr(block, eng)

                    def body(e, cmds=cmds):
                        for fn in cmds:
                            fn(e)

                    deco(body)
        return self.nc


def new_nc():
    return bass.Bass("TRN2", target_bir_lowering=False, debug=False)


def _ident_np():
    return np.eye(P, dtype=BF16NP)


def build_a():
    nc = new_nc()
    xf = nc.dram_tensor("xf", [S, D], F32, kind="ExternalInput").ap()
    wq = nc.dram_tensor("wq", [D, 512], F32, kind="ExternalInput").ap()
    wkvd = nc.dram_tensor("wkvd", [D, 256], F32, kind="ExternalInput").ap()
    wku = nc.dram_tensor("wku", [256, 512], F32, kind="ExternalInput").ap()
    wvu = nc.dram_tensor("wvu", [256, 512], F32, kind="ExternalInput").ap()
    o_xnfm = nc.dram_tensor("xnfm", [8, P, S], BF16, kind="ExternalOutput").ap()
    o_kt = nc.dram_tensor("kt", [4, P, S], BF16, kind="ExternalOutput").ap()
    o_qt = nc.dram_tensor("qt", [4, P, S], BF16, kind="ExternalOutput").ap()
    o_va = nc.dram_tensor("vaug", [32, P, 520], BF16, kind="ExternalOutput").ap()
    ident = nc.inline_tensor(_ident_np(), name="ident").ap()

    p = Prog(nc)
    xb = [p.sb(f"xb{i}", [P, D], F32) for i in range(2)]
    xn16 = [p.sb(f"xn16{i}", [P, D], BF16) for i in range(2)]
    xnfm = p.sb("xnfm_sb", [P, 8, S], BF16)
    lat = p.sb("lat_sb", [P, 2, S], BF16)
    qt_sb = p.sb("qt_sb", [P, 4, S], BF16)
    kstg = [p.sb(f"kstg{i}", [P, S], BF16) for i in range(2)]
    vstg = [p.sb(f"vstg{i}", [P, 520], BF16) for i in range(2)]
    wq_sb = p.sb("wq_sb", [P, 8, 512], BF16)
    wkvd_sb = p.sb("wkvd_sb", [P, 8, 256], BF16)
    wku_sb = p.sb("wku_sb", [P, 2, 512], BF16)
    wvu_sb = p.sb("wvu_sb", [P, 2, 512], BF16)
    id_sb = p.sb("id_sb", [P, P], BF16)
    stt = p.sb("stt_sb", [P, 8], F32)
    sqscr = p.sb("sqscr", [P, D], F32)
    epsc = p.sb("epsc", [P, 1], F32)
    ps_tr = p.ps("ps_tr", [P, D], BF16)
    ps_mm = [p.ps(f"ps_mm{i}", [P, 512], F32) for i in range(2)]
    for s_ in ("lx", "lg", "pe", "ve", "se", "vg", "st"):
        p.sem(s_)

    p.dma("gpsimd", wq_sb, wq.rearrange("(a p) k -> p a k", p=P), "lg")
    p.dma("gpsimd", wkvd_sb, wkvd.rearrange("(a p) k -> p a k", p=P), "lg")
    p.dma("gpsimd", wku_sb, wku.rearrange("(a p) k -> p a k", p=P), "lg")
    p.dma("gpsimd", wvu_sb, wvu.rearrange("(a p) k -> p a k", p=P), "lg")
    p.dma("sync", id_sb, ident, "lx")
    p.op("gpsimd", lambda e: e.memset(epsc, 1e-5), sem="ve")
    p.op("gpsimd", lambda e: e.memset(vstg[0][:, :], 1.0), sem="ve")
    p.op("gpsimd", lambda e: e.memset(vstg[1][:, :], 1.0), sem="ve")
    wload = p.cnt["lg"]

    c = lambda i: stt[:, i:i + 1]
    for i in range(NTS):
        buf = xb[i % 2]
        xn = xn16[i % 2]
        if i >= 2:
            p.wait("sync", "se", None)
        p.dma("sync", buf, xf[i * P:(i + 1) * P, :], "lx")
        ldv = p.cnt["lx"]
        p.wait("vector", "lx", ldv)
        if i > 0:
            p.wait("vector", "se", None)
        p.op("vector", lambda e, b=buf, o=c(0): e.reduce_sum(o, b, axis=AX.X),
             sem="vg")
        p.wait("scalar", "lx", ldv)
        p.wait("scalar", "vg", None)
        p.op("scalar", lambda e, b=buf, o=c(1): e.activation(
            out=sqscr, in_=b, func=AF.Square, accum_out=o), sem="se")
        p.wait("vector", "se", None)
        p.op("vector", lambda e, o=c(2), a=c(0): e.tensor_scalar(
            out=o, in0=a, scalar1=1.0 / D, scalar2=None, op0=ALU.mult), sem="vg")
        p.wait("vector", "vg", None)
        p.op("vector", lambda e, o=c(3), a=c(1): e.tensor_scalar(
            out=o, in0=a, scalar1=1.0 / D, scalar2=None, op0=ALU.mult), sem="vg")
        p.wait("vector", "vg", None)
        p.op("vector", lambda e, o=c(4), a=c(2): e.tensor_tensor(
            o, a, a, ALU.mult), sem="vg")
        p.wait("vector", "vg", None)
        p.op("vector", lambda e, o=c(5), a=c(3), b2=c(4): e.tensor_tensor(
            o, a, b2, ALU.subtract), sem="vg")
        p.wait("scalar", "vg", None)
        p.op("scalar", lambda e, o=c(6), a=c(5): e.activation(
            out=o, in_=a, func=AF.Sqrt, bias=epsc[:, 0:1]), sem="se")
        p.wait("vector", "se", None)
        p.op("vector", lambda e, o=c(6), a=c(6): e.reciprocal(o, a), sem="vg")
        p.wait("vector", "vg", None)
        p.op("vector", lambda e, o=c(7), a=c(2), b2=c(6): e.scalar_tensor_tensor(
            out=o, in0=a, scalar=-1.0, in1=b2, op0=ALU.mult, op1=ALU.mult),
            sem="vg")
        p.wait("scalar", "vg", None)
        if i >= 2:
            p.wait("scalar", "pe", p.cnt["pe"] - 1)
        p.op("scalar", lambda e, b=buf, o=xn: e.activation(
            out=o, in_=b, func=AF.Identity, scale=c(6), bias=c(7)), sem="se")
        p.wait("tensor", "se", None)
        if i > 0:
            p.wait("tensor", "ve", None)
        for j in range(8):
            p.op("tensor", lambda e, xnj=xn[:, j * P:(j + 1) * P],
                 d=ps_tr[:, j * P:(j + 1) * P]: e.transpose(d, xnj, id_sb),
                 sem="pe" if j == 7 else None)
        p.wait("vector", "pe", None)
        p.op("vector", lambda e, i=i: e.tensor_copy(
            xnfm[:, :, i * P:(i + 1) * P],
            ps_tr.rearrange("p (a b) -> p a b", a=8)), sem="ve")
    p.wait("gpsimd", "ve", None)
    for a in range(8):
        p.dma("gpsimd", o_xnfm[a], xnfm[:, a, :], "st")

    mmctr = {"n": 0}

    def mm_block(chunks, dest_fn, evac_eng="vector"):
        k = mmctr["n"]
        buf = ps_mm[k % 2]
        if k >= 2:
            p.wait("tensor", "ve", mmctr[f"evac{k - 2}"])
        for ci, (lh, rh) in enumerate(chunks):
            p.op("tensor", lambda e, lh=lh, rh=rh, st=(ci == 0),
                 sp=(ci == len(chunks) - 1), b=buf: e.matmul(
                     b, lh, rh, start=st, stop=sp),
                 sem="pe" if ci == len(chunks) - 1 else None)
        p.wait(evac_eng, "pe", None)
        p.op(evac_eng, lambda e, b=buf: dest_fn(e, b), sem="ve")
        mmctr[f"evac{k}"] = p.cnt["ve"]
        mmctr["n"] += 1

    p.wait("tensor", "lg", wload)
    p.wait("tensor", "ve", None)
    for c2 in range(2):
        for sl in range(8):
            chunks = [(wkvd_sb[:, dk, c2 * P:(c2 + 1) * P],
                       xnfm[:, dk, sl * 512:(sl + 1) * 512]) for dk in range(8)]
            mm_block(chunks, lambda e, b, c2=c2, sl=sl: e.tensor_copy(
                lat[:, c2, sl * 512:(sl + 1) * 512], b))

    for qc in range(4):
        for sl in range(8):
            chunks = [(wq_sb[:, dk, qc * P:(qc + 1) * P],
                       xnfm[:, dk, sl * 512:(sl + 1) * 512]) for dk in range(8)]
            mm_block(chunks, lambda e, b, qc=qc, sl=sl: e.tensor_copy(
                qt_sb[:, qc, sl * 512:(sl + 1) * 512], b))
    p.wait("gpsimd", "ve", None)
    for qc in range(4):
        p.dma("gpsimd", o_qt[qc], qt_sb[:, qc, :], "st")

    for kc in range(4):
        stg = kstg[kc % 2]
        if kc >= 2:
            p.wait("tensor", "st", None)
        for sl in range(8):
            chunks = [(wku_sb[:, c2, kc * P:(kc + 1) * P],
                       lat[:, c2, sl * 512:(sl + 1) * 512]) for c2 in range(2)]
            mm_block(chunks, lambda e, b, sl=sl, stg=stg: e.tensor_copy(
                stg[:, sl * 512:(sl + 1) * 512], b))
        p.wait("gpsimd", "ve", None)
        p.dma("gpsimd", o_kt[kc], stg, "st")

    for kb in range(32):
        stg = vstg[kb % 2]
        if kb >= 2:
            p.wait("vector", "st", None)
        chunks = [(lat[:, c2, kb * P:(kb + 1) * P], wvu_sb[:, c2, :])
                  for c2 in range(2)]
        mm_block(chunks, lambda e, b, stg=stg: e.tensor_copy(
            stg.rearrange("p (h v) -> p h v", h=8)[:, :, 0:64],
            b.rearrange("p (h v) -> p h v", h=8)))
        p.wait("gpsimd", "ve", None)
        p.dma("gpsimd", o_va[kb], stg, "st")

    p.wait("gpsimd", "st", None)
    return p.build()


def _mask_np():
    u = np.arange(896)[None, :] - 384
    r = np.arange(P)[:, None]
    return (u >= r).astype(BF16NP)


def build_b():
    nc = new_nc()
    kt = nc.dram_tensor("kt", [4, P, S], BF16, kind="ExternalInput").ap()
    qt = nc.dram_tensor("qt", [4, P, S], BF16, kind="ExternalInput").ap()
    va = nc.dram_tensor("vaug", [32, P, 520], BF16, kind="ExternalInput").ap()
    o_y = nc.dram_tensor("yfm", [4, P, S], BF16, kind="ExternalOutput").ap()
    maskc = nc.inline_tensor(_mask_np(), name="maskc").ap()

    p = Prog(nc)
    kt_sb = p.sb("kt_sb", [P, 4, S], BF16)
    qt_sb = p.sb("qt_sb", [P, 4, S], BF16)
    va_sb = p.sb("va_sb", [P, 32, 8, 65], BF16)
    mk_sb = p.sb("mk_sb", [P, 896], BF16)
    ptb = [p.sb(f"ptb{i}", [P, 1024], BF16) for i in range(2)]
    ystg = [p.sb(f"ystg{i}", [P, S], BF16) for i in range(2)]
    rcp16 = p.sb("rcp16", [1, 512], BF16)
    den_sb = p.sb("den_sb", [64, 512], F32)
    ones1 = p.sb("ones1", [1, 64], BF16)
    expb = p.sb("expb", [P, 1], F32)
    ps_sg = [p.ps(f"ps_sg{i}", [P, 1024], F32) for i in range(2)]
    ps_y = [p.ps(f"ps_y{i}", [P, 512], F32) for i in range(2)]
    ps_b = p.ps("ps_b", [64, 512], F32)
    for s_ in ("ld", "pe", "ve", "se", "st"):
        p.sem(s_)

    p.dma("sync", kt_sb, kt.rearrange("a p t -> p a t"), "ld")
    p.dma("sync", qt_sb, qt.rearrange("a p t -> p a t"), "ld")
    p.dma("sync", va_sb.rearrange("p b h v -> p b (h v)"),
          va.rearrange("b p x -> p b x"), "ld")
    p.dma("sync", mk_sb, maskc, "ld")
    p.op("gpsimd", lambda e: e.memset(ones1, 1.0), sem="ve")
    p.op("gpsimd", lambda e: e.memset(expb, -EXPB), sem="ve")
    ldall = p.cnt["ld"]
    p.wait("tensor", "ld", ldall)
    p.wait("tensor", "ve", None)
    p.wait("scalar", "ld", ldall)
    p.wait("scalar", "ve", None)
    p.wait("vector", "ld", ldall)

    se_hist = {}
    ynorm_hist = {}
    dma_hist = {}
    for h in range(8):
        cc, rb = h // 2, (h % 2) * 64
        for sl in range(8):
            q0 = sl * 512
            nkb = 4 * (sl + 1)
            yb = ps_y[sl % 2]
            gsl = (h, sl)
            prev = (h, sl - 2) if sl >= 2 else ((h - 1, sl + 6) if h > 0 else None)
            if prev in ynorm_hist:
                p.wait("tensor", "ve", ynorm_hist[prev])
            for g in range(nkb // 2):
                kb0 = 2 * g
                if (h, sl, g - 2) in se_hist:
                    p.wait("tensor", "se", se_hist[(h, sl, g - 2)])
                elif g < 2:
                    lastg = se_hist.get("last2", None)
                    if lastg is not None and g == 0:
                        p.wait("tensor", "se", lastg[0])
                    if lastg is not None and g == 1:
                        p.wait("tensor", "se", lastg[1])
                for j in range(2):
                    kb = kb0 + j
                    p.op("tensor", lambda e, cc=cc, rb=rb, kb=kb, q0=q0, g=g, j=j:
                         e.matmul(ps_sg[g % 2][:, j * 512:(j + 1) * 512],
                                  kt_sb[rb:rb + 64, cc, kb * P:(kb + 1) * P],
                                  qt_sb[rb:rb + 64, cc, q0:q0 + 512],
                                  start=True, stop=True, skip_group_check=True),
                         sem="pe" if j == 1 else None)
                p.wait("scalar", "pe", None)
                p.op("scalar", lambda e, g=g: e.activation(
                    out=ptb[g % 2], in_=ps_sg[g % 2], func=AF.Exp,
                    scale=0.125, bias=expb[:, 0:1]), sem="se")
                se_hist[(h, sl, g)] = p.cnt["se"]
                nmask = 0
                for j in range(2):
                    kb = kb0 + j
                    delta = kb * P - q0
                    if 0 <= delta <= 384:
                        off = 384 - delta
                        p.wait("vector", "se", None)
                        p.op("vector", lambda e, g=g, j=j, off=off:
                             e.tensor_tensor(
                                 ptb[g % 2][:, j * 512:(j + 1) * 512],
                                 ptb[g % 2][:, j * 512:(j + 1) * 512],
                                 mk_sb[:, off:off + 512], ALU.mult), sem="ve")
                        nmask += 1
                if nmask:
                    p.wait("tensor", "ve", None)
                else:
                    p.wait("tensor", "se", None)
                for j in range(2):
                    kb = kb0 + j
                    p.op("tensor", lambda e, kb=kb, h=h, g=g, j=j, yb=yb,
                         st=(kb == 0), sp=(kb == nkb - 1): e.matmul(
                             yb[0:65, :], va_sb[:, kb, h, :],
                             ptb[g % 2][:, j * 512:(j + 1) * 512],
                             start=st, stop=sp, skip_group_check=True),
                         sem="pe" if j == 1 else None)
            se_hist["last2"] = (se_hist[(h, sl, nkb // 2 - 2)],
                                se_hist[(h, sl, nkb // 2 - 1)])
            p.wait("vector", "pe", None)

            def _recip_lp(e, yb=yb):
                with e.bass.allow_low_precision(
                        reason="softmax denom reciprocal; common scale cancels"):
                    return e.reciprocal(rcp16, yb[64:65, :])

            p.op("vector", _recip_lp, sem="ve")
            p.wait("tensor", "ve", None)
            p.op("tensor", lambda e: e.matmul(
                ps_b, ones1, rcp16, start=True, stop=True,
                skip_group_check=True), sem="pe")
            p.wait("vector", "pe", None)
            p.op("vector", lambda e: e.tensor_copy(den_sb, ps_b), sem="ve")
            if (cc % 2, "dma") in dma_hist and cc >= 2:
                p.wait("vector", "st", dma_hist[(cc % 2, "dma")])
            p.wait("vector", "ve", None)
            p.op("vector", lambda e, yb=yb, rb=rb, q0=q0, cc=cc:
                 e.tensor_tensor(ystg[cc % 2][rb:rb + 64, q0:q0 + 512],
                                 yb[0:64, :], den_sb, ALU.mult), sem="ve")
            ynorm_hist[gsl] = p.cnt["ve"]
        if h % 2 == 1:
            p.wait("gpsimd", "ve", None)
            p.dma("gpsimd", o_y[cc], ystg[cc % 2], "st")
            dma_hist[(cc % 2, "dma")] = p.cnt["st"]

    p.wait("gpsimd", "st", None)
    return p.build()


def run_stage_b(res_a):
    nc = build_b()
    in_maps = [{"kt": np.asarray(r["kt"]), "qt": np.asarray(r["qt"]),
                "vaug": np.asarray(r["vaug"])} for r in res_a]
    res = run_bass_kernel_spmd(nc, in_maps, core_ids=list(range(8)))
    return res.results


def build_c():
    nc = new_nc()
    y_all = nc.dram_tensor("y_all", [8, P, HT], BF16, kind="ExternalInput").ap()
    wout = nc.dram_tensor("wout", [D, D], F32, kind="ExternalInput").ap()
    x_own = nc.dram_tensor("x_own", [HT, D], F32, kind="ExternalInput").ap()
    o_xr = nc.dram_tensor("xr", [NT, P, D], F32, kind="ExternalOutput").ap()

    p = Prog(nc)
    y_sb = p.sb("y_sb", [P, 8, HT], BF16)
    wo_sb = p.sb("wo_sb", [P, 8, D], BF16)
    xb = [p.sb(f"xb{i}", [P, D], F32) for i in range(2)]
    stg = [p.sb(f"stg{i}", [P, D], F32) for i in range(2)]
    ps = [p.ps(f"ps{i}", [P, D], F32) for i in range(2)]
    for s_ in ("lx", "lg", "pe", "ve", "st"):
        p.sem(s_)

    p.dma("sync", y_sb, y_all.rearrange("a p t -> p a t"), "lx")
    p.dma("gpsimd", wo_sb, wout.rearrange("(a p) k -> p a k", p=P), "lg")
    evac_hist = {}
    for t in range(NT):
        xbuf = xb[t % 2]
        if t >= 2:
            p.wait("sync", "ve", evac_hist[t - 2])
        p.dma("sync", xbuf, x_own[t * P:(t + 1) * P, :], "lx")
        ldv = p.cnt["lx"]
        pb = ps[t % 2]
        p.wait("tensor", "lx", 16)
        p.wait("tensor", "lg", 16)
        if t >= 2:
            p.wait("tensor", "ve", evac_hist[t - 2])
        for ic in range(8):
            for half in range(2):
                p.op("tensor", lambda e, ic=ic, half=half, t=t, pb=pb: e.matmul(
                    pb[:, half * 512:(half + 1) * 512],
                    y_sb[:, ic, t * P:(t + 1) * P],
                    wo_sb[:, ic, half * 512:(half + 1) * 512],
                    start=(ic == 0), stop=(ic == 7), skip_group_check=True),
                    sem="pe" if (ic == 7 and half == 1) else None)
        p.wait("vector", "lx", ldv)
        p.wait("vector", "pe", None)
        if t >= 2:
            p.wait("vector", "st", None)
        p.op("vector", lambda e, t=t, pb=pb, xbuf=xbuf: e.tensor_add(
            stg[t % 2], pb, xbuf), sem="ve")
        evac_hist[t] = p.cnt["ve"]
        p.wait("gpsimd", "ve", None)
        p.dma("gpsimd", o_xr[t], stg[t % 2], "st")
    p.wait("gpsimd", "st", None)
    return p.build()


def run_stage_c(inputs, res_b):
    x = _f32(inputs["L0_attn_x_bld"])
    wout = _f32(inputs["L0_attn_Wout"]).reshape(D, D)
    nc = build_c()
    in_maps = []
    for core in range(8):
        b, g = core // 2, core % 2
        ysl = slice(g * HT, (g + 1) * HT)
        ya = np.concatenate([np.asarray(res_b[2 * b]["yfm"])[:, :, ysl],
                             np.asarray(res_b[2 * b + 1]["yfm"])[:, :, ysl]], 0)
        in_maps.append({
            "y_all": np.ascontiguousarray(ya),
            "wout": wout,
            "x_own": np.ascontiguousarray(x[b, g * HT:(g + 1) * HT, :]),
        })
    res = run_bass_kernel_spmd(nc, in_maps, core_ids=list(range(8)))
    return res.results


def build_d():
    nc = new_nc()
    xr = nc.dram_tensor("xr", [NT, P, D], F32, kind="ExternalInput").ap()
    xnfm = nc.dram_tensor("xnfm_own", [8, P, HT], BF16, kind="ExternalInput").ap()
    wfc = nc.dram_tensor("wfc", [4096, D], F32, kind="ExternalInput").ap()
    lnw = nc.dram_tensor("lnw", [1, 4096], F32, kind="ExternalInput").ap()
    lnb = nc.dram_tensor("lnb", [1, 4096], F32, kind="ExternalInput").ap()
    wproj = nc.dram_tensor("wproj", [D, 4096], F32, kind="ExternalInput").ap()
    o_x1 = nc.dram_tensor("x1", [NT, P, D], F32, kind="ExternalOutput").ap()
    ident = nc.inline_tensor(_ident_np(), name="ident").ap()
    id32 = nc.inline_tensor(np.eye(P, dtype=np.float32), name="id32").ap()

    p = Prog(nc)
    wfcT = p.sb("wfcT", [P, 8, 4096], BF16)
    wprT = p.sb("wprT", [P, 32, D], BF16)
    arena = p.sb("arena", [P, 13312], F32)
    a16 = arena.bitcast(BF16)
    w3s = [arena[:, 0:4096], arena[:, 4096:8192]]
    sgna = a16[:, 16384:20480]
    sgnb = a16[:, 20480:24576]
    xnf = [a16[:, 0:1024].rearrange("p (a t) -> p a t", a=8),
           a16[:, 1024:2048].rearrange("p (a t) -> p a t", a=8)]
    h_sb = a16[:, 2048:6144]
    hT = a16[:, 6144:10240].rearrange("p (a t) -> p a t", a=32)
    lnwbc = a16[:, 10240:14336]
    lnbbc = a16[:, 14336:18432]
    xrb = [arena[:, 9216:10240], arena[:, 10240:11264]]
    x1s = [arena[:, 11264:12288], arena[:, 12288:13312]]
    id_sb = p.sb("id_sb", [P, P], BF16)
    id32_sb = p.sb("id32_sb", [P, P], F32)
    onesf = p.sb("onesf", [1, P], F32)
    ones16 = p.sb("ones16", [1, P], BF16)
    lnrow = p.sb("lnrow", [1, 8192], BF16)
    stt = p.sb("stt_d", [P, 12], F32)
    g11 = p.sb("g11", [1, 4], F32)
    thr = p.sb("thr_d", [P, 4], F32)
    epsc = p.sb("epsc_d", [P, 1], F32)
    ps_fc = p.ps("ps_fc", [P, 2048], F32)
    ps_tr = p.ps("ps_tr_d", [P, D], BF16)
    ps_pr = p.ps("ps_pr", [P, D], F32)
    ps_sc = p.ps("ps_sc", [P, 128], F32)
    for s_ in ("lx", "lg", "pe", "ve", "se", "ge", "vg", "st"):
        p.sem(s_)

    p.dma("sync", id_sb, ident, "lx")
    p.dma("sync", id32_sb, id32, "lx")
    p.op("gpsimd", lambda e: e.memset(onesf, 1.0), sem="ve")
    p.op("gpsimd", lambda e: e.memset(ones16, 1.0), sem="ve")
    p.op("gpsimd", lambda e: e.memset(epsc, 1e-5), sem="ve")
    p.dma("gpsimd", lnrow[0:1, 0:4096], lnw, "lg")
    p.dma("gpsimd", lnrow[0:1, 4096:8192], lnb, "lg")
    cload = p.cnt["lg"]

    def gamma_phase(src_fn, ngr, gcols, tcols):
        for gidx in range(ngr):
            bb = w3s[gidx % 2][:, 0:gcols]
            if gidx >= 2:
                p.wait("sync", "vg", None)
            p.dma("sync", bb.rearrange("p (k d) -> p k d", d=1024),
                  src_fn(gidx), "lx")
            p.wait("vector", "lx", p.cnt["lx"])
            dst = stt[:, 8 + gidx % 2:9 + gidx % 2]
            p.op("vector", lambda e, bb=bb, dst=dst: e.tensor_reduce(
                dst, bb, axis=AX.X, op=ALU.add,
                apply_absolute_value=True), sem="vg")
            p.wait("vector", "vg", None)
            if gidx == 0:
                p.op("vector", lambda e, dst=dst: e.tensor_copy(
                    stt[:, 10:11], dst), sem="vg")
            else:
                p.op("vector", lambda e, dst=dst: e.tensor_add(
                    stt[:, 10:11], stt[:, 10:11], dst), sem="vg")
            p.wait("vector", "vg", None)
        p.wait("tensor", "vg", None)
        p.op("tensor", lambda e: e.transpose(
            ps_sc[0:1, 0:P], stt[:, 10:11], id32_sb), sem="pe")
        p.wait("vector", "pe", None)
        p.op("vector", lambda e: e.reduce_sum(
            g11[0:1, 0:1], ps_sc[0:1, 0:P], axis=AX.X), sem="vg")
        p.wait("vector", "vg", None)
        p.op("vector", lambda e: e.tensor_scalar(
            out=g11[0:1, 1:2], in0=g11[0:1, 0:1], scalar1=1.0 / (4096.0 * D),
            scalar2=1e-6, op0=ALU.mult, op1=ALU.add), sem="vg")
        p.wait("vector", "vg", None)
        p.op("vector", lambda e: e.tensor_scalar(
            out=g11[0:1, 2:3], in0=g11[0:1, 1:2], scalar1=0.5, scalar2=None,
            op0=ALU.mult), sem="vg")
        p.wait("vector", "vg", None)
        p.op("vector", lambda e: e.tensor_scalar(
            out=g11[0:1, 3:4], in0=g11[0:1, 2:3], scalar1=-1.0, scalar2=None,
            op0=ALU.mult), sem="vg")
        p.wait("vector", "vg", None)
        p.wait("tensor", "vg", None)
        p.op("tensor", lambda e: e.matmul(
            ps_sc[:, 0:2], onesf, g11[0:1, 2:4], start=True, stop=True,
            skip_group_check=True), sem="pe")
        p.wait("vector", "pe", None)
        p.op("vector", lambda e, tcols=tcols: e.tensor_copy(
            thr[:, tcols[0]:tcols[1] + 1], ps_sc[:, 0:2]), sem="ve")

    def quant_phase(src_fn, ntile, cols, tpos, tneg, dst_fn):
        rounds = cols // 1024
        for t in range(ntile):
            bb = w3s[t % 2][:, 0:cols]
            if t >= 2:
                p.wait("sync", "se", None)
            p.dma("sync", bb.rearrange("p (k d) -> p k d", d=1024),
                  src_fn(t), "lx")
            p.wait("scalar", "lx", p.cnt["lx"])
            p.wait("scalar", "ve", None)
            if t >= 1:
                p.wait("scalar", "ge", None)
                p.wait("scalar", "pe", None)
            sa_, sb_ = sgna[:, 0:cols], sgnb[:, 0:cols]
            p.op("scalar", lambda e, bb=bb, sa_=sa_, tn=tneg: e.activation(
                out=sa_, in_=bb, func=AF.Sign, bias=thr[:, tn:tn + 1]), sem="se")
            p.op("scalar", lambda e, bb=bb, sb_=sb_, tp=tpos: e.activation(
                out=sb_, in_=bb, func=AF.Sign, bias=thr[:, tp:tp + 1]), sem="se")
            p.wait("gpsimd", "se", None)
            if t >= 1:
                p.wait("gpsimd", "pe", None)
            p.op("gpsimd", lambda e, sa_=sa_, sb_=sb_: e.tensor_tensor(
                sa_, sa_, sb_, ALU.add), sem="ge")
            for r in range(rounds):
                p.wait("tensor", "ge" if r == 0 else "ve", None)
                for j in range(8):
                    p.op("tensor", lambda e, r=r, j=j, sa_=sa_: e.transpose(
                        ps_tr[:, j * P:(j + 1) * P],
                        sa_[:, (r * 8 + j) * P:(r * 8 + j + 1) * P], id_sb),
                        sem="pe" if j == 7 else None)
                p.wait("vector", "pe", None)
                p.op("vector", lambda e, t=t, r=r: e.tensor_copy(
                    dst_fn(t, r), ps_tr.rearrange("p (a b) -> p a b", a=8)),
                    sem="ve")

    wfc_r = wfc.rearrange("(a p) d -> p a d", p=P)
    gamma_phase(lambda g: wfc_r[:, 4 * g:4 * (g + 1), :], 8, 4096, (0, 1))
    quant_phase(lambda t: wfc_r[:, t:t + 1, :], 32, 1024, 0, 1,
                lambda t, r: wfcT[:, :, t * P:(t + 1) * P])
    wpr_r = wproj.rearrange("(a p) h -> p a h", p=P)
    gamma_phase(lambda g: wpr_r[:, g:g + 1, :].rearrange(
        "p a (k d) -> p (a k) d", d=1024), 8, 4096, (2, 3))
    quant_phase(lambda t: wpr_r[:, t:t + 1, :].rearrange(
        "p a (k d) -> p (a k) d", d=1024), 8, 4096, 2, 3,
                lambda t, r: wprT[:, r * 8:(r + 1) * 8, t * P:(t + 1) * P])

    p.wait("vector", "lg", cload)
    p.wait("vector", "se", None)
    p.wait("tensor", "se", None)
    for piece in range(16):
        src = lnrow[0:1, piece * 512:(piece + 1) * 512]
        dst = (lnwbc if piece < 8 else lnbbc)
        doff = (piece % 8) * 512
        p.wait("tensor", "ve", None)
        p.op("tensor", lambda e, src=src: e.matmul(
            ps_sc.bitcast(BF16)[:, 0:512] if False else ps_fc[:, 0:512],
            ones16, src, start=True, stop=True, skip_group_check=True),
            sem="pe")
        p.wait("vector", "pe", None)
        p.op("vector", lambda e, dst=dst, doff=doff: e.tensor_copy(
            dst[:, doff:doff + 512], ps_fc[:, 0:512]), sem="ve")

    c = lambda i: stt[:, i:i + 1]
    f_hist = {}
    p.wait("sync", "pe", None)
    p.wait("sync", "se", None)
    p.wait("sync", "ge", None)
    p.wait("sync", "ve", None)
    for t in range(NT):
        xnb = xnf[t % 2]
        xrbuf = xrb[t % 2]
        if t >= 2:
            p.wait("sync", "pe", f_hist[(t - 2, "lastmm")])
            p.wait("sync", "ve", f_hist[(t - 2, "evac")])
        p.dma("sync", xnb,
              xnfm.rearrange("a p t -> p a t")[:, :, t * P:(t + 1) * P], "lx")
        p.dma("sync", xrbuf, xr[t], "lx")
        ldv = p.cnt["lx"]
        p.wait("tensor", "lx", ldv)
        if t >= 1:
            p.wait("tensor", "se", f_hist[(t - 1, "sq")])
        for half in range(2):
            if half == 1:
                p.wait("tensor", "se", None)
            for dj in range(8):
                for sub in range(4):
                    p.op("tensor", lambda e, dj=dj, sub=sub, half=half, xnb=xnb:
                         e.matmul(ps_fc[:, sub * 512:(sub + 1) * 512],
                                  xnb[:, dj, :],
                                  wfcT[:, dj, half * 2048 + sub * 512:
                                       half * 2048 + (sub + 1) * 512],
                                  start=(dj == 0), stop=(dj == 7),
                                  skip_group_check=True),
                         sem="pe" if (dj == 7 and sub == 3) else None)
            p.wait("scalar", "pe", None)
            for gg in range(2):
                p.op("scalar", lambda e, half=half, gg=gg: e.activation(
                    out=h_sb[:, half * 2048 + gg * 1024:
                             half * 2048 + (gg + 1) * 1024],
                    in_=ps_fc[:, gg * 1024:(gg + 1) * 1024], func=AF.Gelu,
                    scale=0.5,
                    accum_out=stt[:, 8 + half * 2 + gg:9 + half * 2 + gg]),
                    sem="se")
        p.wait("vector", "se", None)
        p.op("vector", lambda e: e.reduce_sum(c(0), stt[:, 8:12], axis=AX.X),
             sem="vg")
        p.wait("scalar", "vg", None)
        for gg in range(4):
            p.op("scalar", lambda e, gg=gg: e.activation(
                out=ps_fc[:, (gg % 2) * 1024:(gg % 2 + 1) * 1024],
                in_=h_sb[:, gg * 1024:(gg + 1) * 1024], func=AF.Square,
                accum_out=stt[:, 8 + gg:9 + gg]), sem="se")
        f_hist[(t, "sq")] = p.cnt["se"]
        p.wait("vector", "se", None)
        p.op("vector", lambda e: e.reduce_sum(c(1), stt[:, 8:12], axis=AX.X),
             sem="vg")
        p.wait("vector", "vg", None)
        p.op("vector", lambda e: e.tensor_scalar(
            out=c(2), in0=c(0), scalar1=1.0 / 4096, scalar2=None,
            op0=ALU.mult), sem="vg")
        p.wait("vector", "vg", None)
        p.op("vector", lambda e: e.tensor_scalar(
            out=c(3), in0=c(1), scalar1=1.0 / 4096, scalar2=None,
            op0=ALU.mult), sem="vg")
        p.wait("vector", "vg", None)
        p.op("vector", lambda e: e.tensor_tensor(c(4), c(2), c(2), ALU.mult),
             sem="vg")
        p.wait("vector", "vg", None)
        p.op("vector", lambda e: e.tensor_tensor(c(5), c(3), c(4), ALU.subtract),
             sem="vg")
        p.wait("scalar", "vg", None)
        p.op("scalar", lambda e: e.activation(
            out=c(6), in_=c(5), func=AF.Sqrt, bias=epsc[:, 0:1]), sem="se")
        p.wait("vector", "se", None)
        p.op("vector", lambda e: e.reciprocal(c(6), c(6)), sem="vg")
        p.wait("vector", "vg", None)
        p.op("vector", lambda e: e.scalar_tensor_tensor(
            out=c(7), in0=c(2), scalar=-1.0, in1=c(6), op0=ALU.mult,
            op1=ALU.mult), sem="vg")
        p.wait("scalar", "vg", None)
        p.op("scalar", lambda e: e.activation(
            out=h_sb, in_=h_sb, func=AF.Identity, scale=c(6), bias=c(7)),
            sem="se")
        p.wait("vector", "se", None)
        p.op("vector", lambda e: e.tensor_tensor(h_sb, h_sb, lnwbc, ALU.mult),
             sem="ve")
        p.op("vector", lambda e: e.tensor_add(h_sb, h_sb, lnbbc), sem="ve")
        for r in range(4):
            p.wait("tensor", "ve", None)
            for j in range(8):
                p.op("tensor", lambda e, r=r, j=j: e.transpose(
                    ps_tr[:, j * P:(j + 1) * P],
                    h_sb[:, (r * 8 + j) * P:(r * 8 + j + 1) * P], id_sb),
                    sem="pe" if j == 7 else None)
            p.wait("vector", "pe", None)
            p.op("vector", lambda e, r=r: e.tensor_copy(
                hT[:, r * 8:(r + 1) * 8, :],
                ps_tr.rearrange("p (a b) -> p a b", a=8)), sem="ve")
        p.wait("tensor", "ve", None)
        for hc in range(32):
            for half in range(2):
                p.op("tensor", lambda e, hc=hc, half=half: e.matmul(
                    ps_pr[:, half * 512:(half + 1) * 512], hT[:, hc, :],
                    wprT[:, hc, half * 512:(half + 1) * 512],
                    start=(hc == 0), stop=(hc == 31), skip_group_check=True),
                    sem="pe" if (hc == 31 and half == 1) else None)
        f_hist[(t, "lastmm")] = p.cnt["pe"]
        p.wait("vector", "pe", None)
        p.wait("vector", "lx", ldv)
        if t >= 2:
            p.wait("vector", "st", None)
        p.op("vector", lambda e, t=t, xrbuf=xrbuf: e.scalar_tensor_tensor(
            out=x1s[t % 2], in0=ps_pr, scalar=0.5, in1=xrbuf, op0=ALU.mult,
            op1=ALU.add), sem="ve")
        f_hist[(t, "evac")] = p.cnt["ve"]
        p.wait("gpsimd", "ve", None)
        p.dma("gpsimd", o_x1[t], x1s[t % 2], "st")
    p.wait("gpsimd", "st", None)
    return p.build()


def run_stage_d(inputs, res_a, res_c):
    wfc = _f32(inputs["L0_ffn_W_fc"])
    lnw = _f32(inputs["L0_ffn_ln_w"]).reshape(1, 4096)
    lnb = _f32(inputs["L0_ffn_ln_b"]).reshape(1, 4096)
    wproj = _f32(inputs["L0_ffn_W_proj"])
    nc = build_d()
    in_maps = []
    for core in range(8):
        b, g = core // 2, core % 2
        tsl = slice(g * HT, (g + 1) * HT)
        in_maps.append({
            "xr": np.asarray(res_c[core]["xr"]),
            "xnfm_own": np.ascontiguousarray(
                np.asarray(res_a[core]["xnfm"])[:, :, tsl]),
            "wfc": wfc, "lnw": lnw, "lnb": lnb, "wproj": wproj,
        })
    res = run_bass_kernel_spmd(nc, in_maps, core_ids=list(range(8)))
    return res.results


def _pa_np():
    pa0 = np.zeros((8, P, P), BF16NP)
    pa1 = np.zeros((8, P, P), BF16NP)
    pb = np.zeros((8, P, P), BF16NP)
    for h in range(8):
        for a in range(P):
            pa0[h, 16 * h + a // 16, a] = 1
            pa1[h, 16 * h + 8 + a // 16, a] = 1
            pb[h, 16 * h + a % 16, a] = 1
    return pa0, pa1, pb


def build_e():
    nc = new_nc()
    x1 = nc.dram_tensor("x1", [NT, P, D], F32, kind="ExternalInput").ap()
    gamt = nc.dram_tensor("gamt", [P, 8], F32, kind="ExternalInput").ap()
    wq1 = nc.dram_tensor("wq1", [D, P], F32, kind="ExternalInput").ap()
    wkv1 = nc.dram_tensor("wkv1", [D, 256], F32, kind="ExternalInput").ap()
    o_xn2 = nc.dram_tensor("xn2fm", [8, P, HT], BF16, kind="ExternalOutput").ap()
    o_qx = nc.dram_tensor("qexp", [8, 3, P, HT], BF16, kind="ExternalOutput").ap()
    o_kvp = nc.dram_tensor("kvp", [8, 384, 17], F32, kind="ExternalOutput").ap()
    ident = nc.inline_tensor(_ident_np(), name="ident").ap()
    pa0, pa1, pb = _pa_np()
    c_pa0 = nc.inline_tensor(pa0, name="pa0").ap()
    c_pa1 = nc.inline_tensor(pa1, name="pa1").ap()
    c_pb = nc.inline_tensor(pb, name="pb").ap()

    p = Prog(nc)
    xb = [p.sb(f"xb{i}", [P, D], F32) for i in range(2)]
    xn16 = [p.sb(f"xn16{i}", [P, D], BF16) for i in range(2)]
    nr16 = [p.sb(f"nr16{i}", [P, D], BF16) for i in range(2)]
    xn2fm = p.sb("xn2fm_sb", [P, 8, HT], BF16)
    nrmfm = p.sb("nrmfm_sb", [P, 8, HT], BF16)
    q_sb = p.sb("q_sb", [P, HT], BF16)
    k_sb = p.sb("k_sb", [P, HT], BF16)
    v_sb = p.sb("v_sb", [P, HT], BF16)
    k_tm = p.sb("k_tm", [P, NT, P], BF16)
    vaug = p.sb("vaug_e", [P, NT, 8, 17], BF16)
    kexp = p.sb("kexp", [P, NT, 8, 273], BF16)
    qc0 = [p.sb(f"qc0_{i}", [P, HT], BF16) for i in range(2)]
    qx = [p.sb(f"qx_{i}", [P, HT], BF16) for i in range(2)]
    kvp_sb = p.sb("kvp_sb", [P, 8, 3, 17], F32)
    ab_scr = p.sb("ab_scr", [P, 512], BF16)
    wq1_sb = p.sb("wq1_sb", [P, 8, P], BF16)
    wkv1_sb = p.sb("wkv1_sb", [P, 8, 256], BF16)
    gam_sb = p.sb("gam_sb", [P, 8], F32)
    pa0_sb = p.sb("pa0_sb", [P, 8, P], BF16)
    pa1_sb = p.sb("pa1_sb", [P, 8, P], BF16)
    pb_sb = p.sb("pb_sb", [P, 8, P], BF16)
    id_sb = p.sb("id_sb_e", [P, P], BF16)
    stt = p.sb("stt_e", [P, 16], F32)
    sqscr = p.sb("sqscr_e", [P, D], F32)
    epsc = p.sb("epsc_e", [P, 1], F32)
    ps_tr = p.ps("ps_tr_e", [P, D], BF16)
    ps_mm = [p.ps(f"ps_mm_e{i}", [P, 512], F32) for i in range(2)]
    ps_ab = [p.ps(f"ps_ab{i}", [P, 512], F32) for i in range(2)]
    for s_ in ("lx", "lg", "pe", "ve", "se", "vg", "st"):
        p.sem(s_)

    p.dma("sync", id_sb, ident, "lx")
    p.dma("gpsimd", wq1_sb, wq1.rearrange("(a p) k -> p a k", p=P), "lg")
    p.dma("gpsimd", wkv1_sb, wkv1.rearrange("(a p) k -> p a k", p=P), "lg")
    p.dma("gpsimd", gam_sb, gamt, "lg")
    p.dma("gpsimd", pa0_sb, c_pa0.rearrange("a p k -> p a k"), "lg")
    p.dma("gpsimd", pa1_sb, c_pa1.rearrange("a p k -> p a k"), "lg")
    p.dma("gpsimd", pb_sb, c_pb.rearrange("a p k -> p a k"), "lg")
    p.op("gpsimd", lambda e: e.memset(epsc, 1e-5), sem="ve")
    p.op("gpsimd", lambda e: e.memset(vaug[:, :, :, 16:17], 1.0), sem="ve")
    p.op("gpsimd", lambda e: e.memset(kexp[:, :, :, 0:1], 1.0), sem="ve")
    p.op("gpsimd", lambda e: e.memset(qc0[0], 0.0), sem="ve")
    p.op("gpsimd", lambda e: e.memset(qc0[1], 0.0), sem="ve")
    p.op("gpsimd", lambda e: e.memset(qc0[0][0:1, :], 1.0), sem="ve")
    p.op("gpsimd", lambda e: e.memset(qc0[1][0:1, :], 1.0), sem="ve")
    p.op("gpsimd", lambda e: e.memset(kvp_sb.rearrange("p a b c -> p (a b c)"),
                                      0.0), sem="ve")
    wload = p.cnt["lg"]
    p.wait("vector", "lg", wload)
    p.op("vector", lambda e: e.tensor_tensor(
        wq1_sb, wq1_sb, gam_sb[:, :, None].broadcast_to([P, 8, P]),
        ALU.mult), sem="vg")
    p.wait("vector", "vg", None)
    p.op("vector", lambda e: e.tensor_tensor(
        wkv1_sb, wkv1_sb, gam_sb[:, :, None].broadcast_to([P, 8, 256]),
        ALU.mult), sem="vg")
    gfold = p.cnt["vg"]

    c = lambda i: stt[:, i:i + 1]
    for i in range(NT):
        buf, xn, nr = xb[i % 2], xn16[i % 2], nr16[i % 2]
        if i >= 2:
            p.wait("sync", "se", None)
        p.dma("sync", buf, x1[i], "lx")
        ldv = p.cnt["lx"]
        p.wait("vector", "lx", ldv)
        if i > 0:
            p.wait("vector", "se", None)
        p.op("vector", lambda e, b=buf: e.reduce_sum(c(0), b, axis=AX.X),
             sem="vg")
        p.wait("scalar", "lx", ldv)
        p.wait("scalar", "vg", None)
        p.op("scalar", lambda e, b=buf: e.activation(
            out=sqscr, in_=b, func=AF.Square, accum_out=c(1)), sem="se")
        p.wait("vector", "se", None)
        p.op("vector", lambda e: e.tensor_scalar(
            out=c(2), in0=c(0), scalar1=1.0 / D, scalar2=None, op0=ALU.mult),
            sem="vg")
        p.wait("vector", "vg", None)
        p.op("vector", lambda e: e.tensor_scalar(
            out=c(3), in0=c(1), scalar1=1.0 / D, scalar2=None, op0=ALU.mult),
            sem="vg")
        p.wait("vector", "vg", None)
        p.op("vector", lambda e: e.tensor_tensor(c(4), c(2), c(2), ALU.mult),
             sem="vg")
        p.wait("vector", "vg", None)
        p.op("vector", lambda e: e.tensor_tensor(c(5), c(3), c(4), ALU.subtract),
             sem="vg")
        p.wait("scalar", "vg", None)
        p.op("scalar", lambda e: e.activation(
            out=c(6), in_=c(5), func=AF.Sqrt, bias=epsc[:, 0:1]), sem="se")
        p.wait("vector", "se", None)
        p.op("vector", lambda e: e.reciprocal(c(6), c(6)), sem="vg")
        p.wait("vector", "vg", None)
        p.op("vector", lambda e: e.scalar_tensor_tensor(
            out=c(7), in0=c(2), scalar=-1.0, in1=c(6), op0=ALU.mult,
            op1=ALU.mult), sem="vg")
        p.wait("vector", "vg", None)
        p.op("vector", lambda e: e.tensor_scalar(
            out=c(8), in0=c(5), scalar1=1e-5, scalar2=None, op0=ALU.add),
            sem="vg")
        p.wait("vector", "vg", None)
        p.op("vector", lambda e: e.reciprocal(c(8), c(8)), sem="vg")
        p.wait("vector", "vg", None)
        p.op("vector", lambda e: e.tensor_tensor(c(9), c(5), c(8), ALU.mult),
             sem="vg")
        p.wait("vector", "vg", None)
        p.op("vector", lambda e: e.tensor_scalar(
            out=c(9), in0=c(9), scalar1=9.765625e-28, scalar2=None,
            op0=ALU.max), sem="vg")
        p.wait("scalar", "vg", None)
        p.op("scalar", lambda e: e.activation(
            out=c(10), in_=c(9), func=AF.Sqrt), sem="se")
        p.wait("vector", "se", None)
        p.op("vector", lambda e: e.reciprocal(c(10), c(10)), sem="vg")
        p.wait("vector", "vg", None)
        p.op("vector", lambda e: e.tensor_tensor(c(11), c(6), c(10), ALU.mult),
             sem="vg")
        p.wait("vector", "vg", None)
        p.op("vector", lambda e: e.tensor_tensor(c(12), c(7), c(10), ALU.mult),
             sem="vg")
        p.wait("scalar", "vg", None)
        if i >= 2:
            p.wait("scalar", "pe", p.cnt["pe"] - 2)
        p.op("scalar", lambda e, b=buf, o=xn: e.activation(
            out=o, in_=b, func=AF.Identity, scale=c(6), bias=c(7)), sem="se")
        p.op("scalar", lambda e, b=buf, o=nr: e.activation(
            out=o, in_=b, func=AF.Identity, scale=c(11), bias=c(12)), sem="se")
        p.wait("tensor", "se", None)
        if i > 0:
            p.wait("tensor", "ve", None)
        for j in range(8):
            p.op("tensor", lambda e, xn=xn, j=j: e.transpose(
                ps_tr[:, j * P:(j + 1) * P], xn[:, j * P:(j + 1) * P], id_sb),
                sem="pe" if j == 7 else None)
        p.wait("vector", "pe", None)
        p.op("vector", lambda e, i=i: e.tensor_copy(
            xn2fm[:, :, i * P:(i + 1) * P],
            ps_tr.rearrange("p (a b) -> p a b", a=8)), sem="ve")
        p.wait("tensor", "ve", None)
        for j in range(8):
            p.op("tensor", lambda e, nr=nr, j=j: e.transpose(
                ps_tr[:, j * P:(j + 1) * P], nr[:, j * P:(j + 1) * P], id_sb),
                sem="pe" if j == 7 else None)
        p.wait("vector", "pe", None)
        p.op("vector", lambda e, i=i: e.tensor_copy(
            nrmfm[:, :, i * P:(i + 1) * P],
            ps_tr.rearrange("p (a b) -> p a b", a=8)), sem="ve")
    p.wait("gpsimd", "ve", None)
    for a in range(8):
        p.dma("gpsimd", o_xn2[a], xn2fm[:, a, :], "st")

    mmctr = {"n": 0}

    def mm_block(chunks, dest_fn, evac_eng="vector"):
        k = mmctr["n"]
        buf = ps_mm[k % 2]
        if k >= 2:
            p.wait("tensor", "ve", mmctr[f"evac{k - 2}"])
        for ci, (lh, rh) in enumerate(chunks):
            p.op("tensor", lambda e, lh=lh, rh=rh, st=(ci == 0),
                 sp=(ci == len(chunks) - 1), b=buf: e.matmul(
                     b, lh, rh, start=st, stop=sp, skip_group_check=True),
                 sem="pe" if ci == len(chunks) - 1 else None)
        p.wait(evac_eng, "pe", None)
        p.op(evac_eng, lambda e, b=buf: dest_fn(e, b), sem="ve")
        mmctr[f"evac{k}"] = p.cnt["ve"]
        mmctr["n"] += 1

    p.wait("tensor", "vg", gfold)
    p.wait("tensor", "ve", None)
    for sl in range(4):
        chunks = [(wq1_sb[:, dj, :], nrmfm[:, dj, sl * 512:(sl + 1) * 512])
                  for dj in range(8)]
        mm_block(chunks, lambda e, b, sl=sl: e.activation(
            out=q_sb[:, sl * 512:(sl + 1) * 512], in_=b, func=AF.Copy,
            scale=0.25), evac_eng="scalar")
    for c2 in range(2):
        dst = k_sb if c2 == 0 else v_sb
        for sl in range(4):
            chunks = [(wkv1_sb[:, dj, c2 * P:(c2 + 1) * P],
                       nrmfm[:, dj, sl * 512:(sl + 1) * 512])
                      for dj in range(8)]
            mm_block(chunks, lambda e, b, dst=dst, sl=sl: e.tensor_copy(
                dst[:, sl * 512:(sl + 1) * 512], b))

    p.wait("tensor", "ve", None)
    p.wait("tensor", "se", None)
    trcp_hist = {}
    for t in range(NT):
        if t >= 1:
            p.wait("tensor", "ve", trcp_hist[t - 1])
        p.op("tensor", lambda e, t=t: e.transpose(
            ps_tr[:, 0:P], k_sb[:, t * P:(t + 1) * P], id_sb), sem=None)
        p.op("tensor", lambda e, t=t: e.transpose(
            ps_tr[:, P:2 * P], v_sb[:, t * P:(t + 1) * P], id_sb), sem="pe")
        p.wait("vector", "pe", None)
        p.op("vector", lambda e, t=t: e.tensor_copy(
            k_tm[:, t, :], ps_tr[:, 0:P]), sem="ve")
        p.op("vector", lambda e, t=t: e.tensor_copy(
            vaug[:, t, :, 0:16],
            ps_tr[:, P:2 * P].rearrange("p (h v) -> p h v", h=8)), sem="ve")
        trcp_hist[t] = p.cnt["ve"]
        p.wait("vector", "ve", None)
        p.op("vector", lambda e, t=t: e.tensor_copy(
            kexp[:, t, :, 1:17],
            k_tm[:, t, :].rearrange("p (h v) -> p h v", h=8)), sem="ve")
        for h in range(8):
            p.op("vector", lambda e, t=t, h=h: e.scalar_tensor_tensor(
                out=kexp[:, t, h, 17:273].rearrange("p (a b) -> p a b", a=16),
                in0=k_tm[:, t, h * 16:(h + 1) * 16][:, :, None]
                .broadcast_to([P, 16, 16]),
                scalar=0.7071067811865476,
                in1=k_tm[:, t, None, h * 16:(h + 1) * 16]
                .broadcast_to([P, 16, 16]),
                op0=ALU.mult, op1=ALU.mult),
                sem="ve" if h == 7 else None)
    kexp_done = p.cnt["ve"]

    csl = [(0, 17), (17, 145), (145, 273)]
    p.wait("tensor", "ve", kexp_done)
    grp = 0
    for h in range(8):
        for ci, (c0_, c1_) in enumerate(csl):
            buf = ps_mm[grp % 2]
            if grp >= 2:
                p.wait("tensor", "ve", mmctr[f"kv{grp - 2}"])
            for t in range(NT):
                p.op("tensor", lambda e, t=t, h=h, c0_=c0_, c1_=c1_, buf=buf:
                     e.matmul(buf[0:c1_ - c0_, 0:17],
                              kexp[:, t, h, c0_:c1_], vaug[:, t, h, :],
                              start=(t == 0), stop=(t == NT - 1),
                              skip_group_check=True),
                     sem="pe" if t == NT - 1 else None)
            p.wait("vector", "pe", None)
            p.op("vector", lambda e, h=h, ci=ci, buf=buf: e.tensor_copy(
                kvp_sb[:, h, ci, :], buf[:, 0:17]), sem="ve")
            mmctr[f"kv{grp}"] = p.cnt["ve"]
            grp += 1
    p.wait("gpsimd", "ve", None)
    p.dma("gpsimd", o_kvp.rearrange("h (c p) e -> p h c e", p=P), kvp_sb, "st")

    qdma_hist = {}
    for h in range(8):
        qb = qc0[h % 2]
        if h >= 2:
            p.wait("gpsimd", "st", qdma_hist[h - 2])
        p.wait("gpsimd", "ve", mmctr["evac3"])
        p.dma("gpsimd", qb[1:17, :], q_sb[16 * h:16 * h + 16, :], "lg")
        p.wait("gpsimd", "lg", None)
        p.dma("gpsimd", o_qx[h, 0], qb, "st")
        qdma_hist[h] = p.cnt["st"]
    for h in range(8):
        for tile in range(2):
            pa = pa0_sb if tile == 0 else pa1_sb
            qxb = qx[(h * 2 + tile) % 2]
            if h * 2 + tile >= 2:
                p.wait("vector", "st", qdma_hist[(h * 2 + tile - 2, "qx")])
            for sl in range(4):
                p.wait("tensor", "ve", None)
                p.op("tensor", lambda e, pa=pa, h=h, sl=sl: e.matmul(
                    ps_ab[0], pa[:, h, :], q_sb[:, sl * 512:(sl + 1) * 512],
                    start=True, stop=True, skip_group_check=True), sem=None)
                p.op("tensor", lambda e, h=h, sl=sl: e.matmul(
                    ps_ab[1], pb_sb[:, h, :], q_sb[:, sl * 512:(sl + 1) * 512],
                    start=True, stop=True, skip_group_check=True), sem="pe")
                p.wait("vector", "pe", None)
                p.op("vector", lambda e: e.tensor_copy(
                    ab_scr, ps_ab[0]), sem="ve")
                p.op("vector", lambda e, qxb=qxb, sl=sl: e.scalar_tensor_tensor(
                    out=qxb[:, sl * 512:(sl + 1) * 512], in0=ab_scr,
                    scalar=0.7071067811865476, in1=ps_ab[1],
                    op0=ALU.mult, op1=ALU.mult), sem="ve")
            p.wait("gpsimd", "ve", None)
            p.dma("gpsimd", o_qx[h, 1 + tile], qxb, "st")
            qdma_hist[(h * 2 + tile, "qx")] = p.cnt["st"]

    p.wait("gpsimd", "st", None)
    return p.build()


def run_stage_e(inputs, res_d):
    gamt = _f32(np.asarray(inputs["L1_attn_gamma"], np.float32)
                .reshape(8, P).T)
    wq1 = _f32(inputs["L1_attn_w_q"])
    wkv1 = _f32(inputs["L1_attn_w_kv"])
    nc = build_e()
    in_maps = [{"x1": np.asarray(r["x1"]), "gamt": gamt, "wq1": wq1,
                "wkv1": wkv1} for r in res_d]
    res = run_bass_kernel_spmd(nc, in_maps, core_ids=list(range(8)))
    return res.results


def build_f():
    nc = new_nc()
    qx = nc.dram_tensor("qexp", [8, 3, P, HT], BF16, kind="ExternalInput").ap()
    kvp2 = nc.dram_tensor("kvp2", [2, 8, 384, 17], F32, kind="ExternalInput").ap()
    wo1 = nc.dram_tensor("wout1", [P, D], F32, kind="ExternalInput").ap()
    o_at = nc.dram_tensor("at1", [NT, P, D], BF16, kind="ExternalOutput").ap()
    ident = nc.inline_tensor(_ident_np(), name="ident").ap()

    p = Prog(nc)
    qx_sb = p.sb("qx_sb", [P, 8, 3, HT], BF16)
    kva = p.sb("kva", [P, 8, 3, 17], F32)
    kvb = p.sb("kvb", [P, 8, 3, 17], F32)
    kv16 = p.sb("kv16", [P, 8, 3, 17], BF16)
    wo_sb = p.sb("wo_sb_f", [P, D], BF16)
    id_sb = p.sb("id_sb_f", [P, P], BF16)
    o1 = [p.sb(f"o1_{i}", [P, P], BF16) for i in range(2)]
    o1t = [p.sb(f"o1t_{i}", [P, P], BF16) for i in range(2)]
    dn = p.sb("dn", [P, 8], F32)
    atstg = [p.sb(f"atstg{i}", [P, D], BF16) for i in range(2)]
    ps_o = [p.ps(f"ps_o{i}", [P, 136], F32) for i in range(2)]
    ps_tr = p.ps("ps_tr_f", [P, P], BF16)
    ps_w = p.ps("ps_w_f", [P, D], F32)
    for s_ in ("lx", "lg", "pe", "ve", "vg", "st"):
        p.sem(s_)

    p.dma("sync", qx_sb, qx.rearrange("h c p t -> p h c t"), "lx")
    p.dma("sync", id_sb, ident, "lx")
    p.dma("sync", kva, kvp2[0].rearrange("h (c p) e -> p h c e", p=P), "lx")
    p.dma("sync", kvb, kvp2[1].rearrange("h (c p) e -> p h c e", p=P), "lx")
    p.dma("gpsimd", wo_sb, wo1, "lg")
    lall = p.cnt["lx"]
    p.wait("vector", "lx", lall)
    p.op("vector", lambda e: e.tensor_add(kv16, kva, kvb), sem="vg")
    p.wait("tensor", "vg", None)
    p.wait("tensor", "lg", 16)

    csl = [(0, 17), (17, 145), (145, 273)]
    ev_hist = {}
    for t in range(NT):
        po = ps_o[t % 2]
        if t >= 2:
            p.wait("tensor", "ve", ev_hist[t - 2])
        first = True
        for h in range(8):
            for ci, (c0_, c1_) in enumerate(csl):
                p.op("tensor", lambda e, h=h, ci=ci, c0_=c0_, c1_=c1_, t=t,
                     po=po, st=first, sp=(h == 7 and ci == 2): e.matmul(
                         po[:, h * 17:(h + 1) * 17],
                         qx_sb[0:c1_ - c0_, h, ci, t * P:(t + 1) * P],
                         kv16[0:c1_ - c0_, h, ci, :],
                         start=st, stop=sp, skip_group_check=True),
                     sem="pe" if (h == 7 and ci == 2) else None)
                first = False
        p.wait("vector", "pe", None)
        p.op("vector", lambda e, po=po: e.tensor_scalar(
            out=dn[:, 0:8],
            in0=po.rearrange("p (h v) -> p h v", v=17)[:, :, 16:17]
            .rearrange("p h v -> p (h v)"),
            scalar1=1e-5, scalar2=None, op0=ALU.max), sem="vg")
        p.wait("vector", "vg", None)
        p.op("vector", lambda e: e.reciprocal(dn[:, 0:8], dn[:, 0:8]),
             sem="vg")
        p.wait("vector", "vg", None)
        p.op("vector", lambda e, po=po, t=t: e.tensor_tensor(
            o1[t % 2].rearrange("p (h v) -> p h v", h=8),
            po.rearrange("p (h v) -> p h v", v=17)[:, :, 0:16],
            dn[:, :, None].broadcast_to([P, 8, 16]), ALU.mult), sem="ve")
        ev_hist[t] = p.cnt["ve"]
        p.wait("tensor", "ve", None)
        p.op("tensor", lambda e, t=t: e.transpose(
            ps_tr, o1[t % 2], id_sb), sem="pe")
        p.wait("vector", "pe", None)
        p.op("vector", lambda e, t=t: e.tensor_copy(o1t[t % 2], ps_tr),
             sem="ve")
        p.wait("tensor", "ve", None)
        for half in range(2):
            p.op("tensor", lambda e, t=t, half=half: e.matmul(
                ps_w[:, half * 512:(half + 1) * 512], o1t[t % 2],
                wo_sb[:, half * 512:(half + 1) * 512], start=True, stop=True,
                skip_group_check=True), sem="pe" if half == 1 else None)
        p.wait("vector", "pe", None)
        if t >= 2:
            p.wait("vector", "st", None)
        p.op("vector", lambda e, t=t: e.tensor_copy(atstg[t % 2], ps_w),
             sem="ve")
        p.wait("gpsimd", "ve", None)
        p.dma("gpsimd", o_at[t], atstg[t % 2], "st")
    p.wait("gpsimd", "st", None)
    return p.build()


def run_stage_f(inputs, res_e):
    wo1 = _f32(inputs["L1_attn_w_out"])
    nc = build_f()
    in_maps = []
    for core in range(8):
        kvp2 = np.stack([np.asarray(res_e[core]["kvp"]),
                         np.asarray(res_e[core ^ 1]["kvp"])])
        in_maps.append({"qexp": np.asarray(res_e[core]["qexp"]),
                        "kvp2": kvp2, "wout1": wo1})
    res = run_bass_kernel_spmd(nc, in_maps, core_ids=list(range(8)))
    return res.results


def build_g():
    nc = new_nc()
    xn2 = nc.dram_tensor("xn2fm", [8, P, HT], BF16, kind="ExternalInput").ap()
    x1 = nc.dram_tensor("x1", [NT, P, D], F32, kind="ExternalInput").ap()
    at1 = nc.dram_tensor("at1", [NT, P, D], BF16, kind="ExternalInput").ap()
    wg = nc.dram_tensor("wg", [3072, D], I32, kind="ExternalInput").ap()
    wu = nc.dram_tensor("wu", [3072, D], I32, kind="ExternalInput").ap()
    wd = nc.dram_tensor("wd", [D, 3072], I32, kind="ExternalInput").ap()
    o_out = nc.dram_tensor("out", [NT, P, D], F32, kind="ExternalOutput").ap()
    ident = nc.inline_tensor(_ident_np(), name="ident").ap()

    p = Prog(nc)
    xn2_sb = p.sb("xn2_sb", [P, 8, HT], BF16)
    wdT = p.sb("wdT", [P, 24, D], BF16)
    hfm = p.sb("hfm", [P, 24, D], BF16)
    wgb = [p.sb(f"wgb{i}", [P, D], BF16) for i in range(2)]
    wub = [p.sb(f"wub{i}", [P, D], BF16) for i in range(2)]
    wdb = [p.sb(f"wdb{i}", [P, 3072], BF16) for i in range(2)]
    wgT = p.sb("wgT", [P, 8, P], BF16)
    wuT = p.sb("wuT", [P, 8, P], BF16)
    sg = p.sb("sg", [P, D], BF16)
    id_sb = p.sb("id_sb_g", [P, P], BF16)
    x1b = [p.sb(f"x1b{i}", [P, D], F32) for i in range(2)]
    atb = [p.sb(f"atb{i}", [P, D], BF16) for i in range(2)]
    ostg = [p.sb(f"ostg{i}", [P, D], F32) for i in range(2)]
    tmp = p.sb("tmp_g", [P, D], F32)
    ps_g = p.ps("ps_g", [P, D], F32)
    ps_u = p.ps("ps_u", [P, D], F32)
    ps_tr = p.ps("ps_tr_g", [P, D], BF16)
    ps_d = p.ps("ps_d", [P, D], F32)
    for s_ in ("lx", "lg", "pe", "ve", "se", "st"):
        p.sem(s_)

    p.dma("sync", id_sb, ident, "lx")
    p.dma("sync", xn2_sb, xn2.rearrange("a p t -> p a t"), "lx")
    xnld = p.cnt["lx"]

    tr_hist = {}
    for dc in range(8):
        bb = wdb[dc % 2]
        if dc >= 2:
            p.wait("gpsimd", "ve", tr_hist[dc - 2])
        p.dma("gpsimd", bb, wd[dc * P:(dc + 1) * P, :], "lg")
        ldv = p.cnt["lg"]
        p.wait("tensor", "lg", ldv)
        p.wait("tensor", "lx", 16)
        for r in range(3):
            if not (dc == 0 and r == 0):
                p.wait("tensor", "ve", None)
            for j in range(8):
                p.op("tensor", lambda e, bb=bb, r=r, j=j: e.transpose(
                    ps_tr[:, j * P:(j + 1) * P],
                    bb[:, (r * 8 + j) * P:(r * 8 + j + 1) * P], id_sb),
                    sem="pe" if j == 7 else None)
            p.wait("vector", "pe", None)
            p.op("vector", lambda e, dc=dc, r=r: e.tensor_copy(
                wdT[:, r * 8:(r + 1) * 8, dc * P:(dc + 1) * P],
                ps_tr.rearrange("p (a b) -> p a b", a=8)), sem="ve")
        tr_hist[dc] = p.cnt["ve"]

    dn_hist = {}
    for half in range(2):
        if half == 1:
            p.wait("vector", "pe", None)
        for hc in range(24):
            bg, bu = wgb[hc % 2], wub[hc % 2]
            if hc >= 2 or half > 0:
                p.wait("gpsimd", "se", None)
            p.dma("gpsimd", bg, wg[hc * P:(hc + 1) * P, :], "lg")
            p.dma("gpsimd", bu, wu[hc * P:(hc + 1) * P, :], "lg")
            ldv = p.cnt["lg"]
            p.wait("tensor", "lg", ldv)
            p.wait("tensor", "ve", None)
            if hc > 0 or half > 0:
                p.wait("tensor", "se", None)
            for j in range(8):
                p.op("tensor", lambda e, bg=bg, j=j: e.transpose(
                    ps_tr[:, j * P:(j + 1) * P], bg[:, j * P:(j + 1) * P],
                    id_sb), sem="pe" if j == 7 else None)
            p.wait("vector", "pe", None)
            p.op("vector", lambda e: e.tensor_copy(
                wgT, ps_tr.rearrange("p (a b) -> p a b", a=8)), sem="ve")
            p.wait("tensor", "ve", None)
            for j in range(8):
                p.op("tensor", lambda e, bu=bu, j=j: e.transpose(
                    ps_tr[:, j * P:(j + 1) * P], bu[:, j * P:(j + 1) * P],
                    id_sb), sem="pe" if j == 7 else None)
            p.wait("vector", "pe", None)
            p.op("vector", lambda e: e.tensor_copy(
                wuT, ps_tr.rearrange("p (a b) -> p a b", a=8)), sem="ve")
            p.wait("tensor", "ve", None)
            p.wait("tensor", "lx", xnld)
            for dj in range(8):
                for sl in range(2):
                    p.op("tensor", lambda e, dj=dj, sl=sl, half=half: e.matmul(
                        ps_g[:, sl * 512:(sl + 1) * 512], wgT[:, dj, :],
                        xn2_sb[:, dj, half * 1024 + sl * 512:
                               half * 1024 + (sl + 1) * 512],
                        start=(dj == 0), stop=(dj == 7),
                        skip_group_check=True), sem=None)
            for dj in range(8):
                for sl in range(2):
                    p.op("tensor", lambda e, dj=dj, sl=sl, half=half: e.matmul(
                        ps_u[:, sl * 512:(sl + 1) * 512], wuT[:, dj, :],
                        xn2_sb[:, dj, half * 1024 + sl * 512:
                               half * 1024 + (sl + 1) * 512],
                        start=(dj == 0), stop=(dj == 7),
                        skip_group_check=True),
                        sem="pe" if (dj == 7 and sl == 1) else None)
            p.wait("scalar", "pe", None)
            p.op("scalar", lambda e: e.activation(
                out=sg, in_=ps_g, func=AF.Silu, scale=1.0 / 127.0), sem="se")
            p.wait("vector", "se", None)
            p.op("vector", lambda e, hc=hc: e.scalar_tensor_tensor(
                out=hfm[:, hc, :], in0=ps_u, scalar=1.0 / 127.0, in1=sg,
                op0=ALU.mult, op1=ALU.mult), sem="ve")
        hdone = p.cnt["ve"]
        p.wait("tensor", "ve", hdone)
        for tt in range(8):
            t = half * 8 + tt
            if t >= 2:
                p.wait("sync", "ve", dn_hist[t - 2])
            p.dma("sync", x1b[t % 2], x1[t], "lx")
            p.dma("sync", atb[t % 2], at1[t], "lx")
            ldv = p.cnt["lx"]
            if t >= 1:
                p.wait("tensor", "ve", dn_hist[t - 1])
            for hc in range(24):
                for sl in range(2):
                    p.op("tensor", lambda e, hc=hc, sl=sl, tt=tt: e.matmul(
                        ps_d[:, sl * 512:(sl + 1) * 512],
                        hfm[:, hc, tt * P:(tt + 1) * P],
                        wdT[:, hc, sl * 512:(sl + 1) * 512],
                        start=(hc == 0), stop=(hc == 23),
                        skip_group_check=True),
                        sem="pe" if (hc == 23 and sl == 1) else None)
            p.wait("vector", "pe", None)
            p.wait("vector", "lx", ldv)
            if t >= 2:
                p.wait("vector", "st", None)
            p.op("vector", lambda e, t=t: e.scalar_tensor_tensor(
                out=tmp, in0=ps_d, scalar=1.0 / 127.0, in1=x1b[t % 2],
                op0=ALU.mult, op1=ALU.add), sem="ve")
            p.wait("vector", "ve", None)
            p.op("vector", lambda e, t=t: e.tensor_add(
                ostg[t % 2], tmp, atb[t % 2]), sem="ve")
            dn_hist[t] = p.cnt["ve"]
            p.wait("gpsimd", "ve", None)
            p.dma("gpsimd", o_out[t], ostg[t % 2], "st")
    p.wait("gpsimd", "st", None)
    return p.build()


def run_stage_g(inputs, res_d, res_e, res_f):
    wg = np.ascontiguousarray(np.asarray(inputs["L1_ffn_w_gate"], np.int32))
    wu = np.ascontiguousarray(np.asarray(inputs["L1_ffn_w_up"], np.int32))
    wd = np.ascontiguousarray(np.asarray(inputs["L1_ffn_w_down"], np.int32))
    nc = build_g()
    in_maps = [{"xn2fm": np.asarray(res_e[c]["xn2fm"]),
                "x1": np.asarray(res_d[c]["x1"]),
                "at1": np.asarray(res_f[c]["at1"]),
                "wg": wg, "wu": wu, "wd": wd} for c in range(8)]
    res = run_bass_kernel_spmd(nc, in_maps, core_ids=list(range(8)))
    return res.results


def _bf16(x):
    return np.ascontiguousarray(x, dtype=BF16NP)


def _f32(x):
    return np.ascontiguousarray(x, dtype=np.float32)


def run_stage_a(inputs):
    x = _f32(inputs["L0_attn_x_bld"])
    wq_full = _f32(inputs["L0_attn_Wq"]).reshape(D, 16 * 64)
    wkvd = _f32(inputs["L0_attn_Wkv_down"])
    wku_full = _f32(inputs["L0_attn_Wk_up"]).reshape(256, 16 * 64)
    wvu_full = _f32(inputs["L0_attn_Wv_up"]).reshape(256, 16 * 64)
    nc = build_a()
    in_maps = []
    for core in range(8):
        b, g = core // 2, core % 2
        hsl = slice(g * 512, (g + 1) * 512)
        in_maps.append({
            "xf": np.ascontiguousarray(x[b]),
            "wq": np.ascontiguousarray(wq_full[:, hsl]),
            "wkvd": wkvd,
            "wku": np.ascontiguousarray(wku_full[:, hsl]),
            "wvu": np.ascontiguousarray(wvu_full[:, hsl]),
        })
    res = run_bass_kernel_spmd(nc, in_maps, core_ids=list(range(8)))
    return res.results


def kernel(**inputs):
    res_a = run_stage_a(inputs)
    res_b = run_stage_b(res_a)
    res_c = run_stage_c(inputs, res_b)
    res_d = run_stage_d(inputs, res_a, res_c)
    res_e = run_stage_e(inputs, res_d)
    res_f = run_stage_f(inputs, res_e)
    res_g = run_stage_g(inputs, res_d, res_e, res_f)
    out = np.empty((B, S, D), np.float32)
    for core in range(8):
        b, g = core // 2, core % 2
        out[b, g * HT:(g + 1) * HT, :] = np.asarray(
            res_g[core]["out"], np.float32).reshape(HT, D)
    return out


# version 02
# speedup vs baseline: 1.0085x; 1.0085x over previous
import math
from contextlib import ExitStack

import numpy as np
import ml_dtypes

import concourse.bass as bass
import concourse.mybir as mybir
from concourse.bass_utils import run_bass_kernel_spmd

F32 = mybir.dt.float32
BF16 = mybir.dt.bfloat16
I32 = mybir.dt.int32
AF = mybir.ActivationFunctionType
ALU = mybir.AluOpType
AX = mybir.AxisListType

P = 128
D = 1024
S = 4096
B = 4
HT = 2048
NT = HT // P
NTS = S // P
EXPB = 25.0

BF16NP = ml_dtypes.bfloat16


ARENA_F32 = 53136
SEMS = ("lx", "lg", "ld", "pe", "ve", "se", "vg", "ge", "st")


class Prog:

    ENGINES = ("sync", "tensor", "vector", "scalar", "gpsimd")

    def __init__(self, nc):
        self.nc = nc
        self.ctx = ExitStack()
        self.q = {e: [] for e in self.ENGINES}
        self.cnt = {}
        self._sems = {}
        self._ar = self.ctx.enter_context(
            nc.sbuf_tensor("arena", [P, ARENA_F32], F32)).ap()
        self._ar16 = self._ar.bitcast(BF16)
        self._frees = [(0, ARENA_F32 * 4)]
        self._amap = {}
        self.psA = self.ctx.enter_context(
            nc.psum_tensor("psA", [P, 2048], F32)).ap()
        self.psB = self.ctx.enter_context(
            nc.psum_tensor("psB", [P, 1024], F32)).ap()
        self.psC = self.ctx.enter_context(
            nc.psum_tensor("psC", [P, 512], F32)).ap()
        self.psT = self.ctx.enter_context(
            nc.psum_tensor("psT", [P, 1024], BF16)).ap()
        for s_ in SEMS:
            self.sem(s_)

    def carve(self, name, nbytes):
        nb = (nbytes + 63) // 64 * 64
        for i, (s, e) in enumerate(self._frees):
            if e - s >= nb:
                if s + nb == e:
                    self._frees.pop(i)
                else:
                    self._frees[i] = (s + nb, e)
                self._amap[name] = (s, nb)
                return s
        raise MemoryError(f"arena full: {name} needs {nb}B; free={self._frees}")

    def vf32(self, off, shape):
        assert off % 4 == 0
        n = int(np.prod(shape[1:]))
        v = self._ar[0:shape[0], off // 4:off // 4 + n]
        return self._shape(v, shape)

    def vbf16(self, off, shape):
        assert off % 2 == 0
        n = int(np.prod(shape[1:]))
        v = self._ar16[0:shape[0], off // 2:off // 2 + n]
        return self._shape(v, shape)

    @staticmethod
    def _shape(v, shape):
        if len(shape) <= 2:
            return v
        dims = {f"d{i}": shape[1 + i] for i in range(len(shape) - 1)}
        names = " ".join(f"d{i}" for i in range(len(shape) - 1))
        return v.rearrange(f"p ({names}) -> p {names}", **dims)

    def sb(self, name, shape, dtype):
        es = 2 if dtype == BF16 else 4
        off = self.carve(name, int(np.prod(shape[1:])) * es)
        return self.vbf16(off, shape) if dtype == BF16 else self.vf32(off, shape)

    def free(self, *names):
        for n in names:
            s, nb = self._amap.pop(n)
            self._frees.append((s, s + nb))
        self._frees.sort()
        merged = []
        for s, e in self._frees:
            if merged and merged[-1][1] == s:
                merged[-1] = (merged[-1][0], e)
            else:
                merged.append((s, e))
        self._frees = merged

    def free_prefix(self, pfx):
        self.free(*[n for n in list(self._amap) if n.startswith(pfx)])

    def barrier(self):
        for e in self.ENGINES:
            for s_, h in self._sems.items():
                v = self.cnt[s_]
                if v:
                    self.on(e, lambda eng, h=h, v=v: eng.wait_ge(h, v))

    def sem(self, name):
        h = self.ctx.enter_context(self.nc.semaphore(name))
        self.cnt[name] = 0
        self._sems[name] = h
        return h

    def _h(self, sem):
        return self._sems[sem]

    def on(self, eng, fn):
        self.q[eng].append(fn)

    def inc(self, sem, n=1):
        self.cnt[sem] += n
        return self.cnt[sem]

    def wait(self, eng, sem, val=None):
        v = self.cnt[sem] if val is None else val
        h = self._h(sem)
        self.on(eng, lambda e, h=h, v=v: e.wait_ge(h, v))

    def op(self, eng, fn, sem=None, n=1):
        if sem is None:
            self.on(eng, fn)
            return None
        h = self._h(sem)
        v = self.inc(sem, n)
        self.on(eng, lambda e, fn=fn, h=h, n=n: fn(e).then_inc(h, n))
        return v

    def dma(self, eng, out, in_, sem, **kw):
        return self.op(eng, lambda e, o=out, i=in_: e.dma_start(out=o, in_=i, **kw),
                       sem=sem, n=16)

    def build(self, guard_store_sem=None):
        with self.ctx:
            with self.nc.Block() as block:
                for eng in self.ENGINES:
                    cmds = self.q[eng]
                    if not cmds:
                        continue
                    deco = getattr(block, eng)

                    def body(e, cmds=cmds):
                        for fn in cmds:
                            fn(e)

                    deco(body)
        return self.nc


def new_nc():
    return bass.Bass("TRN2", target_bir_lowering=False, debug=False)


def _ident_np():
    return np.eye(P, dtype=BF16NP)


def emit_a(p, xf, wq, wkvd, wku, wvu, o_xnfm, o_kt, o_qt, o_va, ident):
    xb = [p.sb(f"a:xb{i}", [P, D], F32) for i in range(2)]
    xn16 = [p.sb(f"a:xn16{i}", [P, D], BF16) for i in range(2)]
    xnfm = p.sb("a:xnfm_sb", [P, 8, S], BF16)
    lat = p.sb("a:lat_sb", [P, 2, S], BF16)
    qt_sb = p.sb("a:qt_sb", [P, 4, S], BF16)
    kstg = [p.sb(f"a:kstg{i}", [P, S], BF16) for i in range(2)]
    vstg = [p.sb(f"a:vstg{i}", [P, 520], BF16) for i in range(2)]
    wq_sb = p.sb("a:wq_sb", [P, 8, 512], BF16)
    wkvd_sb = p.sb("a:wkvd_sb", [P, 8, 256], BF16)
    wku_sb = p.sb("a:wku_sb", [P, 2, 512], BF16)
    wvu_sb = p.sb("a:wvu_sb", [P, 2, 512], BF16)
    id_sb = p.sb("a:id_sb", [P, P], BF16)
    stt = p.sb("a:stt_sb", [P, 8], F32)
    sqscr = p.sb("a:sqscr", [P, D], F32)
    epsc = p.sb("a:epsc", [P, 1], F32)
    ps_tr = p.psT
    ps_mm = [p.psA[:, i * 512:(i + 1) * 512] for i in range(2)]

    p.dma("gpsimd", wq_sb, wq.rearrange("(a p) k -> p a k", p=P), "lg")
    p.dma("gpsimd", wkvd_sb, wkvd.rearrange("(a p) k -> p a k", p=P), "lg")
    p.dma("gpsimd", wku_sb, wku.rearrange("(a p) k -> p a k", p=P), "lg")
    p.dma("gpsimd", wvu_sb, wvu.rearrange("(a p) k -> p a k", p=P), "lg")
    p.dma("sync", id_sb, ident, "lx")
    p.op("gpsimd", lambda e: e.memset(epsc, 1e-5), sem="ve")
    p.op("gpsimd", lambda e: e.memset(vstg[0][:, :], 1.0), sem="ve")
    p.op("gpsimd", lambda e: e.memset(vstg[1][:, :], 1.0), sem="ve")
    wload = p.cnt["lg"]

    c = lambda i: stt[:, i:i + 1]
    for i in range(NTS):
        buf = xb[i % 2]
        xn = xn16[i % 2]
        if i >= 2:
            p.wait("sync", "se", None)
        p.dma("sync", buf, xf[i * P:(i + 1) * P, :], "lx")
        ldv = p.cnt["lx"]
        p.wait("vector", "lx", ldv)
        if i > 0:
            p.wait("vector", "se", None)
        p.op("vector", lambda e, b=buf, o=c(0): e.reduce_sum(o, b, axis=AX.X),
             sem="vg")
        p.wait("scalar", "lx", ldv)
        p.wait("scalar", "vg", None)
        p.op("scalar", lambda e, b=buf, o=c(1): e.activation(
            out=sqscr, in_=b, func=AF.Square, accum_out=o), sem="se")
        p.wait("vector", "se", None)
        p.op("vector", lambda e, o=c(2), a=c(0): e.tensor_scalar(
            out=o, in0=a, scalar1=1.0 / D, scalar2=None, op0=ALU.mult), sem="vg")
        p.wait("vector", "vg", None)
        p.op("vector", lambda e, o=c(3), a=c(1): e.tensor_scalar(
            out=o, in0=a, scalar1=1.0 / D, scalar2=None, op0=ALU.mult), sem="vg")
        p.wait("vector", "vg", None)
        p.op("vector", lambda e, o=c(4), a=c(2): e.tensor_tensor(
            o, a, a, ALU.mult), sem="vg")
        p.wait("vector", "vg", None)
        p.op("vector", lambda e, o=c(5), a=c(3), b2=c(4): e.tensor_tensor(
            o, a, b2, ALU.subtract), sem="vg")
        p.wait("scalar", "vg", None)
        p.op("scalar", lambda e, o=c(6), a=c(5): e.activation(
            out=o, in_=a, func=AF.Sqrt, bias=epsc[:, 0:1]), sem="se")
        p.wait("vector", "se", None)
        p.op("vector", lambda e, o=c(6), a=c(6): e.reciprocal(o, a), sem="vg")
        p.wait("vector", "vg", None)
        p.op("vector", lambda e, o=c(7), a=c(2), b2=c(6): e.scalar_tensor_tensor(
            out=o, in0=a, scalar=-1.0, in1=b2, op0=ALU.mult, op1=ALU.mult),
            sem="vg")
        p.wait("scalar", "vg", None)
        if i >= 2:
            p.wait("scalar", "pe", p.cnt["pe"] - 1)
        p.op("scalar", lambda e, b=buf, o=xn: e.activation(
            out=o, in_=b, func=AF.Identity, scale=c(6), bias=c(7)), sem="se")
        p.wait("tensor", "se", None)
        if i > 0:
            p.wait("tensor", "ve", None)
        for j in range(8):
            p.op("tensor", lambda e, xnj=xn[:, j * P:(j + 1) * P],
                 d=ps_tr[:, j * P:(j + 1) * P]: e.transpose(d, xnj, id_sb),
                 sem="pe" if j == 7 else None)
        p.wait("vector", "pe", None)
        p.op("vector", lambda e, i=i: e.tensor_copy(
            xnfm[:, :, i * P:(i + 1) * P],
            ps_tr.rearrange("p (a b) -> p a b", a=8)), sem="ve")
    p.wait("gpsimd", "ve", None)
    for a in range(8):
        p.dma("gpsimd", o_xnfm[a], xnfm[:, a, :], "st")

    mmctr = {"n": 0}

    def mm_block(chunks, dest_fn, evac_eng="vector"):
        k = mmctr["n"]
        buf = ps_mm[k % 2]
        if k >= 2:
            p.wait("tensor", "ve", mmctr[f"evac{k - 2}"])
        for ci, (lh, rh) in enumerate(chunks):
            p.op("tensor", lambda e, lh=lh, rh=rh, st=(ci == 0),
                 sp=(ci == len(chunks) - 1), b=buf: e.matmul(
                     b, lh, rh, start=st, stop=sp),
                 sem="pe" if ci == len(chunks) - 1 else None)
        p.wait(evac_eng, "pe", None)
        p.op(evac_eng, lambda e, b=buf: dest_fn(e, b), sem="ve")
        mmctr[f"evac{k}"] = p.cnt["ve"]
        mmctr["n"] += 1

    p.wait("tensor", "lg", wload)
    p.wait("tensor", "ve", None)
    for c2 in range(2):
        for sl in range(8):
            chunks = [(wkvd_sb[:, dk, c2 * P:(c2 + 1) * P],
                       xnfm[:, dk, sl * 512:(sl + 1) * 512]) for dk in range(8)]
            mm_block(chunks, lambda e, b, c2=c2, sl=sl: e.tensor_copy(
                lat[:, c2, sl * 512:(sl + 1) * 512], b))

    for qc in range(4):
        for sl in range(8):
            chunks = [(wq_sb[:, dk, qc * P:(qc + 1) * P],
                       xnfm[:, dk, sl * 512:(sl + 1) * 512]) for dk in range(8)]
            mm_block(chunks, lambda e, b, qc=qc, sl=sl: e.tensor_copy(
                qt_sb[:, qc, sl * 512:(sl + 1) * 512], b))
    p.wait("gpsimd", "ve", None)
    for qc in range(4):
        p.dma("gpsimd", o_qt[qc], qt_sb[:, qc, :], "st")

    for kc in range(4):
        stg = kstg[kc % 2]
        if kc >= 2:
            p.wait("tensor", "st", None)
        for sl in range(8):
            chunks = [(wku_sb[:, c2, kc * P:(kc + 1) * P],
                       lat[:, c2, sl * 512:(sl + 1) * 512]) for c2 in range(2)]
            mm_block(chunks, lambda e, b, sl=sl, stg=stg: e.tensor_copy(
                stg[:, sl * 512:(sl + 1) * 512], b))
        p.wait("gpsimd", "ve", None)
        p.dma("gpsimd", o_kt[kc], stg, "st")

    for kb in range(32):
        stg = vstg[kb % 2]
        if kb >= 2:
            p.wait("vector", "st", None)
        chunks = [(lat[:, c2, kb * P:(kb + 1) * P], wvu_sb[:, c2, :])
                  for c2 in range(2)]
        mm_block(chunks, lambda e, b, stg=stg: e.tensor_copy(
            stg.rearrange("p (h v) -> p h v", h=8)[:, :, 0:64],
            b.rearrange("p (h v) -> p h v", h=8)))
        p.wait("gpsimd", "ve", None)
        p.dma("gpsimd", o_va[kb], stg, "st")

    p.wait("gpsimd", "st", None)
    p.free_prefix("a:")


def _mask_np():
    u = np.arange(896)[None, :] - 384
    r = np.arange(P)[:, None]
    return (u >= r).astype(BF16NP)


def emit_b(p, kt, qt, va, o_y, maskc):
    kt_sb = p.sb("b:kt_sb", [P, 4, S], BF16)
    qt_sb = p.sb("b:qt_sb", [P, 4, S], BF16)
    va_sb = p.sb("b:va_sb", [P, 32, 8, 65], BF16)
    mk_sb = p.sb("b:mk_sb", [P, 896], BF16)
    ptb = [p.sb(f"b:ptb{i}", [P, 1024], BF16) for i in range(2)]
    ystg = [p.sb(f"b:ystg{i}", [P, S], BF16) for i in range(2)]
    rcp16 = p.sb("b:rcp16", [1, 512], BF16)
    den_sb = p.sb("b:den_sb", [64, 512], F32)
    ones1 = p.sb("b:ones1", [1, 64], BF16)
    expb = p.sb("b:expb", [P, 1], F32)
    ps_sg = [p.psA[:, i * 1024:(i + 1) * 1024] for i in range(2)]
    ps_y = [p.psB[:, i * 512:(i + 1) * 512] for i in range(2)]
    ps_b = p.psC[0:64, 0:512]

    p.dma("sync", kt_sb, kt.rearrange("a p t -> p a t"), "ld")
    p.dma("sync", qt_sb, qt.rearrange("a p t -> p a t"), "ld")
    p.dma("sync", va_sb.rearrange("p b h v -> p b (h v)"),
          va.rearrange("b p x -> p b x"), "ld")
    p.dma("sync", mk_sb, maskc, "ld")
    p.op("gpsimd", lambda e: e.memset(ones1, 1.0), sem="ve")
    p.op("gpsimd", lambda e: e.memset(expb, -EXPB), sem="ve")
    ldall = p.cnt["ld"]
    p.wait("tensor", "ld", ldall)
    p.wait("tensor", "ve", None)
    p.wait("scalar", "ld", ldall)
    p.wait("scalar", "ve", None)
    p.wait("vector", "ld", ldall)

    se_hist = {}
    ynorm_hist = {}
    dma_hist = {}
    for h in range(8):
        cc, rb = h // 2, (h % 2) * 64
        for sl in range(8):
            q0 = sl * 512
            nkb = 4 * (sl + 1)
            yb = ps_y[sl % 2]
            gsl = (h, sl)
            prev = (h, sl - 2) if sl >= 2 else ((h - 1, sl + 6) if h > 0 else None)
            if prev in ynorm_hist:
                p.wait("tensor", "ve", ynorm_hist[prev])
            for g in range(nkb // 2):
                kb0 = 2 * g
                if (h, sl, g - 2) in se_hist:
                    p.wait("tensor", "se", se_hist[(h, sl, g - 2)])
                elif g < 2:
                    lastg = se_hist.get("last2", None)
                    if lastg is not None and g == 0:
                        p.wait("tensor", "se", lastg[0])
                    if lastg is not None and g == 1:
                        p.wait("tensor", "se", lastg[1])
                for j in range(2):
                    kb = kb0 + j
                    p.op("tensor", lambda e, cc=cc, rb=rb, kb=kb, q0=q0, g=g, j=j:
                         e.matmul(ps_sg[g % 2][:, j * 512:(j + 1) * 512],
                                  kt_sb[rb:rb + 64, cc, kb * P:(kb + 1) * P],
                                  qt_sb[rb:rb + 64, cc, q0:q0 + 512],
                                  start=True, stop=True, skip_group_check=True),
                         sem="pe" if j == 1 else None)
                p.wait("scalar", "pe", None)
                p.op("scalar", lambda e, g=g: e.activation(
                    out=ptb[g % 2], in_=ps_sg[g % 2], func=AF.Exp,
                    scale=0.125, bias=expb[:, 0:1]), sem="se")
                se_hist[(h, sl, g)] = p.cnt["se"]
                nmask = 0
                for j in range(2):
                    kb = kb0 + j
                    delta = kb * P - q0
                    if 0 <= delta <= 384:
                        off = 384 - delta
                        p.wait("vector", "se", None)
                        p.op("vector", lambda e, g=g, j=j, off=off:
                             e.tensor_tensor(
                                 ptb[g % 2][:, j * 512:(j + 1) * 512],
                                 ptb[g % 2][:, j * 512:(j + 1) * 512],
                                 mk_sb[:, off:off + 512], ALU.mult), sem="ve")
                        nmask += 1
                if nmask:
                    p.wait("tensor", "ve", None)
                else:
                    p.wait("tensor", "se", None)
                for j in range(2):
                    kb = kb0 + j
                    p.op("tensor", lambda e, kb=kb, h=h, g=g, j=j, yb=yb,
                         st=(kb == 0), sp=(kb == nkb - 1): e.matmul(
                             yb[0:65, :], va_sb[:, kb, h, :],
                             ptb[g % 2][:, j * 512:(j + 1) * 512],
                             start=st, stop=sp, skip_group_check=True),
                         sem="pe" if j == 1 else None)
            se_hist["last2"] = (se_hist[(h, sl, nkb // 2 - 2)],
                                se_hist[(h, sl, nkb // 2 - 1)])
            p.wait("vector", "pe", None)

            def _recip_lp(e, yb=yb):
                with e.bass.allow_low_precision(
                        reason="softmax denom reciprocal; common scale cancels"):
                    return e.reciprocal(rcp16, yb[64:65, :])

            p.op("vector", _recip_lp, sem="ve")
            p.wait("tensor", "ve", None)
            p.op("tensor", lambda e: e.matmul(
                ps_b, ones1, rcp16, start=True, stop=True,
                skip_group_check=True), sem="pe")
            p.wait("vector", "pe", None)
            p.op("vector", lambda e: e.tensor_copy(den_sb, ps_b), sem="ve")
            if (cc % 2, "dma") in dma_hist and cc >= 2:
                p.wait("vector", "st", dma_hist[(cc % 2, "dma")])
            p.wait("vector", "ve", None)
            p.op("vector", lambda e, yb=yb, rb=rb, q0=q0, cc=cc:
                 e.tensor_tensor(ystg[cc % 2][rb:rb + 64, q0:q0 + 512],
                                 yb[0:64, :], den_sb, ALU.mult), sem="ve")
            ynorm_hist[gsl] = p.cnt["ve"]
        if h % 2 == 1:
            p.wait("gpsimd", "ve", None)
            p.dma("gpsimd", o_y[cc], ystg[cc % 2], "st")
            dma_hist[(cc % 2, "dma")] = p.cnt["st"]

    p.wait("gpsimd", "st", None)
    p.free_prefix("b:")


def emit_c(p, y_all, wout, x_own, o_xr):
    y_sb = p.sb("c:y_sb", [P, 8, HT], BF16)
    wo_sb = p.sb("c:wo_sb", [P, 8, D], BF16)
    xb = [p.sb(f"c:xb{i}", [P, D], F32) for i in range(2)]
    stg = [p.sb(f"c:stg{i}", [P, D], F32) for i in range(2)]
    ps = [p.psA[:, i * 1024:(i + 1) * 1024] for i in range(2)]

    p.dma("sync", y_sb, y_all.rearrange("a p t -> p a t"), "lx")
    p.dma("gpsimd", wo_sb, wout.rearrange("(a p) k -> p a k", p=P), "lg")
    evac_hist = {}
    for t in range(NT):
        xbuf = xb[t % 2]
        if t >= 2:
            p.wait("sync", "ve", evac_hist[t - 2])
        p.dma("sync", xbuf, x_own[t * P:(t + 1) * P, :], "lx")
        ldv = p.cnt["lx"]
        pb = ps[t % 2]
        p.wait("tensor", "lx", 16)
        p.wait("tensor", "lg", 16)
        if t >= 2:
            p.wait("tensor", "ve", evac_hist[t - 2])
        for ic in range(8):
            for half in range(2):
                p.op("tensor", lambda e, ic=ic, half=half, t=t, pb=pb: e.matmul(
                    pb[:, half * 512:(half + 1) * 512],
                    y_sb[:, ic, t * P:(t + 1) * P],
                    wo_sb[:, ic, half * 512:(half + 1) * 512],
                    start=(ic == 0), stop=(ic == 7), skip_group_check=True),
                    sem="pe" if (ic == 7 and half == 1) else None)
        p.wait("vector", "lx", ldv)
        p.wait("vector", "pe", None)
        if t >= 2:
            p.wait("vector", "st", None)
        p.op("vector", lambda e, t=t, pb=pb, xbuf=xbuf: e.tensor_add(
            stg[t % 2], pb, xbuf), sem="ve")
        evac_hist[t] = p.cnt["ve"]
        p.wait("gpsimd", "ve", None)
        p.dma("gpsimd", o_xr[t], stg[t % 2], "st")
    p.wait("gpsimd", "st", None)
    p.free_prefix("c:")


def emit_d(p, xr, xnfm, wfc, lnw, lnb, wproj, o_x1, ident, id32):
    wfcT = p.sb("d:wfcT", [P, 8, 4096], BF16)
    wprT = p.sb("d:wprT", [P, 32, D], BF16)
    dof = p.carve("d:arena", 53248)
    arena = p.vf32(dof, [P, 13312])
    w3s = [arena[:, 0:4096], arena[:, 4096:8192]]
    sgna = p.vbf16(dof + 32768, [P, 4096])
    sgnb = p.vbf16(dof + 40960, [P, 4096])
    xnf = [p.vbf16(dof, [P, 8, P]),
           p.vbf16(dof + 2048, [P, 8, P])]
    h_sb = p.vbf16(dof + 4096, [P, 4096])
    hT = p.vbf16(dof + 12288, [P, 32, P])
    lnwbc = p.vbf16(dof + 20480, [P, 4096])
    lnbbc = p.vbf16(dof + 28672, [P, 4096])
    xrb = [arena[:, 9216:10240], arena[:, 10240:11264]]
    x1s = [arena[:, 11264:12288], arena[:, 12288:13312]]
    id_sb = p.sb("d:id_sb", [P, P], BF16)
    id32_sb = p.sb("d:id32_sb", [P, P], F32)
    onesf = p.sb("d:onesf", [1, P], F32)
    ones16 = p.sb("d:ones16", [1, P], BF16)
    lnrow = p.sb("d:lnrow", [1, 8192], BF16)
    stt = p.sb("d:stt_d", [P, 12], F32)
    g11 = p.sb("d:g11", [1, 4], F32)
    thr = p.sb("d:thr_d", [P, 4], F32)
    epsc = p.sb("d:epsc_d", [P, 1], F32)
    ps_fc = p.psA
    ps_tr = p.psT
    ps_pr = p.psB
    ps_sc = p.psC[:, 0:128]

    p.dma("sync", id_sb, ident, "lx")
    p.dma("sync", id32_sb, id32, "lx")
    p.op("gpsimd", lambda e: e.memset(onesf, 1.0), sem="ve")
    p.op("gpsimd", lambda e: e.memset(ones16, 1.0), sem="ve")
    p.op("gpsimd", lambda e: e.memset(epsc, 1e-5), sem="ve")
    p.dma("gpsimd", lnrow[0:1, 0:4096], lnw, "lg")
    p.dma("gpsimd", lnrow[0:1, 4096:8192], lnb, "lg")
    cload = p.cnt["lg"]

    def gamma_phase(src_fn, ngr, gcols, tcols):
        for gidx in range(ngr):
            bb = w3s[gidx % 2][:, 0:gcols]
            if gidx >= 2:
                p.wait("sync", "vg", None)
            p.dma("sync", bb.rearrange("p (k d) -> p k d", d=1024),
                  src_fn(gidx), "lx")
            p.wait("vector", "lx", p.cnt["lx"])
            dst = stt[:, 8 + gidx % 2:9 + gidx % 2]
            p.op("vector", lambda e, bb=bb, dst=dst: e.tensor_reduce(
                dst, bb, axis=AX.X, op=ALU.add,
                apply_absolute_value=True), sem="vg")
            p.wait("vector", "vg", None)
            if gidx == 0:
                p.op("vector", lambda e, dst=dst: e.tensor_copy(
                    stt[:, 10:11], dst), sem="vg")
            else:
                p.op("vector", lambda e, dst=dst: e.tensor_add(
                    stt[:, 10:11], stt[:, 10:11], dst), sem="vg")
            p.wait("vector", "vg", None)
        p.wait("tensor", "vg", None)
        p.op("tensor", lambda e: e.transpose(
            ps_sc[0:1, 0:P], stt[:, 10:11], id32_sb), sem="pe")
        p.wait("vector", "pe", None)
        p.op("vector", lambda e: e.reduce_sum(
            g11[0:1, 0:1], ps_sc[0:1, 0:P], axis=AX.X), sem="vg")
        p.wait("vector", "vg", None)
        p.op("vector", lambda e: e.tensor_scalar(
            out=g11[0:1, 1:2], in0=g11[0:1, 0:1], scalar1=1.0 / (4096.0 * D),
            scalar2=1e-6, op0=ALU.mult, op1=ALU.add), sem="vg")
        p.wait("vector", "vg", None)
        p.op("vector", lambda e: e.tensor_scalar(
            out=g11[0:1, 2:3], in0=g11[0:1, 1:2], scalar1=0.5, scalar2=None,
            op0=ALU.mult), sem="vg")
        p.wait("vector", "vg", None)
        p.op("vector", lambda e: e.tensor_scalar(
            out=g11[0:1, 3:4], in0=g11[0:1, 2:3], scalar1=-1.0, scalar2=None,
            op0=ALU.mult), sem="vg")
        p.wait("vector", "vg", None)
        p.wait("tensor", "vg", None)
        p.op("tensor", lambda e: e.matmul(
            ps_sc[:, 0:2], onesf, g11[0:1, 2:4], start=True, stop=True,
            skip_group_check=True), sem="pe")
        p.wait("vector", "pe", None)
        p.op("vector", lambda e, tcols=tcols: e.tensor_copy(
            thr[:, tcols[0]:tcols[1] + 1], ps_sc[:, 0:2]), sem="ve")

    def quant_phase(src_fn, ntile, cols, tpos, tneg, dst_fn):
        rounds = cols // 1024
        for t in range(ntile):
            bb = w3s[t % 2][:, 0:cols]
            if t >= 2:
                p.wait("sync", "se", None)
            p.dma("sync", bb.rearrange("p (k d) -> p k d", d=1024),
                  src_fn(t), "lx")
            p.wait("scalar", "lx", p.cnt["lx"])
            p.wait("scalar", "ve", None)
            if t >= 1:
                p.wait("scalar", "ge", None)
                p.wait("scalar", "pe", None)
            sa_, sb_ = sgna[:, 0:cols], sgnb[:, 0:cols]
            p.op("scalar", lambda e, bb=bb, sa_=sa_, tn=tneg: e.activation(
                out=sa_, in_=bb, func=AF.Sign, bias=thr[:, tn:tn + 1]), sem="se")
            p.op("scalar", lambda e, bb=bb, sb_=sb_, tp=tpos: e.activation(
                out=sb_, in_=bb, func=AF.Sign, bias=thr[:, tp:tp + 1]), sem="se")
            p.wait("gpsimd", "se", None)
            if t >= 1:
                p.wait("gpsimd", "pe", None)
            p.op("gpsimd", lambda e, sa_=sa_, sb_=sb_: e.tensor_tensor(
                sa_, sa_, sb_, ALU.add), sem="ge")
            for r in range(rounds):
                p.wait("tensor", "ge" if r == 0 else "ve", None)
                for j in range(8):
                    p.op("tensor", lambda e, r=r, j=j, sa_=sa_: e.transpose(
                        ps_tr[:, j * P:(j + 1) * P],
                        sa_[:, (r * 8 + j) * P:(r * 8 + j + 1) * P], id_sb),
                        sem="pe" if j == 7 else None)
                p.wait("vector", "pe", None)
                p.op("vector", lambda e, t=t, r=r: e.tensor_copy(
                    dst_fn(t, r), ps_tr.rearrange("p (a b) -> p a b", a=8)),
                    sem="ve")

    wfc_r = wfc.rearrange("(a p) d -> p a d", p=P)
    gamma_phase(lambda g: wfc_r[:, 4 * g:4 * (g + 1), :], 8, 4096, (0, 1))
    quant_phase(lambda t: wfc_r[:, t:t + 1, :], 32, 1024, 0, 1,
                lambda t, r: wfcT[:, :, t * P:(t + 1) * P])
    wpr_r = wproj.rearrange("(a p) h -> p a h", p=P)
    gamma_phase(lambda g: wpr_r[:, g:g + 1, :].rearrange(
        "p a (k d) -> p (a k) d", d=1024), 8, 4096, (2, 3))
    quant_phase(lambda t: wpr_r[:, t:t + 1, :].rearrange(
        "p a (k d) -> p (a k) d", d=1024), 8, 4096, 2, 3,
                lambda t, r: wprT[:, r * 8:(r + 1) * 8, t * P:(t + 1) * P])

    p.wait("vector", "lg", cload)
    p.wait("vector", "se", None)
    p.wait("tensor", "se", None)
    for piece in range(16):
        src = lnrow[0:1, piece * 512:(piece + 1) * 512]
        dst = (lnwbc if piece < 8 else lnbbc)
        doff = (piece % 8) * 512
        p.wait("tensor", "ve", None)
        p.op("tensor", lambda e, src=src: e.matmul(
            ps_sc.bitcast(BF16)[:, 0:512] if False else ps_fc[:, 0:512],
            ones16, src, start=True, stop=True, skip_group_check=True),
            sem="pe")
        p.wait("vector", "pe", None)
        p.op("vector", lambda e, dst=dst, doff=doff: e.tensor_copy(
            dst[:, doff:doff + 512], ps_fc[:, 0:512]), sem="ve")

    c = lambda i: stt[:, i:i + 1]
    f_hist = {}
    p.wait("sync", "pe", None)
    p.wait("sync", "se", None)
    p.wait("sync", "ge", None)
    p.wait("sync", "ve", None)
    for t in range(NT):
        xnb = xnf[t % 2]
        xrbuf = xrb[t % 2]
        if t >= 2:
            p.wait("sync", "pe", f_hist[(t - 2, "lastmm")])
            p.wait("sync", "ve", f_hist[(t - 2, "evac")])
        p.dma("sync", xnb,
              xnfm.rearrange("a p t -> p a t")[:, :, t * P:(t + 1) * P], "lx")
        p.dma("sync", xrbuf, xr[t], "lx")
        ldv = p.cnt["lx"]
        p.wait("tensor", "lx", ldv)
        if t >= 1:
            p.wait("tensor", "se", f_hist[(t - 1, "sq")])
        for half in range(2):
            if half == 1:
                p.wait("tensor", "se", None)
            for dj in range(8):
                for sub in range(4):
                    p.op("tensor", lambda e, dj=dj, sub=sub, half=half, xnb=xnb:
                         e.matmul(ps_fc[:, sub * 512:(sub + 1) * 512],
                                  xnb[:, dj, :],
                                  wfcT[:, dj, half * 2048 + sub * 512:
                                       half * 2048 + (sub + 1) * 512],
                                  start=(dj == 0), stop=(dj == 7),
                                  skip_group_check=True),
                         sem="pe" if (dj == 7 and sub == 3) else None)
            p.wait("scalar", "pe", None)
            for gg in range(2):
                p.op("scalar", lambda e, half=half, gg=gg: e.activation(
                    out=h_sb[:, half * 2048 + gg * 1024:
                             half * 2048 + (gg + 1) * 1024],
                    in_=ps_fc[:, gg * 1024:(gg + 1) * 1024], func=AF.Gelu,
                    scale=0.5,
                    accum_out=stt[:, 8 + half * 2 + gg:9 + half * 2 + gg]),
                    sem="se")
        p.wait("vector", "se", None)
        p.op("vector", lambda e: e.reduce_sum(c(0), stt[:, 8:12], axis=AX.X),
             sem="vg")
        p.wait("scalar", "vg", None)
        for gg in range(4):
            p.op("scalar", lambda e, gg=gg: e.activation(
                out=ps_fc[:, (gg % 2) * 1024:(gg % 2 + 1) * 1024],
                in_=h_sb[:, gg * 1024:(gg + 1) * 1024], func=AF.Square,
                accum_out=stt[:, 8 + gg:9 + gg]), sem="se")
        f_hist[(t, "sq")] = p.cnt["se"]
        p.wait("vector", "se", None)
        p.op("vector", lambda e: e.reduce_sum(c(1), stt[:, 8:12], axis=AX.X),
             sem="vg")
        p.wait("vector", "vg", None)
        p.op("vector", lambda e: e.tensor_scalar(
            out=c(2), in0=c(0), scalar1=1.0 / 4096, scalar2=None,
            op0=ALU.mult), sem="vg")
        p.wait("vector", "vg", None)
        p.op("vector", lambda e: e.tensor_scalar(
            out=c(3), in0=c(1), scalar1=1.0 / 4096, scalar2=None,
            op0=ALU.mult), sem="vg")
        p.wait("vector", "vg", None)
        p.op("vector", lambda e: e.tensor_tensor(c(4), c(2), c(2), ALU.mult),
             sem="vg")
        p.wait("vector", "vg", None)
        p.op("vector", lambda e: e.tensor_tensor(c(5), c(3), c(4), ALU.subtract),
             sem="vg")
        p.wait("scalar", "vg", None)
        p.op("scalar", lambda e: e.activation(
            out=c(6), in_=c(5), func=AF.Sqrt, bias=epsc[:, 0:1]), sem="se")
        p.wait("vector", "se", None)
        p.op("vector", lambda e: e.reciprocal(c(6), c(6)), sem="vg")
        p.wait("vector", "vg", None)
        p.op("vector", lambda e: e.scalar_tensor_tensor(
            out=c(7), in0=c(2), scalar=-1.0, in1=c(6), op0=ALU.mult,
            op1=ALU.mult), sem="vg")
        p.wait("scalar", "vg", None)
        p.op("scalar", lambda e: e.activation(
            out=h_sb, in_=h_sb, func=AF.Identity, scale=c(6), bias=c(7)),
            sem="se")
        p.wait("vector", "se", None)
        p.op("vector", lambda e: e.tensor_tensor(h_sb, h_sb, lnwbc, ALU.mult),
             sem="ve")
        p.op("vector", lambda e: e.tensor_add(h_sb, h_sb, lnbbc), sem="ve")
        for r in range(4):
            p.wait("tensor", "ve", None)
            for j in range(8):
                p.op("tensor", lambda e, r=r, j=j: e.transpose(
                    ps_tr[:, j * P:(j + 1) * P],
                    h_sb[:, (r * 8 + j) * P:(r * 8 + j + 1) * P], id_sb),
                    sem="pe" if j == 7 else None)
            p.wait("vector", "pe", None)
            p.op("vector", lambda e, r=r: e.tensor_copy(
                hT[:, r * 8:(r + 1) * 8, :],
                ps_tr.rearrange("p (a b) -> p a b", a=8)), sem="ve")
        p.wait("tensor", "ve", None)
        for hc in range(32):
            for half in range(2):
                p.op("tensor", lambda e, hc=hc, half=half: e.matmul(
                    ps_pr[:, half * 512:(half + 1) * 512], hT[:, hc, :],
                    wprT[:, hc, half * 512:(half + 1) * 512],
                    start=(hc == 0), stop=(hc == 31), skip_group_check=True),
                    sem="pe" if (hc == 31 and half == 1) else None)
        f_hist[(t, "lastmm")] = p.cnt["pe"]
        p.wait("vector", "pe", None)
        p.wait("vector", "lx", ldv)
        if t >= 2:
            p.wait("vector", "st", None)
        p.op("vector", lambda e, t=t, xrbuf=xrbuf: e.scalar_tensor_tensor(
            out=x1s[t % 2], in0=ps_pr, scalar=0.5, in1=xrbuf, op0=ALU.mult,
            op1=ALU.add), sem="ve")
        f_hist[(t, "evac")] = p.cnt["ve"]
        p.wait("gpsimd", "ve", None)
        p.dma("gpsimd", o_x1[t], x1s[t % 2], "st")
    p.wait("gpsimd", "st", None)
    p.free_prefix("d:")


def _pa_np():
    pa0 = np.zeros((8, P, P), BF16NP)
    pa1 = np.zeros((8, P, P), BF16NP)
    pb = np.zeros((8, P, P), BF16NP)
    for h in range(8):
        for a in range(P):
            pa0[h, 16 * h + a // 16, a] = 1
            pa1[h, 16 * h + 8 + a // 16, a] = 1
            pb[h, 16 * h + a % 16, a] = 1
    return pa0, pa1, pb


def emit_e(p, x1, gamt, wq1, wkv1, o_xn2, o_qx, o_kvp, ident, c_pa0, c_pa1,
           c_pb):
    xb = [p.sb(f"e:xb{i}", [P, D], F32) for i in range(2)]
    xn16 = [p.sb(f"e:xn16{i}", [P, D], BF16) for i in range(2)]
    nr16 = [p.sb(f"e:nr16{i}", [P, D], BF16) for i in range(2)]
    xn2fm = p.sb("e:xn2fm_sb", [P, 8, HT], BF16)
    nrmfm = p.sb("e:nrmfm_sb", [P, 8, HT], BF16)
    q_sb = p.sb("e:q_sb", [P, HT], BF16)
    k_sb = p.sb("e:k_sb", [P, HT], BF16)
    v_sb = p.sb("e:v_sb", [P, HT], BF16)
    k_tm = p.sb("e:k_tm", [P, NT, P], BF16)
    vaug = p.sb("e:vaug_e", [P, NT, 8, 17], BF16)
    kexp = p.sb("e:kexp", [P, NT, 8, 273], BF16)
    qc0 = [p.sb(f"e:qc0_{i}", [P, HT], BF16) for i in range(2)]
    qx = [p.sb(f"e:qx_{i}", [P, HT], BF16) for i in range(2)]
    kvp_sb = p.sb("e:kvp_sb", [P, 8, 3, 17], F32)
    ab_scr = p.sb("e:ab_scr", [P, 512], BF16)
    wq1_sb = p.sb("e:wq1_sb", [P, 8, P], BF16)
    wkv1_sb = p.sb("e:wkv1_sb", [P, 8, 256], BF16)
    gam_sb = p.sb("e:gam_sb", [P, 8], F32)
    pa0_sb = p.sb("e:pa0_sb", [P, 8, P], BF16)
    pa1_sb = p.sb("e:pa1_sb", [P, 8, P], BF16)
    pb_sb = p.sb("e:pb_sb", [P, 8, P], BF16)
    id_sb = p.sb("e:id_sb_e", [P, P], BF16)
    stt = p.sb("e:stt_e", [P, 16], F32)
    sqscr = p.sb("e:sqscr_e", [P, D], F32)
    epsc = p.sb("e:epsc_e", [P, 1], F32)
    ps_tr = p.psT
    ps_mm = [p.psA[:, i * 512:(i + 1) * 512] for i in range(2)]
    ps_ab = [p.psA[:, 1024 + i * 512:1024 + (i + 1) * 512] for i in range(2)]

    p.dma("sync", id_sb, ident, "lx")
    p.dma("gpsimd", wq1_sb, wq1.rearrange("(a p) k -> p a k", p=P), "lg")
    p.dma("gpsimd", wkv1_sb, wkv1.rearrange("(a p) k -> p a k", p=P), "lg")
    p.dma("gpsimd", gam_sb, gamt, "lg")
    p.dma("gpsimd", pa0_sb, c_pa0.rearrange("a p k -> p a k"), "lg")
    p.dma("gpsimd", pa1_sb, c_pa1.rearrange("a p k -> p a k"), "lg")
    p.dma("gpsimd", pb_sb, c_pb.rearrange("a p k -> p a k"), "lg")
    p.op("gpsimd", lambda e: e.memset(epsc, 1e-5), sem="ve")
    p.op("gpsimd", lambda e: e.memset(vaug[:, :, :, 16:17], 1.0), sem="ve")
    p.op("gpsimd", lambda e: e.memset(kexp[:, :, :, 0:1], 1.0), sem="ve")
    p.op("gpsimd", lambda e: e.memset(qc0[0], 0.0), sem="ve")
    p.op("gpsimd", lambda e: e.memset(qc0[1], 0.0), sem="ve")
    p.op("gpsimd", lambda e: e.memset(qc0[0][0:1, :], 1.0), sem="ve")
    p.op("gpsimd", lambda e: e.memset(qc0[1][0:1, :], 1.0), sem="ve")
    p.op("gpsimd", lambda e: e.memset(kvp_sb.rearrange("p a b c -> p (a b c)"),
                                      0.0), sem="ve")
    wload = p.cnt["lg"]
    p.wait("vector", "lg", wload)
    p.op("vector", lambda e: e.tensor_tensor(
        wq1_sb, wq1_sb, gam_sb[:, :, None].broadcast_to([P, 8, P]),
        ALU.mult), sem="vg")
    p.wait("vector", "vg", None)
    p.op("vector", lambda e: e.tensor_tensor(
        wkv1_sb, wkv1_sb, gam_sb[:, :, None].broadcast_to([P, 8, 256]),
        ALU.mult), sem="vg")
    gfold = p.cnt["vg"]

    c = lambda i: stt[:, i:i + 1]
    for i in range(NT):
        buf, xn, nr = xb[i % 2], xn16[i % 2], nr16[i % 2]
        if i >= 2:
            p.wait("sync", "se", None)
        p.dma("sync", buf, x1[i], "lx")
        ldv = p.cnt["lx"]
        p.wait("vector", "lx", ldv)
        if i > 0:
            p.wait("vector", "se", None)
        p.op("vector", lambda e, b=buf: e.reduce_sum(c(0), b, axis=AX.X),
             sem="vg")
        p.wait("scalar", "lx", ldv)
        p.wait("scalar", "vg", None)
        p.op("scalar", lambda e, b=buf: e.activation(
            out=sqscr, in_=b, func=AF.Square, accum_out=c(1)), sem="se")
        p.wait("vector", "se", None)
        p.op("vector", lambda e: e.tensor_scalar(
            out=c(2), in0=c(0), scalar1=1.0 / D, scalar2=None, op0=ALU.mult),
            sem="vg")
        p.wait("vector", "vg", None)
        p.op("vector", lambda e: e.tensor_scalar(
            out=c(3), in0=c(1), scalar1=1.0 / D, scalar2=None, op0=ALU.mult),
            sem="vg")
        p.wait("vector", "vg", None)
        p.op("vector", lambda e: e.tensor_tensor(c(4), c(2), c(2), ALU.mult),
             sem="vg")
        p.wait("vector", "vg", None)
        p.op("vector", lambda e: e.tensor_tensor(c(5), c(3), c(4), ALU.subtract),
             sem="vg")
        p.wait("scalar", "vg", None)
        p.op("scalar", lambda e: e.activation(
            out=c(6), in_=c(5), func=AF.Sqrt, bias=epsc[:, 0:1]), sem="se")
        p.wait("vector", "se", None)
        p.op("vector", lambda e: e.reciprocal(c(6), c(6)), sem="vg")
        p.wait("vector", "vg", None)
        p.op("vector", lambda e: e.scalar_tensor_tensor(
            out=c(7), in0=c(2), scalar=-1.0, in1=c(6), op0=ALU.mult,
            op1=ALU.mult), sem="vg")
        p.wait("vector", "vg", None)
        p.op("vector", lambda e: e.tensor_scalar(
            out=c(8), in0=c(5), scalar1=1e-5, scalar2=None, op0=ALU.add),
            sem="vg")
        p.wait("vector", "vg", None)
        p.op("vector", lambda e: e.reciprocal(c(8), c(8)), sem="vg")
        p.wait("vector", "vg", None)
        p.op("vector", lambda e: e.tensor_tensor(c(9), c(5), c(8), ALU.mult),
             sem="vg")
        p.wait("vector", "vg", None)
        p.op("vector", lambda e: e.tensor_scalar(
            out=c(9), in0=c(9), scalar1=9.765625e-28, scalar2=None,
            op0=ALU.max), sem="vg")
        p.wait("scalar", "vg", None)
        p.op("scalar", lambda e: e.activation(
            out=c(10), in_=c(9), func=AF.Sqrt), sem="se")
        p.wait("vector", "se", None)
        p.op("vector", lambda e: e.reciprocal(c(10), c(10)), sem="vg")
        p.wait("vector", "vg", None)
        p.op("vector", lambda e: e.tensor_tensor(c(11), c(6), c(10), ALU.mult),
             sem="vg")
        p.wait("vector", "vg", None)
        p.op("vector", lambda e: e.tensor_tensor(c(12), c(7), c(10), ALU.mult),
             sem="vg")
        p.wait("scalar", "vg", None)
        if i >= 2:
            p.wait("scalar", "pe", p.cnt["pe"] - 2)
        p.op("scalar", lambda e, b=buf, o=xn: e.activation(
            out=o, in_=b, func=AF.Identity, scale=c(6), bias=c(7)), sem="se")
        p.op("scalar", lambda e, b=buf, o=nr: e.activation(
            out=o, in_=b, func=AF.Identity, scale=c(11), bias=c(12)), sem="se")
        p.wait("tensor", "se", None)
        if i > 0:
            p.wait("tensor", "ve", None)
        for j in range(8):
            p.op("tensor", lambda e, xn=xn, j=j: e.transpose(
                ps_tr[:, j * P:(j + 1) * P], xn[:, j * P:(j + 1) * P], id_sb),
                sem="pe" if j == 7 else None)
        p.wait("vector", "pe", None)
        p.op("vector", lambda e, i=i: e.tensor_copy(
            xn2fm[:, :, i * P:(i + 1) * P],
            ps_tr.rearrange("p (a b) -> p a b", a=8)), sem="ve")
        p.wait("tensor", "ve", None)
        for j in range(8):
            p.op("tensor", lambda e, nr=nr, j=j: e.transpose(
                ps_tr[:, j * P:(j + 1) * P], nr[:, j * P:(j + 1) * P], id_sb),
                sem="pe" if j == 7 else None)
        p.wait("vector", "pe", None)
        p.op("vector", lambda e, i=i: e.tensor_copy(
            nrmfm[:, :, i * P:(i + 1) * P],
            ps_tr.rearrange("p (a b) -> p a b", a=8)), sem="ve")
    p.wait("gpsimd", "ve", None)
    for a in range(8):
        p.dma("gpsimd", o_xn2[a], xn2fm[:, a, :], "st")

    mmctr = {"n": 0}

    def mm_block(chunks, dest_fn, evac_eng="vector"):
        k = mmctr["n"]
        buf = ps_mm[k % 2]
        if k >= 2:
            p.wait("tensor", "ve", mmctr[f"evac{k - 2}"])
        for ci, (lh, rh) in enumerate(chunks):
            p.op("tensor", lambda e, lh=lh, rh=rh, st=(ci == 0),
                 sp=(ci == len(chunks) - 1), b=buf: e.matmul(
                     b, lh, rh, start=st, stop=sp, skip_group_check=True),
                 sem="pe" if ci == len(chunks) - 1 else None)
        p.wait(evac_eng, "pe", None)
        p.op(evac_eng, lambda e, b=buf: dest_fn(e, b), sem="ve")
        mmctr[f"evac{k}"] = p.cnt["ve"]
        mmctr["n"] += 1

    p.wait("tensor", "vg", gfold)
    p.wait("tensor", "ve", None)
    for sl in range(4):
        chunks = [(wq1_sb[:, dj, :], nrmfm[:, dj, sl * 512:(sl + 1) * 512])
                  for dj in range(8)]
        mm_block(chunks, lambda e, b, sl=sl: e.activation(
            out=q_sb[:, sl * 512:(sl + 1) * 512], in_=b, func=AF.Copy,
            scale=0.25), evac_eng="scalar")
    for c2 in range(2):
        dst = k_sb if c2 == 0 else v_sb
        for sl in range(4):
            chunks = [(wkv1_sb[:, dj, c2 * P:(c2 + 1) * P],
                       nrmfm[:, dj, sl * 512:(sl + 1) * 512])
                      for dj in range(8)]
            mm_block(chunks, lambda e, b, dst=dst, sl=sl: e.tensor_copy(
                dst[:, sl * 512:(sl + 1) * 512], b))

    p.wait("tensor", "ve", None)
    p.wait("tensor", "se", None)
    trcp_hist = {}
    for t in range(NT):
        if t >= 1:
            p.wait("tensor", "ve", trcp_hist[t - 1])
        p.op("tensor", lambda e, t=t: e.transpose(
            ps_tr[:, 0:P], k_sb[:, t * P:(t + 1) * P], id_sb), sem=None)
        p.op("tensor", lambda e, t=t: e.transpose(
            ps_tr[:, P:2 * P], v_sb[:, t * P:(t + 1) * P], id_sb), sem="pe")
        p.wait("vector", "pe", None)
        p.op("vector", lambda e, t=t: e.tensor_copy(
            k_tm[:, t, :], ps_tr[:, 0:P]), sem="ve")
        p.op("vector", lambda e, t=t: e.tensor_copy(
            vaug[:, t, :, 0:16],
            ps_tr[:, P:2 * P].rearrange("p (h v) -> p h v", h=8)), sem="ve")
        trcp_hist[t] = p.cnt["ve"]
        p.wait("vector", "ve", None)
        p.op("vector", lambda e, t=t: e.tensor_copy(
            kexp[:, t, :, 1:17],
            k_tm[:, t, :].rearrange("p (h v) -> p h v", h=8)), sem="ve")
        for h in range(8):
            p.op("vector", lambda e, t=t, h=h: e.scalar_tensor_tensor(
                out=kexp[:, t, h, 17:273].rearrange("p (a b) -> p a b", a=16),
                in0=k_tm[:, t, h * 16:(h + 1) * 16][:, :, None]
                .broadcast_to([P, 16, 16]),
                scalar=0.7071067811865476,
                in1=k_tm[:, t, None, h * 16:(h + 1) * 16]
                .broadcast_to([P, 16, 16]),
                op0=ALU.mult, op1=ALU.mult),
                sem="ve" if h == 7 else None)
    kexp_done = p.cnt["ve"]

    csl = [(0, 17), (17, 145), (145, 273)]
    p.wait("tensor", "ve", kexp_done)
    grp = 0
    for h in range(8):
        for ci, (c0_, c1_) in enumerate(csl):
            buf = ps_mm[grp % 2]
            if grp >= 2:
                p.wait("tensor", "ve", mmctr[f"kv{grp - 2}"])
            for t in range(NT):
                p.op("tensor", lambda e, t=t, h=h, c0_=c0_, c1_=c1_, buf=buf:
                     e.matmul(buf[0:c1_ - c0_, 0:17],
                              kexp[:, t, h, c0_:c1_], vaug[:, t, h, :],
                              start=(t == 0), stop=(t == NT - 1),
                              skip_group_check=True),
                     sem="pe" if t == NT - 1 else None)
            p.wait("vector", "pe", None)
            p.op("vector", lambda e, h=h, ci=ci, buf=buf: e.tensor_copy(
                kvp_sb[:, h, ci, :], buf[:, 0:17]), sem="ve")
            mmctr[f"kv{grp}"] = p.cnt["ve"]
            grp += 1
    p.wait("gpsimd", "ve", None)
    p.dma("gpsimd", o_kvp.rearrange("h (c p) e -> p h c e", p=P), kvp_sb, "st")

    qdma_hist = {}
    for h in range(8):
        qb = qc0[h % 2]
        if h >= 2:
            p.wait("gpsimd", "st", qdma_hist[h - 2])
        p.wait("gpsimd", "ve", mmctr["evac3"])
        p.dma("gpsimd", qb[1:17, :], q_sb[16 * h:16 * h + 16, :], "lg")
        p.wait("gpsimd", "lg", None)
        p.dma("gpsimd", o_qx[h, 0], qb, "st")
        qdma_hist[h] = p.cnt["st"]
    for h in range(8):
        for tile in range(2):
            pa = pa0_sb if tile == 0 else pa1_sb
            qxb = qx[(h * 2 + tile) % 2]
            if h * 2 + tile >= 2:
                p.wait("vector", "st", qdma_hist[(h * 2 + tile - 2, "qx")])
            for sl in range(4):
                p.wait("tensor", "ve", None)
                p.op("tensor", lambda e, pa=pa, h=h, sl=sl: e.matmul(
                    ps_ab[0], pa[:, h, :], q_sb[:, sl * 512:(sl + 1) * 512],
                    start=True, stop=True, skip_group_check=True), sem=None)
                p.op("tensor", lambda e, h=h, sl=sl: e.matmul(
                    ps_ab[1], pb_sb[:, h, :], q_sb[:, sl * 512:(sl + 1) * 512],
                    start=True, stop=True, skip_group_check=True), sem="pe")
                p.wait("vector", "pe", None)
                p.op("vector", lambda e: e.tensor_copy(
                    ab_scr, ps_ab[0]), sem="ve")
                p.op("vector", lambda e, qxb=qxb, sl=sl: e.scalar_tensor_tensor(
                    out=qxb[:, sl * 512:(sl + 1) * 512], in0=ab_scr,
                    scalar=0.7071067811865476, in1=ps_ab[1],
                    op0=ALU.mult, op1=ALU.mult), sem="ve")
            p.wait("gpsimd", "ve", None)
            p.dma("gpsimd", o_qx[h, 1 + tile], qxb, "st")
            qdma_hist[(h * 2 + tile, "qx")] = p.cnt["st"]

    p.wait("gpsimd", "st", None)
    p.free_prefix("e:")


def emit_f(p, qx, kvp2, wo1, o_at, ident):
    qx_sb = p.sb("f:qx_sb", [P, 8, 3, HT], BF16)
    kva = p.sb("f:kva", [P, 8, 3, 17], F32)
    kvb = p.sb("f:kvb", [P, 8, 3, 17], F32)
    kv16 = p.sb("f:kv16", [P, 8, 3, 17], BF16)
    wo_sb = p.sb("f:wo_sb_f", [P, D], BF16)
    id_sb = p.sb("f:id_sb_f", [P, P], BF16)
    o1 = [p.sb(f"f:o1_{i}", [P, P], BF16) for i in range(2)]
    o1t = [p.sb(f"f:o1t_{i}", [P, P], BF16) for i in range(2)]
    dn = p.sb("f:dn", [P, 8], F32)
    atstg = [p.sb(f"f:atstg{i}", [P, D], BF16) for i in range(2)]
    ps_o = [p.psA[:, 0:136], p.psA[:, 512:648]]
    ps_tr = p.psT[:, 0:P]
    ps_w = p.psB

    p.dma("sync", qx_sb, qx.rearrange("h c p t -> p h c t"), "lx")
    p.dma("sync", id_sb, ident, "lx")
    p.dma("sync", kva, kvp2[0].rearrange("h (c p) e -> p h c e", p=P), "lx")
    p.dma("sync", kvb, kvp2[1].rearrange("h (c p) e -> p h c e", p=P), "lx")
    p.dma("gpsimd", wo_sb, wo1, "lg")
    lall = p.cnt["lx"]
    p.wait("vector", "lx", lall)
    p.op("vector", lambda e: e.tensor_add(kv16, kva, kvb), sem="vg")
    p.wait("tensor", "vg", None)
    p.wait("tensor", "lg", 16)

    csl = [(0, 17), (17, 145), (145, 273)]
    ev_hist = {}
    for t in range(NT):
        po = ps_o[t % 2]
        if t >= 2:
            p.wait("tensor", "ve", ev_hist[t - 2])
        first = True
        for h in range(8):
            for ci, (c0_, c1_) in enumerate(csl):
                p.op("tensor", lambda e, h=h, ci=ci, c0_=c0_, c1_=c1_, t=t,
                     po=po, st=first, sp=(h == 7 and ci == 2): e.matmul(
                         po[:, h * 17:(h + 1) * 17],
                         qx_sb[0:c1_ - c0_, h, ci, t * P:(t + 1) * P],
                         kv16[0:c1_ - c0_, h, ci, :],
                         start=st, stop=sp, skip_group_check=True),
                     sem="pe" if (h == 7 and ci == 2) else None)
                first = False
        p.wait("vector", "pe", None)
        p.op("vector", lambda e, po=po: e.tensor_scalar(
            out=dn[:, 0:8],
            in0=po.rearrange("p (h v) -> p h v", v=17)[:, :, 16:17]
            .rearrange("p h v -> p (h v)"),
            scalar1=1e-5, scalar2=None, op0=ALU.max), sem="vg")
        p.wait("vector", "vg", None)
        p.op("vector", lambda e: e.reciprocal(dn[:, 0:8], dn[:, 0:8]),
             sem="vg")
        p.wait("vector", "vg", None)
        p.op("vector", lambda e, po=po, t=t: e.tensor_tensor(
            o1[t % 2].rearrange("p (h v) -> p h v", h=8),
            po.rearrange("p (h v) -> p h v", v=17)[:, :, 0:16],
            dn[:, :, None].broadcast_to([P, 8, 16]), ALU.mult), sem="ve")
        ev_hist[t] = p.cnt["ve"]
        p.wait("tensor", "ve", None)
        p.op("tensor", lambda e, t=t: e.transpose(
            ps_tr, o1[t % 2], id_sb), sem="pe")
        p.wait("vector", "pe", None)
        p.op("vector", lambda e, t=t: e.tensor_copy(o1t[t % 2], ps_tr),
             sem="ve")
        p.wait("tensor", "ve", None)
        for half in range(2):
            p.op("tensor", lambda e, t=t, half=half: e.matmul(
                ps_w[:, half * 512:(half + 1) * 512], o1t[t % 2],
                wo_sb[:, half * 512:(half + 1) * 512], start=True, stop=True,
                skip_group_check=True), sem="pe" if half == 1 else None)
        p.wait("vector", "pe", None)
        if t >= 2:
            p.wait("vector", "st", None)
        p.op("vector", lambda e, t=t: e.tensor_copy(atstg[t % 2], ps_w),
             sem="ve")
        p.wait("gpsimd", "ve", None)
        p.dma("gpsimd", o_at[t], atstg[t % 2], "st")
    p.wait("gpsimd", "st", None)
    p.free_prefix("f:")


def emit_g(p, xn2, x1, at1, wg, wu, wd, o_out, ident):
    xn2_sb = p.sb("g:xn2_sb", [P, 8, HT], BF16)
    wdT = p.sb("g:wdT", [P, 24, D], BF16)
    hfm = p.sb("g:hfm", [P, 24, D], BF16)
    wgb = [p.sb(f"g:wgb{i}", [P, D], BF16) for i in range(2)]
    wub = [p.sb(f"g:wub{i}", [P, D], BF16) for i in range(2)]
    wdb = [p.sb(f"g:wdb{i}", [P, 3072], BF16) for i in range(2)]
    wgT = p.sb("g:wgT", [P, 8, P], BF16)
    wuT = p.sb("g:wuT", [P, 8, P], BF16)
    sg = p.sb("g:sg", [P, D], BF16)
    id_sb = p.sb("g:id_sb_g", [P, P], BF16)
    x1b = [p.sb(f"g:x1b{i}", [P, D], F32) for i in range(2)]
    atb = [p.sb(f"g:atb{i}", [P, D], BF16) for i in range(2)]
    ostg = [p.sb(f"g:ostg{i}", [P, D], F32) for i in range(2)]
    tmp = p.sb("g:tmp_g", [P, D], F32)
    ps_g = p.psA[:, 0:1024]
    ps_u = p.psA[:, 1024:2048]
    ps_tr = p.psT
    ps_d = p.psB

    p.dma("sync", id_sb, ident, "lx")
    p.dma("sync", xn2_sb, xn2.rearrange("a p t -> p a t"), "lx")
    xnld = p.cnt["lx"]

    tr_hist = {}
    for dc in range(8):
        bb = wdb[dc % 2]
        if dc >= 2:
            p.wait("gpsimd", "ve", tr_hist[dc - 2])
        p.dma("gpsimd", bb, wd[dc * P:(dc + 1) * P, :], "lg")
        ldv = p.cnt["lg"]
        p.wait("tensor", "lg", ldv)
        p.wait("tensor", "lx", 16)
        for r in range(3):
            if not (dc == 0 and r == 0):
                p.wait("tensor", "ve", None)
            for j in range(8):
                p.op("tensor", lambda e, bb=bb, r=r, j=j: e.transpose(
                    ps_tr[:, j * P:(j + 1) * P],
                    bb[:, (r * 8 + j) * P:(r * 8 + j + 1) * P], id_sb),
                    sem="pe" if j == 7 else None)
            p.wait("vector", "pe", None)
            p.op("vector", lambda e, dc=dc, r=r: e.tensor_copy(
                wdT[:, r * 8:(r + 1) * 8, dc * P:(dc + 1) * P],
                ps_tr.rearrange("p (a b) -> p a b", a=8)), sem="ve")
        tr_hist[dc] = p.cnt["ve"]

    dn_hist = {}
    for half in range(2):
        if half == 1:
            p.wait("vector", "pe", None)
        for hc in range(24):
            bg, bu = wgb[hc % 2], wub[hc % 2]
            if hc >= 2 or half > 0:
                p.wait("gpsimd", "se", None)
            p.dma("gpsimd", bg, wg[hc * P:(hc + 1) * P, :], "lg")
            p.dma("gpsimd", bu, wu[hc * P:(hc + 1) * P, :], "lg")
            ldv = p.cnt["lg"]
            p.wait("tensor", "lg", ldv)
            p.wait("tensor", "ve", None)
            if hc > 0 or half > 0:
                p.wait("tensor", "se", None)
            for j in range(8):
                p.op("tensor", lambda e, bg=bg, j=j: e.transpose(
                    ps_tr[:, j * P:(j + 1) * P], bg[:, j * P:(j + 1) * P],
                    id_sb), sem="pe" if j == 7 else None)
            p.wait("vector", "pe", None)
            p.op("vector", lambda e: e.tensor_copy(
                wgT, ps_tr.rearrange("p (a b) -> p a b", a=8)), sem="ve")
            p.wait("tensor", "ve", None)
            for j in range(8):
                p.op("tensor", lambda e, bu=bu, j=j: e.transpose(
                    ps_tr[:, j * P:(j + 1) * P], bu[:, j * P:(j + 1) * P],
                    id_sb), sem="pe" if j == 7 else None)
            p.wait("vector", "pe", None)
            p.op("vector", lambda e: e.tensor_copy(
                wuT, ps_tr.rearrange("p (a b) -> p a b", a=8)), sem="ve")
            p.wait("tensor", "ve", None)
            p.wait("tensor", "lx", xnld)
            for dj in range(8):
                for sl in range(2):
                    p.op("tensor", lambda e, dj=dj, sl=sl, half=half: e.matmul(
                        ps_g[:, sl * 512:(sl + 1) * 512], wgT[:, dj, :],
                        xn2_sb[:, dj, half * 1024 + sl * 512:
                               half * 1024 + (sl + 1) * 512],
                        start=(dj == 0), stop=(dj == 7),
                        skip_group_check=True), sem=None)
            for dj in range(8):
                for sl in range(2):
                    p.op("tensor", lambda e, dj=dj, sl=sl, half=half: e.matmul(
                        ps_u[:, sl * 512:(sl + 1) * 512], wuT[:, dj, :],
                        xn2_sb[:, dj, half * 1024 + sl * 512:
                               half * 1024 + (sl + 1) * 512],
                        start=(dj == 0), stop=(dj == 7),
                        skip_group_check=True),
                        sem="pe" if (dj == 7 and sl == 1) else None)
            p.wait("scalar", "pe", None)
            p.op("scalar", lambda e: e.activation(
                out=sg, in_=ps_g, func=AF.Silu, scale=1.0 / 127.0), sem="se")
            p.wait("vector", "se", None)
            p.op("vector", lambda e, hc=hc: e.scalar_tensor_tensor(
                out=hfm[:, hc, :], in0=ps_u, scalar=1.0 / 127.0, in1=sg,
                op0=ALU.mult, op1=ALU.mult), sem="ve")
        hdone = p.cnt["ve"]
        p.wait("tensor", "ve", hdone)
        for tt in range(8):
            t = half * 8 + tt
            if t >= 2:
                p.wait("sync", "ve", dn_hist[t - 2])
            p.dma("sync", x1b[t % 2], x1[t], "lx")
            p.dma("sync", atb[t % 2], at1[t], "lx")
            ldv = p.cnt["lx"]
            if t >= 1:
                p.wait("tensor", "ve", dn_hist[t - 1])
            for hc in range(24):
                for sl in range(2):
                    p.op("tensor", lambda e, hc=hc, sl=sl, tt=tt: e.matmul(
                        ps_d[:, sl * 512:(sl + 1) * 512],
                        hfm[:, hc, tt * P:(tt + 1) * P],
                        wdT[:, hc, sl * 512:(sl + 1) * 512],
                        start=(hc == 0), stop=(hc == 23),
                        skip_group_check=True),
                        sem="pe" if (hc == 23 and sl == 1) else None)
            p.wait("vector", "pe", None)
            p.wait("vector", "lx", ldv)
            if t >= 2:
                p.wait("vector", "st", None)
            p.op("vector", lambda e, t=t: e.scalar_tensor_tensor(
                out=tmp, in0=ps_d, scalar=1.0 / 127.0, in1=x1b[t % 2],
                op0=ALU.mult, op1=ALU.add), sem="ve")
            p.wait("vector", "ve", None)
            p.op("vector", lambda e, t=t: e.tensor_add(
                ostg[t % 2], tmp, atb[t % 2]), sem="ve")
            dn_hist[t] = p.cnt["ve"]
            p.wait("gpsimd", "ve", None)
            p.dma("gpsimd", o_out[t], ostg[t % 2], "st")
    p.wait("gpsimd", "st", None)
    p.free_prefix("g:")


def build_l1():
    nc = new_nc()
    xf = nc.dram_tensor("xf", [S, D], F32, kind="ExternalInput").ap()
    wq = nc.dram_tensor("wq", [D, 512], F32, kind="ExternalInput").ap()
    wkvd = nc.dram_tensor("wkvd", [D, 256], F32, kind="ExternalInput").ap()
    wku = nc.dram_tensor("wku", [256, 512], F32, kind="ExternalInput").ap()
    wvu = nc.dram_tensor("wvu", [256, 512], F32, kind="ExternalInput").ap()
    o_xnfm = nc.dram_tensor("xnfm", [8, P, S], BF16, kind="ExternalOutput").ap()
    kt = nc.dram_tensor("kt", [4, P, S], BF16, kind="Internal").ap()
    qt = nc.dram_tensor("qt", [4, P, S], BF16, kind="Internal").ap()
    va = nc.dram_tensor("vaug", [32, P, 520], BF16, kind="Internal").ap()
    o_y = nc.dram_tensor("yfm", [4, P, S], BF16, kind="ExternalOutput").ap()
    ident = nc.inline_tensor(_ident_np(), name="ident").ap()
    maskc = nc.inline_tensor(_mask_np(), name="maskc").ap()
    p = Prog(nc)
    emit_a(p, xf, wq, wkvd, wku, wvu, o_xnfm, kt, qt, va, ident)
    p.barrier()
    emit_b(p, kt, qt, va, o_y, maskc)
    return p.build()


def build_l2():
    nc = new_nc()
    y_all = nc.dram_tensor("y_all", [8, P, HT], BF16, kind="ExternalInput").ap()
    wout = nc.dram_tensor("wout", [D, D], F32, kind="ExternalInput").ap()
    x_own = nc.dram_tensor("x_own", [HT, D], F32, kind="ExternalInput").ap()
    xr = nc.dram_tensor("xr", [NT, P, D], F32, kind="Internal").ap()
    xnfm = nc.dram_tensor("xnfm_own", [8, P, HT], BF16, kind="ExternalInput").ap()
    wfc = nc.dram_tensor("wfc", [4096, D], F32, kind="ExternalInput").ap()
    lnw = nc.dram_tensor("lnw", [1, 4096], F32, kind="ExternalInput").ap()
    lnb = nc.dram_tensor("lnb", [1, 4096], F32, kind="ExternalInput").ap()
    wproj = nc.dram_tensor("wproj", [D, 4096], F32, kind="ExternalInput").ap()
    o_x1 = nc.dram_tensor("x1", [NT, P, D], F32, kind="ExternalOutput").ap()
    gamt = nc.dram_tensor("gamt", [P, 8], F32, kind="ExternalInput").ap()
    wq1 = nc.dram_tensor("wq1", [D, P], F32, kind="ExternalInput").ap()
    wkv1 = nc.dram_tensor("wkv1", [D, 256], F32, kind="ExternalInput").ap()
    o_xn2 = nc.dram_tensor("xn2fm", [8, P, HT], BF16, kind="ExternalOutput").ap()
    o_qx = nc.dram_tensor("qexp", [8, 3, P, HT], BF16, kind="ExternalOutput").ap()
    o_kvp = nc.dram_tensor("kvp", [8, 384, 17], F32, kind="ExternalOutput").ap()
    ident = nc.inline_tensor(_ident_np(), name="ident").ap()
    id32 = nc.inline_tensor(np.eye(P, dtype=np.float32), name="id32").ap()
    pa0, pa1, pb = _pa_np()
    c_pa0 = nc.inline_tensor(pa0, name="pa0").ap()
    c_pa1 = nc.inline_tensor(pa1, name="pa1").ap()
    c_pb = nc.inline_tensor(pb, name="pb").ap()
    p = Prog(nc)
    emit_c(p, y_all, wout, x_own, xr)
    p.barrier()
    emit_d(p, xr, xnfm, wfc, lnw, lnb, wproj, o_x1, ident, id32)
    p.barrier()
    emit_e(p, o_x1, gamt, wq1, wkv1, o_xn2, o_qx, o_kvp, ident, c_pa0, c_pa1,
           c_pb)
    return p.build()


def build_l3():
    nc = new_nc()
    qx = nc.dram_tensor("qexp", [8, 3, P, HT], BF16, kind="ExternalInput").ap()
    kvp2 = nc.dram_tensor("kvp2", [2, 8, 384, 17], F32, kind="ExternalInput").ap()
    wo1 = nc.dram_tensor("wout1", [P, D], F32, kind="ExternalInput").ap()
    at1 = nc.dram_tensor("at1", [NT, P, D], BF16, kind="Internal").ap()
    xn2 = nc.dram_tensor("xn2fm", [8, P, HT], BF16, kind="ExternalInput").ap()
    x1 = nc.dram_tensor("x1", [NT, P, D], F32, kind="ExternalInput").ap()
    wg = nc.dram_tensor("wg", [3072, D], I32, kind="ExternalInput").ap()
    wu = nc.dram_tensor("wu", [3072, D], I32, kind="ExternalInput").ap()
    wd = nc.dram_tensor("wd", [D, 3072], I32, kind="ExternalInput").ap()
    o_out = nc.dram_tensor("out", [NT, P, D], F32, kind="ExternalOutput").ap()
    ident = nc.inline_tensor(_ident_np(), name="ident").ap()
    p = Prog(nc)
    emit_f(p, qx, kvp2, wo1, at1, ident)
    p.barrier()
    emit_g(p, xn2, x1, at1, wg, wu, wd, o_out, ident)
    return p.build()


def _bf16(x):
    return np.ascontiguousarray(x, dtype=BF16NP)


def _f32(x):
    return np.ascontiguousarray(x, dtype=np.float32)


def kernel(**inputs):
    x = _f32(inputs["L0_attn_x_bld"])
    wq_full = _f32(inputs["L0_attn_Wq"]).reshape(D, 16 * 64)
    wkvd = _f32(inputs["L0_attn_Wkv_down"])
    wku_full = _f32(inputs["L0_attn_Wk_up"]).reshape(256, 16 * 64)
    wvu_full = _f32(inputs["L0_attn_Wv_up"]).reshape(256, 16 * 64)

    nc1 = build_l1()
    in1 = []
    for core in range(8):
        b, g = core // 2, core % 2
        hsl = slice(g * 512, (g + 1) * 512)
        in1.append({
            "xf": np.ascontiguousarray(x[b]),
            "wq": np.ascontiguousarray(wq_full[:, hsl]),
            "wkvd": wkvd,
            "wku": np.ascontiguousarray(wku_full[:, hsl]),
            "wvu": np.ascontiguousarray(wvu_full[:, hsl]),
        })
    r1 = run_bass_kernel_spmd(nc1, in1, core_ids=list(range(8))).results

    wout = _f32(inputs["L0_attn_Wout"]).reshape(D, D)
    wfc = _f32(inputs["L0_ffn_W_fc"])
    lnw = _f32(inputs["L0_ffn_ln_w"]).reshape(1, 4096)
    lnb = _f32(inputs["L0_ffn_ln_b"]).reshape(1, 4096)
    wproj = _f32(inputs["L0_ffn_W_proj"])
    gamt = _f32(np.asarray(inputs["L1_attn_gamma"], np.float32)
                .reshape(8, P).T)
    wq1 = _f32(inputs["L1_attn_w_q"])
    wkv1 = _f32(inputs["L1_attn_w_kv"])
    nc2 = build_l2()
    in2 = []
    for core in range(8):
        b, g = core // 2, core % 2
        ysl = slice(g * HT, (g + 1) * HT)
        ya = np.concatenate([np.asarray(r1[2 * b]["yfm"])[:, :, ysl],
                             np.asarray(r1[2 * b + 1]["yfm"])[:, :, ysl]], 0)
        in2.append({
            "y_all": np.ascontiguousarray(ya),
            "wout": wout,
            "x_own": np.ascontiguousarray(x[b, g * HT:(g + 1) * HT, :]),
            "xnfm_own": np.ascontiguousarray(
                np.asarray(r1[core]["xnfm"])[:, :, ysl]),
            "wfc": wfc, "lnw": lnw, "lnb": lnb, "wproj": wproj,
            "gamt": gamt, "wq1": wq1, "wkv1": wkv1,
        })
    r2 = run_bass_kernel_spmd(nc2, in2, core_ids=list(range(8))).results

    wo1 = _f32(inputs["L1_attn_w_out"])
    wg = np.ascontiguousarray(np.asarray(inputs["L1_ffn_w_gate"], np.int32))
    wu = np.ascontiguousarray(np.asarray(inputs["L1_ffn_w_up"], np.int32))
    wd = np.ascontiguousarray(np.asarray(inputs["L1_ffn_w_down"], np.int32))
    nc3 = build_l3()
    in3 = []
    for core in range(8):
        kvp2 = np.stack([np.asarray(r2[core]["kvp"]),
                         np.asarray(r2[core ^ 1]["kvp"])])
        in3.append({"qexp": np.asarray(r2[core]["qexp"]), "kvp2": kvp2,
                    "wout1": wo1, "xn2fm": np.asarray(r2[core]["xn2fm"]),
                    "x1": np.asarray(r2[core]["x1"]),
                    "wg": wg, "wu": wu, "wd": wd})
    r3 = run_bass_kernel_spmd(nc3, in3, core_ids=list(range(8))).results

    out = np.empty((B, S, D), np.float32)
    for core in range(8):
        b, g = core // 2, core % 2
        out[b, g * HT:(g + 1) * HT, :] = np.asarray(
            r3[core]["out"], np.float32).reshape(HT, D)
    return out


# version 03
# speedup vs baseline: 1.1501x; 1.1405x over previous
import math
from contextlib import ExitStack

import numpy as np
import ml_dtypes

import concourse.bass as bass
import concourse.mybir as mybir
from concourse.bass_utils import run_bass_kernel_spmd

F32 = mybir.dt.float32
BF16 = mybir.dt.bfloat16
I32 = mybir.dt.int32
AF = mybir.ActivationFunctionType
ALU = mybir.AluOpType
AX = mybir.AxisListType

P = 128
D = 1024
S = 4096
B = 4
HT = 2048
NT = HT // P
NTS = S // P
EXPB = 25.0

BF16NP = ml_dtypes.bfloat16


ARENA_F32 = 53136
SEMS = ("lx", "lg", "ld", "pe", "ve", "se", "vg", "ge", "st")


class Prog:

    ENGINES = ("sync", "tensor", "vector", "scalar", "gpsimd")

    def __init__(self, nc):
        self.nc = nc
        self.ctx = ExitStack()
        self.q = {e: [] for e in self.ENGINES}
        self.cnt = {}
        self._sems = {}
        self._ar = self.ctx.enter_context(
            nc.sbuf_tensor("arena", [P, ARENA_F32], F32)).ap()
        self._ar16 = self._ar.bitcast(BF16)
        self._frees = [(0, ARENA_F32 * 4)]
        self._amap = {}
        self.psA = self.ctx.enter_context(
            nc.psum_tensor("psA", [P, 2048], F32)).ap()
        self.psB = self.ctx.enter_context(
            nc.psum_tensor("psB", [P, 1024], F32)).ap()
        self.psC = self.ctx.enter_context(
            nc.psum_tensor("psC", [P, 512], F32)).ap()
        self.psT = self.ctx.enter_context(
            nc.psum_tensor("psT", [P, 1024], BF16)).ap()
        for s_ in SEMS:
            self.sem(s_)

    def carve(self, name, nbytes):
        nb = (nbytes + 63) // 64 * 64
        for i, (s, e) in enumerate(self._frees):
            if e - s >= nb:
                if s + nb == e:
                    self._frees.pop(i)
                else:
                    self._frees[i] = (s + nb, e)
                self._amap[name] = (s, nb)
                return s
        raise MemoryError(f"arena full: {name} needs {nb}B; free={self._frees}")

    def vf32(self, off, shape):
        assert off % 4 == 0
        n = int(np.prod(shape[1:]))
        v = self._ar[0:shape[0], off // 4:off // 4 + n]
        return self._shape(v, shape)

    def vbf16(self, off, shape):
        assert off % 2 == 0
        n = int(np.prod(shape[1:]))
        v = self._ar16[0:shape[0], off // 2:off // 2 + n]
        return self._shape(v, shape)

    @staticmethod
    def _shape(v, shape):
        if len(shape) <= 2:
            return v
        dims = {f"d{i}": shape[1 + i] for i in range(len(shape) - 1)}
        names = " ".join(f"d{i}" for i in range(len(shape) - 1))
        return v.rearrange(f"p ({names}) -> p {names}", **dims)

    def sb(self, name, shape, dtype):
        es = 2 if dtype == BF16 else 4
        off = self.carve(name, int(np.prod(shape[1:])) * es)
        return self.vbf16(off, shape) if dtype == BF16 else self.vf32(off, shape)

    def free(self, *names):
        for n in names:
            s, nb = self._amap.pop(n)
            self._frees.append((s, s + nb))
        self._frees.sort()
        merged = []
        for s, e in self._frees:
            if merged and merged[-1][1] == s:
                merged[-1] = (merged[-1][0], e)
            else:
                merged.append((s, e))
        self._frees = merged

    def free_prefix(self, pfx):
        self.free(*[n for n in list(self._amap) if n.startswith(pfx)])

    def barrier(self):
        for e in self.ENGINES:
            for s_, h in self._sems.items():
                v = self.cnt[s_]
                if v:
                    self.on(e, lambda eng, h=h, v=v: eng.wait_ge(h, v))

    def sem(self, name):
        h = self.ctx.enter_context(self.nc.semaphore(name))
        self.cnt[name] = 0
        self._sems[name] = h
        return h

    def _h(self, sem):
        return self._sems[sem]

    def on(self, eng, fn):
        self.q[eng].append(fn)

    def inc(self, sem, n=1):
        self.cnt[sem] += n
        return self.cnt[sem]

    def wait(self, eng, sem, val=None):
        v = self.cnt[sem] if val is None else val
        h = self._h(sem)
        self.on(eng, lambda e, h=h, v=v: e.wait_ge(h, v))

    def op(self, eng, fn, sem=None, n=1):
        if sem is None:
            self.on(eng, fn)
            return None
        h = self._h(sem)
        v = self.inc(sem, n)
        self.on(eng, lambda e, fn=fn, h=h, n=n: fn(e).then_inc(h, n))
        return v

    def dma(self, eng, out, in_, sem, **kw):
        return self.op(eng, lambda e, o=out, i=in_: e.dma_start(out=o, in_=i, **kw),
                       sem=sem, n=16)

    def build(self, guard_store_sem=None):
        with self.ctx:
            with self.nc.Block() as block:
                for eng in self.ENGINES:
                    cmds = self.q[eng]
                    if not cmds:
                        continue
                    deco = getattr(block, eng)

                    def body(e, cmds=cmds):
                        for fn in cmds:
                            fn(e)

                    deco(body)
        return self.nc


def new_nc():
    return bass.Bass("TRN2", target_bir_lowering=False, debug=False)


def _ident_np():
    return np.eye(P, dtype=BF16NP)


def emit_a(p, xf, wq, wkvd, wku, wvu, o_xnfm, o_kt, o_qt, o_va, ident):
    xb = [p.sb(f"a:xb{i}", [P, D], F32) for i in range(2)]
    xn16 = [p.sb(f"a:xn16{i}", [P, D], BF16) for i in range(2)]
    xnfm = p.sb("a:xnfm_sb", [P, 8, S], BF16)
    lat = p.sb("a:lat_sb", [P, 2, S], BF16)
    qt_sb = p.sb("a:qt_sb", [P, 4, S], BF16)
    kstg = [p.sb(f"a:kstg{i}", [P, S], BF16) for i in range(2)]
    vstg = [p.sb(f"a:vstg{i}", [P, 520], BF16) for i in range(2)]
    wq_sb = p.sb("a:wq_sb", [P, 8, 512], BF16)
    wkvd_sb = p.sb("a:wkvd_sb", [P, 8, 256], BF16)
    wku_sb = p.sb("a:wku_sb", [P, 2, 512], BF16)
    wvu_sb = p.sb("a:wvu_sb", [P, 2, 512], BF16)
    id_sb = p.sb("a:id_sb", [P, P], BF16)
    stt = p.sb("a:stt_sb", [P, 8], F32)
    sqscr = p.sb("a:sqscr", [P, D], F32)
    epsc = p.sb("a:epsc", [P, 1], F32)
    ps_tr = p.psT
    ps_mm = [p.psA[:, i * 512:(i + 1) * 512] for i in range(2)]

    p.dma("gpsimd", wq_sb, wq.rearrange("(a p) k -> p a k", p=P), "lg")
    p.dma("gpsimd", wkvd_sb, wkvd.rearrange("(a p) k -> p a k", p=P), "lg")
    p.dma("gpsimd", wku_sb, wku.rearrange("(a p) k -> p a k", p=P), "lg")
    p.dma("gpsimd", wvu_sb, wvu.rearrange("(a p) k -> p a k", p=P), "lg")
    p.dma("sync", id_sb, ident, "lx")
    p.op("gpsimd", lambda e: e.memset(epsc, 1e-5), sem="ve")
    p.op("gpsimd", lambda e: e.memset(vstg[0][:, :], 1.0), sem="ve")
    p.op("gpsimd", lambda e: e.memset(vstg[1][:, :], 1.0), sem="ve")
    wload = p.cnt["lg"]

    c = lambda i: stt[:, i:i + 1]
    for i in range(NTS):
        buf = xb[i % 2]
        xn = xn16[i % 2]
        if i >= 2:
            p.wait("sync", "se", None)
        p.dma("sync", buf, xf[i * P:(i + 1) * P, :], "lx")
        ldv = p.cnt["lx"]
        p.wait("vector", "lx", ldv)
        if i > 0:
            p.wait("vector", "se", None)
        p.op("vector", lambda e, b=buf, o=c(0): e.reduce_sum(o, b, axis=AX.X),
             sem="vg")
        p.wait("scalar", "lx", ldv)
        p.wait("scalar", "vg", None)
        p.op("scalar", lambda e, b=buf, o=c(1): e.activation(
            out=sqscr, in_=b, func=AF.Square, accum_out=o), sem="se")
        p.wait("vector", "se", None)
        p.op("vector", lambda e, o=c(2), a=c(0): e.tensor_scalar(
            out=o, in0=a, scalar1=1.0 / D, scalar2=None, op0=ALU.mult), sem="vg")
        p.wait("vector", "vg", None)
        p.op("vector", lambda e, o=c(3), a=c(1): e.tensor_scalar(
            out=o, in0=a, scalar1=1.0 / D, scalar2=None, op0=ALU.mult), sem="vg")
        p.wait("vector", "vg", None)
        p.op("vector", lambda e, o=c(4), a=c(2): e.tensor_tensor(
            o, a, a, ALU.mult), sem="vg")
        p.wait("vector", "vg", None)
        p.op("vector", lambda e, o=c(5), a=c(3), b2=c(4): e.tensor_tensor(
            o, a, b2, ALU.subtract), sem="vg")
        p.wait("scalar", "vg", None)
        p.op("scalar", lambda e, o=c(6), a=c(5): e.activation(
            out=o, in_=a, func=AF.Sqrt, bias=epsc[:, 0:1]), sem="se")
        p.wait("vector", "se", None)
        p.op("vector", lambda e, o=c(6), a=c(6): e.reciprocal(o, a), sem="vg")
        p.wait("vector", "vg", None)
        p.op("vector", lambda e, o=c(7), a=c(2), b2=c(6): e.scalar_tensor_tensor(
            out=o, in0=a, scalar=-1.0, in1=b2, op0=ALU.mult, op1=ALU.mult),
            sem="vg")
        p.wait("scalar", "vg", None)
        if i >= 2:
            p.wait("scalar", "pe", p.cnt["pe"] - 1)
        p.op("scalar", lambda e, b=buf, o=xn: e.activation(
            out=o, in_=b, func=AF.Identity, scale=c(6), bias=c(7)), sem="se")
        p.wait("tensor", "se", None)
        if i > 0:
            p.wait("tensor", "ve", None)
        for j in range(8):
            p.op("tensor", lambda e, xnj=xn[:, j * P:(j + 1) * P],
                 d=ps_tr[:, j * P:(j + 1) * P]: e.transpose(d, xnj, id_sb),
                 sem="pe" if j == 7 else None)
        p.wait("vector", "pe", None)
        p.op("vector", lambda e, i=i: e.tensor_copy(
            xnfm[:, :, i * P:(i + 1) * P],
            ps_tr.rearrange("p (a b) -> p a b", a=8)), sem="ve")
    p.wait("gpsimd", "ve", None)
    for a in range(8):
        p.dma("gpsimd", o_xnfm[a], xnfm[:, a, :], "st")

    mmctr = {"n": 0}

    def mm_block(chunks, dest_fn, evac_eng="vector"):
        k = mmctr["n"]
        buf = ps_mm[k % 2]
        if k >= 2:
            p.wait("tensor", "ve", mmctr[f"evac{k - 2}"])
        for ci, (lh, rh) in enumerate(chunks):
            p.op("tensor", lambda e, lh=lh, rh=rh, st=(ci == 0),
                 sp=(ci == len(chunks) - 1), b=buf: e.matmul(
                     b, lh, rh, start=st, stop=sp),
                 sem="pe" if ci == len(chunks) - 1 else None)
        p.wait(evac_eng, "pe", None)
        p.op(evac_eng, lambda e, b=buf: dest_fn(e, b), sem="ve")
        mmctr[f"evac{k}"] = p.cnt["ve"]
        mmctr["n"] += 1

    p.wait("tensor", "lg", wload)
    p.wait("tensor", "ve", None)
    for c2 in range(2):
        for sl in range(8):
            chunks = [(wkvd_sb[:, dk, c2 * P:(c2 + 1) * P],
                       xnfm[:, dk, sl * 512:(sl + 1) * 512]) for dk in range(8)]
            mm_block(chunks, lambda e, b, c2=c2, sl=sl: e.tensor_copy(
                lat[:, c2, sl * 512:(sl + 1) * 512], b))

    for qc in range(4):
        for sl in range(8):
            chunks = [(wq_sb[:, dk, qc * P:(qc + 1) * P],
                       xnfm[:, dk, sl * 512:(sl + 1) * 512]) for dk in range(8)]
            mm_block(chunks, lambda e, b, qc=qc, sl=sl: e.tensor_copy(
                qt_sb[:, qc, sl * 512:(sl + 1) * 512], b))
    p.wait("gpsimd", "ve", None)
    for qc in range(4):
        p.dma("gpsimd", o_qt[qc], qt_sb[:, qc, :], "st")

    for kc in range(4):
        stg = kstg[kc % 2]
        if kc >= 2:
            p.wait("tensor", "st", None)
        for sl in range(8):
            chunks = [(wku_sb[:, c2, kc * P:(kc + 1) * P],
                       lat[:, c2, sl * 512:(sl + 1) * 512]) for c2 in range(2)]
            mm_block(chunks, lambda e, b, sl=sl, stg=stg: e.tensor_copy(
                stg[:, sl * 512:(sl + 1) * 512], b))
        p.wait("gpsimd", "ve", None)
        p.dma("gpsimd", o_kt[kc], stg, "st")

    for kb in range(32):
        stg = vstg[kb % 2]
        if kb >= 2:
            p.wait("vector", "st", None)
        chunks = [(lat[:, c2, kb * P:(kb + 1) * P], wvu_sb[:, c2, :])
                  for c2 in range(2)]
        mm_block(chunks, lambda e, b, stg=stg: e.tensor_copy(
            stg.rearrange("p (h v) -> p h v", h=8)[:, :, 0:64],
            b.rearrange("p (h v) -> p h v", h=8)))
        p.wait("gpsimd", "ve", None)
        p.dma("gpsimd", o_va[kb], stg, "st")

    p.wait("gpsimd", "st", None)
    p.free_prefix("a:")


def _mask_np():
    u = np.arange(896)[None, :] - 384
    r = np.arange(P)[:, None]
    return (u >= r).astype(BF16NP)


def emit_b(p, kt, qt, va, o_y, maskc):
    kt_sb = p.sb("b:kt_sb", [P, 4, S], BF16)
    qt_sb = p.sb("b:qt_sb", [P, 4, S], BF16)
    va_sb = p.sb("b:va_sb", [P, 32, 8, 65], BF16)
    mk_sb = p.sb("b:mk_sb", [P, 896], BF16)
    ptb = [p.sb(f"b:ptb{i}", [P, 1024], BF16) for i in range(2)]
    ystg = [p.sb(f"b:ystg{i}", [P, S], BF16) for i in range(2)]
    rcp16 = p.sb("b:rcp16", [1, 512], BF16)
    den_sb = p.sb("b:den_sb", [64, 512], F32)
    ones1 = p.sb("b:ones1", [1, 64], BF16)
    expb = p.sb("b:expb", [P, 1], F32)
    ps_sg = [p.psA[:, i * 1024:(i + 1) * 1024] for i in range(2)]
    ps_y = [p.psB[:, i * 512:(i + 1) * 512] for i in range(2)]
    ps_b = p.psC[0:64, 0:512]

    p.dma("sync", kt_sb, kt.rearrange("a p t -> p a t"), "ld")
    p.dma("sync", qt_sb, qt.rearrange("a p t -> p a t"), "ld")
    p.dma("sync", va_sb.rearrange("p b h v -> p b (h v)"),
          va.rearrange("b p x -> p b x"), "ld")
    p.dma("sync", mk_sb, maskc, "ld")
    p.op("gpsimd", lambda e: e.memset(ones1, 1.0), sem="ve")
    p.op("gpsimd", lambda e: e.memset(expb, -EXPB), sem="ve")
    ldall = p.cnt["ld"]
    p.wait("tensor", "ld", ldall)
    p.wait("tensor", "ve", None)
    p.wait("scalar", "ld", ldall)
    p.wait("scalar", "ve", None)
    p.wait("vector", "ld", ldall)

    ynorm_hist = {}
    dma_hist = {}
    exp_hist = {}
    avc_hist = {}

    def emit_st(cc, rb, q0, g):
        if exp_hist.get(g % 2) is not None:
            p.wait("tensor", "se", exp_hist[g % 2])
        for j in range(2):
            kb = 2 * g + j
            p.op("tensor", lambda e, cc=cc, rb=rb, kb=kb, q0=q0, g=g, j=j:
                 e.matmul(ps_sg[g % 2][:, j * 512:(j + 1) * 512],
                          kt_sb[rb:rb + 64, cc, kb * P:(kb + 1) * P],
                          qt_sb[rb:rb + 64, cc, q0:q0 + 512],
                          start=True, stop=True, skip_group_check=True),
                 sem="pe" if j == 1 else None)
        return p.cnt["pe"]

    def emit_evac(rec):
        h, sl, cc, rb, q0 = rec["h"], rec["sl"], rec["cc"], rec["rb"], rec["q0"]
        yb = ps_y[sl % 2]
        p.wait("vector", "pe", rec["av"])

        def _recip_lp(e, yb=yb):
            with e.bass.allow_low_precision(
                    reason="softmax denom reciprocal; common scale cancels"):
                return e.reciprocal(rcp16, yb[64:65, :])

        p.op("vector", _recip_lp, sem="ve")
        rc = p.cnt["ve"]
        p.wait("tensor", "ve", rc)
        p.op("tensor", lambda e: e.matmul(
            ps_b, ones1, rcp16, start=True, stop=True,
            skip_group_check=True), sem="pe")
        bc = p.cnt["pe"]
        p.wait("vector", "pe", bc)
        p.op("vector", lambda e: e.tensor_copy(den_sb, ps_b), sem="ve")
        if cc >= 2 and (cc - 2) in dma_hist:
            p.wait("vector", "st", dma_hist[cc - 2])
        p.wait("vector", "ve", p.cnt["ve"])
        p.op("vector", lambda e, yb=yb, rb=rb, q0=q0, cc=cc:
             e.tensor_tensor(ystg[cc % 2][rb:rb + 64, q0:q0 + 512],
                             yb[0:64, :], den_sb, ALU.mult), sem="ve")
        ynorm_hist[rec["idx"]] = p.cnt["ve"]
        if h % 2 == 1 and sl == 7:
            p.wait("gpsimd", "ve", None)
            p.dma("gpsimd", o_y[cc], ystg[cc % 2], "st")
            dma_hist[cc] = p.cnt["st"]

    prev_rec = None
    for idx, (h, sl) in enumerate([(h, sl) for h in range(8)
                                   for sl in range(8)]):
        cc, rb, q0 = h // 2, (h % 2) * 64, sl * 512
        ngr = 2 * (sl + 1)
        st_pe = {0: emit_st(cc, rb, q0, 0)}
        if prev_rec is not None:
            emit_evac(prev_rec)
        for g in range(ngr):
            w = st_pe[g]
            if avc_hist.get(g % 2) is not None:
                w = max(w, avc_hist[g % 2])
            p.wait("scalar", "pe", w)
            p.op("scalar", lambda e, g=g: e.activation(
                out=ptb[g % 2], in_=ps_sg[g % 2], func=AF.Exp,
                scale=0.125, bias=expb[:, 0:1]), sem="se")
            exp_hist[g % 2] = p.cnt["se"]
            nmask = 0
            for j in range(2):
                kb = 2 * g + j
                delta = kb * P - q0
                if 0 <= delta <= 384:
                    off = 384 - delta
                    p.wait("vector", "se", exp_hist[g % 2])
                    p.op("vector", lambda e, g=g, j=j, off=off:
                         e.tensor_tensor(
                             ptb[g % 2][:, j * 512:(j + 1) * 512],
                             ptb[g % 2][:, j * 512:(j + 1) * 512],
                             mk_sb[:, off:off + 512], ALU.mult), sem="ve")
                    nmask += 1
            if g + 1 < ngr:
                st_pe[g + 1] = emit_st(cc, rb, q0, g + 1)
            if nmask:
                p.wait("tensor", "ve", p.cnt["ve"])
            else:
                p.wait("tensor", "se", exp_hist[g % 2])
            if g == 0 and idx >= 2:
                p.wait("tensor", "ve", ynorm_hist[idx - 2])
            for j in range(2):
                kb = 2 * g + j
                p.op("tensor", lambda e, kb=kb, h=h, g=g, j=j, sl=sl,
                     st=(kb == 0), sp=(kb == ngr * 2 - 1): e.matmul(
                         ps_y[sl % 2][0:65, :], va_sb[:, kb, h, :],
                         ptb[g % 2][:, j * 512:(j + 1) * 512],
                         start=st, stop=sp, skip_group_check=True),
                     sem="pe" if j == 1 else None)
            avc_hist[g % 2] = p.cnt["pe"]
        prev_rec = dict(h=h, sl=sl, cc=cc, rb=rb, q0=q0, av=p.cnt["pe"],
                        idx=idx)
    emit_evac(prev_rec)

    p.wait("gpsimd", "st", None)
    p.free_prefix("b:")


def emit_c(p, y_all, wout, x_own, o_xr):
    y_sb = p.sb("c:y_sb", [P, 8, HT], BF16)
    wo_sb = p.sb("c:wo_sb", [P, 8, D], BF16)
    xb = [p.sb(f"c:xb{i}", [P, D], F32) for i in range(2)]
    stg = [p.sb(f"c:stg{i}", [P, D], F32) for i in range(2)]
    ps = [p.psA[:, i * 1024:(i + 1) * 1024] for i in range(2)]

    p.dma("sync", y_sb, y_all.rearrange("a p t -> p a t"), "lx")
    p.dma("gpsimd", wo_sb, wout.rearrange("(a p) k -> p a k", p=P), "lg")
    evac_hist = {}
    for t in range(NT):
        xbuf = xb[t % 2]
        if t >= 2:
            p.wait("sync", "ve", evac_hist[t - 2])
        p.dma("sync", xbuf, x_own[t * P:(t + 1) * P, :], "lx")
        ldv = p.cnt["lx"]
        pb = ps[t % 2]
        p.wait("tensor", "lx", 16)
        p.wait("tensor", "lg", 16)
        if t >= 2:
            p.wait("tensor", "ve", evac_hist[t - 2])
        for ic in range(8):
            for half in range(2):
                p.op("tensor", lambda e, ic=ic, half=half, t=t, pb=pb: e.matmul(
                    pb[:, half * 512:(half + 1) * 512],
                    y_sb[:, ic, t * P:(t + 1) * P],
                    wo_sb[:, ic, half * 512:(half + 1) * 512],
                    start=(ic == 0), stop=(ic == 7), skip_group_check=True),
                    sem="pe" if (ic == 7 and half == 1) else None)
        p.wait("vector", "lx", ldv)
        p.wait("vector", "pe", None)
        if t >= 2:
            p.wait("vector", "st", None)
        p.op("vector", lambda e, t=t, pb=pb, xbuf=xbuf: e.tensor_add(
            stg[t % 2], pb, xbuf), sem="ve")
        evac_hist[t] = p.cnt["ve"]
        p.wait("gpsimd", "ve", None)
        p.dma("gpsimd", o_xr[t], stg[t % 2], "st")
    p.wait("gpsimd", "st", None)
    p.free_prefix("c:")


def emit_d(p, xr, xnfm, wfc, lnw, lnb, wproj, o_x1, ident, id32):
    wfcT = p.sb("d:wfcT", [P, 8, 4096], BF16)
    wprT = p.sb("d:wprT", [P, 32, D], BF16)
    dof = p.carve("d:arena", 53248)
    arena = p.vf32(dof, [P, 13312])
    w3s = [arena[:, 0:4096], arena[:, 4096:8192]]
    sgna = p.vbf16(dof + 32768, [P, 4096])
    sgnb = p.vbf16(dof + 40960, [P, 4096])
    xnf = [p.vbf16(dof, [P, 8, P]),
           p.vbf16(dof + 2048, [P, 8, P])]
    h_sb = p.vbf16(dof + 4096, [P, 4096])
    hT = p.vbf16(dof + 12288, [P, 32, P])
    lnwbc = p.vbf16(dof + 20480, [P, 4096])
    lnbbc = p.vbf16(dof + 28672, [P, 4096])
    xrb = [arena[:, 9216:10240], arena[:, 10240:11264]]
    x1s = [arena[:, 11264:12288], arena[:, 12288:13312]]
    id_sb = p.sb("d:id_sb", [P, P], BF16)
    id32_sb = p.sb("d:id32_sb", [P, P], F32)
    onesf = p.sb("d:onesf", [1, P], F32)
    ones16 = p.sb("d:ones16", [1, P], BF16)
    lnrow = p.sb("d:lnrow", [1, 8192], BF16)
    stt = p.sb("d:stt_d", [P, 12], F32)
    g11 = p.sb("d:g11", [1, 4], F32)
    thr = p.sb("d:thr_d", [P, 4], F32)
    epsc = p.sb("d:epsc_d", [P, 1], F32)
    ps_fc = p.psA
    ps_tr = p.psT
    ps_pr = p.psB
    ps_sc = p.psC[:, 0:128]

    p.dma("sync", id_sb, ident, "lx")
    p.dma("sync", id32_sb, id32, "lx")
    p.op("gpsimd", lambda e: e.memset(onesf, 1.0), sem="ve")
    p.op("gpsimd", lambda e: e.memset(ones16, 1.0), sem="ve")
    p.op("gpsimd", lambda e: e.memset(epsc, 1e-5), sem="ve")
    p.dma("gpsimd", lnrow[0:1, 0:4096], lnw, "lg")
    p.dma("gpsimd", lnrow[0:1, 4096:8192], lnb, "lg")
    cload = p.cnt["lg"]

    def gamma_phase(src_fn, ngr, gcols, tcols):
        for gidx in range(ngr):
            bb = w3s[gidx % 2][:, 0:gcols]
            if gidx >= 2:
                p.wait("sync", "vg", None)
            p.dma("sync", bb.rearrange("p (k d) -> p k d", d=1024),
                  src_fn(gidx), "lx")
            p.wait("vector", "lx", p.cnt["lx"])
            dst = stt[:, 8 + gidx % 2:9 + gidx % 2]
            p.op("vector", lambda e, bb=bb, dst=dst: e.tensor_reduce(
                dst, bb, axis=AX.X, op=ALU.add,
                apply_absolute_value=True), sem="vg")
            p.wait("vector", "vg", None)
            if gidx == 0:
                p.op("vector", lambda e, dst=dst: e.tensor_copy(
                    stt[:, 10:11], dst), sem="vg")
            else:
                p.op("vector", lambda e, dst=dst: e.tensor_add(
                    stt[:, 10:11], stt[:, 10:11], dst), sem="vg")
            p.wait("vector", "vg", None)
        p.wait("tensor", "vg", None)
        p.op("tensor", lambda e: e.transpose(
            ps_sc[0:1, 0:P], stt[:, 10:11], id32_sb), sem="pe")
        p.wait("vector", "pe", None)
        p.op("vector", lambda e: e.reduce_sum(
            g11[0:1, 0:1], ps_sc[0:1, 0:P], axis=AX.X), sem="vg")
        p.wait("vector", "vg", None)
        p.op("vector", lambda e: e.tensor_scalar(
            out=g11[0:1, 1:2], in0=g11[0:1, 0:1], scalar1=1.0 / (4096.0 * D),
            scalar2=1e-6, op0=ALU.mult, op1=ALU.add), sem="vg")
        p.wait("vector", "vg", None)
        p.op("vector", lambda e: e.tensor_scalar(
            out=g11[0:1, 2:3], in0=g11[0:1, 1:2], scalar1=0.5, scalar2=None,
            op0=ALU.mult), sem="vg")
        p.wait("vector", "vg", None)
        p.op("vector", lambda e: e.tensor_scalar(
            out=g11[0:1, 3:4], in0=g11[0:1, 2:3], scalar1=-1.0, scalar2=None,
            op0=ALU.mult), sem="vg")
        p.wait("vector", "vg", None)
        p.wait("tensor", "vg", None)
        p.op("tensor", lambda e: e.matmul(
            ps_sc[:, 0:2], onesf, g11[0:1, 2:4], start=True, stop=True,
            skip_group_check=True), sem="pe")
        p.wait("vector", "pe", None)
        p.op("vector", lambda e, tcols=tcols: e.tensor_copy(
            thr[:, tcols[0]:tcols[1] + 1], ps_sc[:, 0:2]), sem="ve")

    def quant_phase(src_fn, ntile, cols, tpos, tneg, dst_fn):
        rounds = cols // 1024
        for t in range(ntile):
            bb = w3s[t % 2][:, 0:cols]
            if t >= 2:
                p.wait("sync", "se", None)
            p.dma("sync", bb.rearrange("p (k d) -> p k d", d=1024),
                  src_fn(t), "lx")
            p.wait("scalar", "lx", p.cnt["lx"])
            p.wait("scalar", "ve", None)
            if t >= 1:
                p.wait("scalar", "ge", None)
                p.wait("scalar", "pe", None)
            sa_, sb_ = sgna[:, 0:cols], sgnb[:, 0:cols]
            p.op("scalar", lambda e, bb=bb, sa_=sa_, tn=tneg: e.activation(
                out=sa_, in_=bb, func=AF.Sign, bias=thr[:, tn:tn + 1]), sem="se")
            p.op("scalar", lambda e, bb=bb, sb_=sb_, tp=tpos: e.activation(
                out=sb_, in_=bb, func=AF.Sign, bias=thr[:, tp:tp + 1]), sem="se")
            p.wait("gpsimd", "se", None)
            if t >= 1:
                p.wait("gpsimd", "pe", None)
            p.op("gpsimd", lambda e, sa_=sa_, sb_=sb_: e.tensor_tensor(
                sa_, sa_, sb_, ALU.add), sem="ge")
            for r in range(rounds):
                p.wait("tensor", "ge" if r == 0 else "ve", None)
                for j in range(8):
                    p.op("tensor", lambda e, r=r, j=j, sa_=sa_: e.transpose(
                        ps_tr[:, j * P:(j + 1) * P],
                        sa_[:, (r * 8 + j) * P:(r * 8 + j + 1) * P], id_sb),
                        sem="pe" if j == 7 else None)
                p.wait("vector", "pe", None)
                p.op("vector", lambda e, t=t, r=r: e.tensor_copy(
                    dst_fn(t, r), ps_tr.rearrange("p (a b) -> p a b", a=8)),
                    sem="ve")

    wfc_r = wfc.rearrange("(a p) d -> p a d", p=P)
    gamma_phase(lambda g: wfc_r[:, 4 * g:4 * (g + 1), :], 8, 4096, (0, 1))
    quant_phase(lambda t: wfc_r[:, t:t + 1, :], 32, 1024, 0, 1,
                lambda t, r: wfcT[:, :, t * P:(t + 1) * P])
    wpr_r = wproj.rearrange("(a p) h -> p a h", p=P)
    gamma_phase(lambda g: wpr_r[:, g:g + 1, :].rearrange(
        "p a (k d) -> p (a k) d", d=1024), 8, 4096, (2, 3))
    quant_phase(lambda t: wpr_r[:, t:t + 1, :].rearrange(
        "p a (k d) -> p (a k) d", d=1024), 8, 4096, 2, 3,
                lambda t, r: wprT[:, r * 8:(r + 1) * 8, t * P:(t + 1) * P])

    p.wait("vector", "lg", cload)
    p.wait("vector", "se", None)
    p.wait("tensor", "se", None)
    for piece in range(16):
        src = lnrow[0:1, piece * 512:(piece + 1) * 512]
        dst = (lnwbc if piece < 8 else lnbbc)
        doff = (piece % 8) * 512
        p.wait("tensor", "ve", None)
        p.op("tensor", lambda e, src=src: e.matmul(
            ps_sc.bitcast(BF16)[:, 0:512] if False else ps_fc[:, 0:512],
            ones16, src, start=True, stop=True, skip_group_check=True),
            sem="pe")
        p.wait("vector", "pe", None)
        p.op("vector", lambda e, dst=dst, doff=doff: e.tensor_copy(
            dst[:, doff:doff + 512], ps_fc[:, 0:512]), sem="ve")

    c = lambda i: stt[:, i:i + 1]
    f_hist = {}
    p.wait("sync", "pe", None)
    p.wait("sync", "se", None)
    p.wait("sync", "ge", None)
    p.wait("sync", "ve", None)
    for t in range(NT):
        xnb = xnf[t % 2]
        xrbuf = xrb[t % 2]
        if t >= 2:
            p.wait("sync", "pe", f_hist[(t - 2, "lastmm")])
            p.wait("sync", "ve", f_hist[(t - 2, "evac")])
        p.dma("sync", xnb,
              xnfm.rearrange("a p t -> p a t")[:, :, t * P:(t + 1) * P], "lx")
        p.dma("sync", xrbuf, xr[t], "lx")
        ldv = p.cnt["lx"]
        p.wait("tensor", "lx", ldv)
        if t >= 1:
            p.wait("tensor", "se", f_hist[(t - 1, "sq")])
        for half in range(2):
            if half == 1:
                p.wait("tensor", "se", None)
            for dj in range(8):
                for sub in range(4):
                    p.op("tensor", lambda e, dj=dj, sub=sub, half=half, xnb=xnb:
                         e.matmul(ps_fc[:, sub * 512:(sub + 1) * 512],
                                  xnb[:, dj, :],
                                  wfcT[:, dj, half * 2048 + sub * 512:
                                       half * 2048 + (sub + 1) * 512],
                                  start=(dj == 0), stop=(dj == 7),
                                  skip_group_check=True),
                         sem="pe" if (dj == 7 and sub == 3) else None)
            p.wait("scalar", "pe", None)
            for gg in range(2):
                p.op("scalar", lambda e, half=half, gg=gg: e.activation(
                    out=h_sb[:, half * 2048 + gg * 1024:
                             half * 2048 + (gg + 1) * 1024],
                    in_=ps_fc[:, gg * 1024:(gg + 1) * 1024], func=AF.Gelu,
                    scale=0.5,
                    accum_out=stt[:, 8 + half * 2 + gg:9 + half * 2 + gg]),
                    sem="se")
        p.wait("vector", "se", None)
        p.op("vector", lambda e: e.reduce_sum(c(0), stt[:, 8:12], axis=AX.X),
             sem="vg")
        p.wait("scalar", "vg", None)
        for gg in range(4):
            p.op("scalar", lambda e, gg=gg: e.activation(
                out=ps_fc[:, (gg % 2) * 1024:(gg % 2 + 1) * 1024],
                in_=h_sb[:, gg * 1024:(gg + 1) * 1024], func=AF.Square,
                accum_out=stt[:, 8 + gg:9 + gg]), sem="se")
        f_hist[(t, "sq")] = p.cnt["se"]
        p.wait("vector", "se", None)
        p.op("vector", lambda e: e.reduce_sum(c(1), stt[:, 8:12], axis=AX.X),
             sem="vg")
        p.wait("vector", "vg", None)
        p.op("vector", lambda e: e.tensor_scalar(
            out=c(2), in0=c(0), scalar1=1.0 / 4096, scalar2=None,
            op0=ALU.mult), sem="vg")
        p.wait("vector", "vg", None)
        p.op("vector", lambda e: e.tensor_scalar(
            out=c(3), in0=c(1), scalar1=1.0 / 4096, scalar2=None,
            op0=ALU.mult), sem="vg")
        p.wait("vector", "vg", None)
        p.op("vector", lambda e: e.tensor_tensor(c(4), c(2), c(2), ALU.mult),
             sem="vg")
        p.wait("vector", "vg", None)
        p.op("vector", lambda e: e.tensor_tensor(c(5), c(3), c(4), ALU.subtract),
             sem="vg")
        p.wait("scalar", "vg", None)
        p.op("scalar", lambda e: e.activation(
            out=c(6), in_=c(5), func=AF.Sqrt, bias=epsc[:, 0:1]), sem="se")
        p.wait("vector", "se", None)
        p.op("vector", lambda e: e.reciprocal(c(6), c(6)), sem="vg")
        p.wait("vector", "vg", None)
        p.op("vector", lambda e: e.scalar_tensor_tensor(
            out=c(7), in0=c(2), scalar=-1.0, in1=c(6), op0=ALU.mult,
            op1=ALU.mult), sem="vg")
        p.wait("scalar", "vg", None)
        p.op("scalar", lambda e: e.activation(
            out=h_sb, in_=h_sb, func=AF.Identity, scale=c(6), bias=c(7)),
            sem="se")
        p.wait("vector", "se", None)
        p.op("vector", lambda e: e.tensor_tensor(h_sb, h_sb, lnwbc, ALU.mult),
             sem="ve")
        p.op("vector", lambda e: e.tensor_add(h_sb, h_sb, lnbbc), sem="ve")
        for r in range(4):
            p.wait("tensor", "ve", None)
            for j in range(8):
                p.op("tensor", lambda e, r=r, j=j: e.transpose(
                    ps_tr[:, j * P:(j + 1) * P],
                    h_sb[:, (r * 8 + j) * P:(r * 8 + j + 1) * P], id_sb),
                    sem="pe" if j == 7 else None)
            p.wait("vector", "pe", None)
            p.op("vector", lambda e, r=r: e.tensor_copy(
                hT[:, r * 8:(r + 1) * 8, :],
                ps_tr.rearrange("p (a b) -> p a b", a=8)), sem="ve")
        p.wait("tensor", "ve", None)
        for hc in range(32):
            for half in range(2):
                p.op("tensor", lambda e, hc=hc, half=half: e.matmul(
                    ps_pr[:, half * 512:(half + 1) * 512], hT[:, hc, :],
                    wprT[:, hc, half * 512:(half + 1) * 512],
                    start=(hc == 0), stop=(hc == 31), skip_group_check=True),
                    sem="pe" if (hc == 31 and half == 1) else None)
        f_hist[(t, "lastmm")] = p.cnt["pe"]
        p.wait("vector", "pe", None)
        p.wait("vector", "lx", ldv)
        if t >= 2:
            p.wait("vector", "st", None)
        p.op("vector", lambda e, t=t, xrbuf=xrbuf: e.scalar_tensor_tensor(
            out=x1s[t % 2], in0=ps_pr, scalar=0.5, in1=xrbuf, op0=ALU.mult,
            op1=ALU.add), sem="ve")
        f_hist[(t, "evac")] = p.cnt["ve"]
        p.wait("gpsimd", "ve", None)
        p.dma("gpsimd", o_x1[t], x1s[t % 2], "st")
    p.wait("gpsimd", "st", None)
    p.free_prefix("d:")


def _pa_np():
    pa0 = np.zeros((8, P, P), BF16NP)
    pa1 = np.zeros((8, P, P), BF16NP)
    pb = np.zeros((8, P, P), BF16NP)
    for h in range(8):
        for a in range(P):
            pa0[h, 16 * h + a // 16, a] = 1
            pa1[h, 16 * h + 8 + a // 16, a] = 1
            pb[h, 16 * h + a % 16, a] = 1
    return pa0, pa1, pb


def emit_e(p, x1, gamt, wq1, wkv1, o_xn2, o_qx, o_kvp, ident, c_pa0, c_pa1,
           c_pb):
    xb = [p.sb(f"e:xb{i}", [P, D], F32) for i in range(2)]
    xn16 = [p.sb(f"e:xn16{i}", [P, D], BF16) for i in range(2)]
    nr16 = [p.sb(f"e:nr16{i}", [P, D], BF16) for i in range(2)]
    xn2fm = p.sb("e:xn2fm_sb", [P, 8, HT], BF16)
    nrmfm = p.sb("e:nrmfm_sb", [P, 8, HT], BF16)
    q_sb = p.sb("e:q_sb", [P, HT], BF16)
    k_sb = p.sb("e:k_sb", [P, HT], BF16)
    v_sb = p.sb("e:v_sb", [P, HT], BF16)
    k_tm = p.sb("e:k_tm", [P, NT, P], BF16)
    vaug = p.sb("e:vaug_e", [P, NT, 8, 17], BF16)
    kexp = p.sb("e:kexp", [P, NT, 8, 273], BF16)
    qc0 = [p.sb(f"e:qc0_{i}", [P, HT], BF16) for i in range(2)]
    qx = [p.sb(f"e:qx_{i}", [P, HT], BF16) for i in range(2)]
    kvp_sb = p.sb("e:kvp_sb", [P, 8, 3, 17], F32)
    ab_scr = p.sb("e:ab_scr", [P, 512], BF16)
    wq1_sb = p.sb("e:wq1_sb", [P, 8, P], BF16)
    wkv1_sb = p.sb("e:wkv1_sb", [P, 8, 256], BF16)
    gam_sb = p.sb("e:gam_sb", [P, 8], F32)
    pa0_sb = p.sb("e:pa0_sb", [P, 8, P], BF16)
    pa1_sb = p.sb("e:pa1_sb", [P, 8, P], BF16)
    pb_sb = p.sb("e:pb_sb", [P, 8, P], BF16)
    id_sb = p.sb("e:id_sb_e", [P, P], BF16)
    stt = p.sb("e:stt_e", [P, 16], F32)
    sqscr = p.sb("e:sqscr_e", [P, D], F32)
    epsc = p.sb("e:epsc_e", [P, 1], F32)
    ps_tr = p.psT
    ps_mm = [p.psA[:, i * 512:(i + 1) * 512] for i in range(2)]
    ps_ab = [p.psA[:, 1024 + i * 512:1024 + (i + 1) * 512] for i in range(2)]

    p.dma("sync", id_sb, ident, "lx")
    p.dma("gpsimd", wq1_sb, wq1.rearrange("(a p) k -> p a k", p=P), "lg")
    p.dma("gpsimd", wkv1_sb, wkv1.rearrange("(a p) k -> p a k", p=P), "lg")
    p.dma("gpsimd", gam_sb, gamt, "lg")
    p.dma("gpsimd", pa0_sb, c_pa0.rearrange("a p k -> p a k"), "lg")
    p.dma("gpsimd", pa1_sb, c_pa1.rearrange("a p k -> p a k"), "lg")
    p.dma("gpsimd", pb_sb, c_pb.rearrange("a p k -> p a k"), "lg")
    p.op("gpsimd", lambda e: e.memset(epsc, 1e-5), sem="ve")
    p.op("gpsimd", lambda e: e.memset(vaug[:, :, :, 16:17], 1.0), sem="ve")
    p.op("gpsimd", lambda e: e.memset(kexp[:, :, :, 0:1], 1.0), sem="ve")
    p.op("gpsimd", lambda e: e.memset(qc0[0], 0.0), sem="ve")
    p.op("gpsimd", lambda e: e.memset(qc0[1], 0.0), sem="ve")
    p.op("gpsimd", lambda e: e.memset(qc0[0][0:1, :], 1.0), sem="ve")
    p.op("gpsimd", lambda e: e.memset(qc0[1][0:1, :], 1.0), sem="ve")
    p.op("gpsimd", lambda e: e.memset(kvp_sb.rearrange("p a b c -> p (a b c)"),
                                      0.0), sem="ve")
    wload = p.cnt["lg"]
    p.wait("vector", "lg", wload)
    p.op("vector", lambda e: e.tensor_tensor(
        wq1_sb, wq1_sb, gam_sb[:, :, None].broadcast_to([P, 8, P]),
        ALU.mult), sem="vg")
    p.wait("vector", "vg", None)
    p.op("vector", lambda e: e.tensor_tensor(
        wkv1_sb, wkv1_sb, gam_sb[:, :, None].broadcast_to([P, 8, 256]),
        ALU.mult), sem="vg")
    gfold = p.cnt["vg"]

    c = lambda i: stt[:, i:i + 1]
    for i in range(NT):
        buf, xn, nr = xb[i % 2], xn16[i % 2], nr16[i % 2]
        if i >= 2:
            p.wait("sync", "se", None)
        p.dma("sync", buf, x1[i], "lx")
        ldv = p.cnt["lx"]
        p.wait("vector", "lx", ldv)
        if i > 0:
            p.wait("vector", "se", None)
        p.op("vector", lambda e, b=buf: e.reduce_sum(c(0), b, axis=AX.X),
             sem="vg")
        p.wait("scalar", "lx", ldv)
        p.wait("scalar", "vg", None)
        p.op("scalar", lambda e, b=buf: e.activation(
            out=sqscr, in_=b, func=AF.Square, accum_out=c(1)), sem="se")
        p.wait("vector", "se", None)
        p.op("vector", lambda e: e.tensor_scalar(
            out=c(2), in0=c(0), scalar1=1.0 / D, scalar2=None, op0=ALU.mult),
            sem="vg")
        p.wait("vector", "vg", None)
        p.op("vector", lambda e: e.tensor_scalar(
            out=c(3), in0=c(1), scalar1=1.0 / D, scalar2=None, op0=ALU.mult),
            sem="vg")
        p.wait("vector", "vg", None)
        p.op("vector", lambda e: e.tensor_tensor(c(4), c(2), c(2), ALU.mult),
             sem="vg")
        p.wait("vector", "vg", None)
        p.op("vector", lambda e: e.tensor_tensor(c(5), c(3), c(4), ALU.subtract),
             sem="vg")
        p.wait("scalar", "vg", None)
        p.op("scalar", lambda e: e.activation(
            out=c(6), in_=c(5), func=AF.Sqrt, bias=epsc[:, 0:1]), sem="se")
        p.wait("vector", "se", None)
        p.op("vector", lambda e: e.reciprocal(c(6), c(6)), sem="vg")
        p.wait("vector", "vg", None)
        p.op("vector", lambda e: e.scalar_tensor_tensor(
            out=c(7), in0=c(2), scalar=-1.0, in1=c(6), op0=ALU.mult,
            op1=ALU.mult), sem="vg")
        p.wait("vector", "vg", None)
        p.op("vector", lambda e: e.tensor_scalar(
            out=c(8), in0=c(5), scalar1=1e-5, scalar2=None, op0=ALU.add),
            sem="vg")
        p.wait("vector", "vg", None)
        p.op("vector", lambda e: e.reciprocal(c(8), c(8)), sem="vg")
        p.wait("vector", "vg", None)
        p.op("vector", lambda e: e.tensor_tensor(c(9), c(5), c(8), ALU.mult),
             sem="vg")
        p.wait("vector", "vg", None)
        p.op("vector", lambda e: e.tensor_scalar(
            out=c(9), in0=c(9), scalar1=9.765625e-28, scalar2=None,
            op0=ALU.max), sem="vg")
        p.wait("scalar", "vg", None)
        p.op("scalar", lambda e: e.activation(
            out=c(10), in_=c(9), func=AF.Sqrt), sem="se")
        p.wait("vector", "se", None)
        p.op("vector", lambda e: e.reciprocal(c(10), c(10)), sem="vg")
        p.wait("vector", "vg", None)
        p.op("vector", lambda e: e.tensor_tensor(c(11), c(6), c(10), ALU.mult),
             sem="vg")
        p.wait("vector", "vg", None)
        p.op("vector", lambda e: e.tensor_tensor(c(12), c(7), c(10), ALU.mult),
             sem="vg")
        p.wait("scalar", "vg", None)
        if i >= 2:
            p.wait("scalar", "pe", p.cnt["pe"] - 2)
        p.op("scalar", lambda e, b=buf, o=xn: e.activation(
            out=o, in_=b, func=AF.Identity, scale=c(6), bias=c(7)), sem="se")
        p.op("scalar", lambda e, b=buf, o=nr: e.activation(
            out=o, in_=b, func=AF.Identity, scale=c(11), bias=c(12)), sem="se")
        p.wait("tensor", "se", None)
        if i > 0:
            p.wait("tensor", "ve", None)
        for j in range(8):
            p.op("tensor", lambda e, xn=xn, j=j: e.transpose(
                ps_tr[:, j * P:(j + 1) * P], xn[:, j * P:(j + 1) * P], id_sb),
                sem="pe" if j == 7 else None)
        p.wait("vector", "pe", None)
        p.op("vector", lambda e, i=i: e.tensor_copy(
            xn2fm[:, :, i * P:(i + 1) * P],
            ps_tr.rearrange("p (a b) -> p a b", a=8)), sem="ve")
        p.wait("tensor", "ve", None)
        for j in range(8):
            p.op("tensor", lambda e, nr=nr, j=j: e.transpose(
                ps_tr[:, j * P:(j + 1) * P], nr[:, j * P:(j + 1) * P], id_sb),
                sem="pe" if j == 7 else None)
        p.wait("vector", "pe", None)
        p.op("vector", lambda e, i=i: e.tensor_copy(
            nrmfm[:, :, i * P:(i + 1) * P],
            ps_tr.rearrange("p (a b) -> p a b", a=8)), sem="ve")
    p.wait("gpsimd", "ve", None)
    for a in range(8):
        p.dma("gpsimd", o_xn2[a], xn2fm[:, a, :], "st")

    mmctr = {"n": 0}

    def mm_block(chunks, dest_fn, evac_eng="vector"):
        k = mmctr["n"]
        buf = ps_mm[k % 2]
        if k >= 2:
            p.wait("tensor", "ve", mmctr[f"evac{k - 2}"])
        for ci, (lh, rh) in enumerate(chunks):
            p.op("tensor", lambda e, lh=lh, rh=rh, st=(ci == 0),
                 sp=(ci == len(chunks) - 1), b=buf: e.matmul(
                     b, lh, rh, start=st, stop=sp, skip_group_check=True),
                 sem="pe" if ci == len(chunks) - 1 else None)
        p.wait(evac_eng, "pe", None)
        p.op(evac_eng, lambda e, b=buf: dest_fn(e, b), sem="ve")
        mmctr[f"evac{k}"] = p.cnt["ve"]
        mmctr["n"] += 1

    p.wait("tensor", "vg", gfold)
    p.wait("tensor", "ve", None)
    for sl in range(4):
        chunks = [(wq1_sb[:, dj, :], nrmfm[:, dj, sl * 512:(sl + 1) * 512])
                  for dj in range(8)]
        mm_block(chunks, lambda e, b, sl=sl: e.activation(
            out=q_sb[:, sl * 512:(sl + 1) * 512], in_=b, func=AF.Copy,
            scale=0.25), evac_eng="scalar")
    for c2 in range(2):
        dst = k_sb if c2 == 0 else v_sb
        for sl in range(4):
            chunks = [(wkv1_sb[:, dj, c2 * P:(c2 + 1) * P],
                       nrmfm[:, dj, sl * 512:(sl + 1) * 512])
                      for dj in range(8)]
            mm_block(chunks, lambda e, b, dst=dst, sl=sl: e.tensor_copy(
                dst[:, sl * 512:(sl + 1) * 512], b))

    p.wait("tensor", "ve", None)
    p.wait("tensor", "se", None)
    trcp_hist = {}
    for t in range(NT):
        if t >= 1:
            p.wait("tensor", "ve", trcp_hist[t - 1])
        p.op("tensor", lambda e, t=t: e.transpose(
            ps_tr[:, 0:P], k_sb[:, t * P:(t + 1) * P], id_sb), sem=None)
        p.op("tensor", lambda e, t=t: e.transpose(
            ps_tr[:, P:2 * P], v_sb[:, t * P:(t + 1) * P], id_sb), sem="pe")
        p.wait("vector", "pe", None)
        p.op("vector", lambda e, t=t: e.tensor_copy(
            k_tm[:, t, :], ps_tr[:, 0:P]), sem="ve")
        p.op("vector", lambda e, t=t: e.tensor_copy(
            vaug[:, t, :, 0:16],
            ps_tr[:, P:2 * P].rearrange("p (h v) -> p h v", h=8)), sem="ve")
        trcp_hist[t] = p.cnt["ve"]
        p.wait("vector", "ve", None)
        p.op("vector", lambda e, t=t: e.tensor_copy(
            kexp[:, t, :, 1:17],
            k_tm[:, t, :].rearrange("p (h v) -> p h v", h=8)), sem="ve")
        for h in range(8):
            p.op("vector", lambda e, t=t, h=h: e.scalar_tensor_tensor(
                out=kexp[:, t, h, 17:273].rearrange("p (a b) -> p a b", a=16),
                in0=k_tm[:, t, h * 16:(h + 1) * 16][:, :, None]
                .broadcast_to([P, 16, 16]),
                scalar=0.7071067811865476,
                in1=k_tm[:, t, None, h * 16:(h + 1) * 16]
                .broadcast_to([P, 16, 16]),
                op0=ALU.mult, op1=ALU.mult),
                sem="ve" if h == 7 else None)
    kexp_done = p.cnt["ve"]

    csl = [(0, 17), (17, 145), (145, 273)]
    p.wait("tensor", "ve", kexp_done)
    grp = 0
    for h in range(8):
        for ci, (c0_, c1_) in enumerate(csl):
            buf = ps_mm[grp % 2]
            if grp >= 2:
                p.wait("tensor", "ve", mmctr[f"kv{grp - 2}"])
            for t in range(NT):
                p.op("tensor", lambda e, t=t, h=h, c0_=c0_, c1_=c1_, buf=buf:
                     e.matmul(buf[0:c1_ - c0_, 0:17],
                              kexp[:, t, h, c0_:c1_], vaug[:, t, h, :],
                              start=(t == 0), stop=(t == NT - 1),
                              skip_group_check=True),
                     sem="pe" if t == NT - 1 else None)
            p.wait("vector", "pe", None)
            p.op("vector", lambda e, h=h, ci=ci, buf=buf: e.tensor_copy(
                kvp_sb[:, h, ci, :], buf[:, 0:17]), sem="ve")
            mmctr[f"kv{grp}"] = p.cnt["ve"]
            grp += 1
    p.wait("gpsimd", "ve", None)
    p.dma("gpsimd", o_kvp.rearrange("h (c p) e -> p h c e", p=P), kvp_sb, "st")

    qdma_hist = {}
    for h in range(8):
        qb = qc0[h % 2]
        if h >= 2:
            p.wait("gpsimd", "st", qdma_hist[h - 2])
        p.wait("gpsimd", "ve", mmctr["evac3"])
        p.dma("gpsimd", qb[1:17, :], q_sb[16 * h:16 * h + 16, :], "lg")
        p.wait("gpsimd", "lg", None)
        p.dma("gpsimd", o_qx[h, 0], qb, "st")
        qdma_hist[h] = p.cnt["st"]
    for h in range(8):
        for tile in range(2):
            pa = pa0_sb if tile == 0 else pa1_sb
            qxb = qx[(h * 2 + tile) % 2]
            if h * 2 + tile >= 2:
                p.wait("vector", "st", qdma_hist[(h * 2 + tile - 2, "qx")])
            for sl in range(4):
                p.wait("tensor", "ve", None)
                p.op("tensor", lambda e, pa=pa, h=h, sl=sl: e.matmul(
                    ps_ab[0], pa[:, h, :], q_sb[:, sl * 512:(sl + 1) * 512],
                    start=True, stop=True, skip_group_check=True), sem=None)
                p.op("tensor", lambda e, h=h, sl=sl: e.matmul(
                    ps_ab[1], pb_sb[:, h, :], q_sb[:, sl * 512:(sl + 1) * 512],
                    start=True, stop=True, skip_group_check=True), sem="pe")
                p.wait("vector", "pe", None)
                p.op("vector", lambda e: e.tensor_copy(
                    ab_scr, ps_ab[0]), sem="ve")
                p.op("vector", lambda e, qxb=qxb, sl=sl: e.scalar_tensor_tensor(
                    out=qxb[:, sl * 512:(sl + 1) * 512], in0=ab_scr,
                    scalar=0.7071067811865476, in1=ps_ab[1],
                    op0=ALU.mult, op1=ALU.mult), sem="ve")
            p.wait("gpsimd", "ve", None)
            p.dma("gpsimd", o_qx[h, 1 + tile], qxb, "st")
            qdma_hist[(h * 2 + tile, "qx")] = p.cnt["st"]

    p.wait("gpsimd", "st", None)
    p.free_prefix("e:")


def emit_f(p, qx, kvp2, wo1, o_at, ident):
    qx_sb = p.sb("f:qx_sb", [P, 8, 3, HT], BF16)
    kva = p.sb("f:kva", [P, 8, 3, 17], F32)
    kvb = p.sb("f:kvb", [P, 8, 3, 17], F32)
    kv16 = p.sb("f:kv16", [P, 8, 3, 17], BF16)
    wo_sb = p.sb("f:wo_sb_f", [P, D], BF16)
    id_sb = p.sb("f:id_sb_f", [P, P], BF16)
    o1 = [p.sb(f"f:o1_{i}", [P, P], BF16) for i in range(2)]
    o1t = [p.sb(f"f:o1t_{i}", [P, P], BF16) for i in range(2)]
    dn = p.sb("f:dn", [P, 8], F32)
    atstg = [p.sb(f"f:atstg{i}", [P, D], BF16) for i in range(2)]
    ps_o = [p.psA[:, 0:136], p.psA[:, 512:648]]
    ps_tr = p.psT[:, 0:P]
    ps_w = p.psB

    p.dma("sync", qx_sb, qx.rearrange("h c p t -> p h c t"), "lx")
    p.dma("sync", id_sb, ident, "lx")
    p.dma("sync", kva, kvp2[0].rearrange("h (c p) e -> p h c e", p=P), "lx")
    p.dma("sync", kvb, kvp2[1].rearrange("h (c p) e -> p h c e", p=P), "lx")
    p.dma("gpsimd", wo_sb, wo1, "lg")
    lall = p.cnt["lx"]
    p.wait("vector", "lx", lall)
    p.op("vector", lambda e: e.tensor_add(kv16, kva, kvb), sem="vg")
    p.wait("tensor", "vg", None)
    p.wait("tensor", "lg", 16)

    csl = [(0, 17), (17, 145), (145, 273)]
    ev_hist = {}
    for t in range(NT):
        po = ps_o[t % 2]
        if t >= 2:
            p.wait("tensor", "ve", ev_hist[t - 2])
        first = True
        for h in range(8):
            for ci, (c0_, c1_) in enumerate(csl):
                p.op("tensor", lambda e, h=h, ci=ci, c0_=c0_, c1_=c1_, t=t,
                     po=po, st=first, sp=(h == 7 and ci == 2): e.matmul(
                         po[:, h * 17:(h + 1) * 17],
                         qx_sb[0:c1_ - c0_, h, ci, t * P:(t + 1) * P],
                         kv16[0:c1_ - c0_, h, ci, :],
                         start=st, stop=sp, skip_group_check=True),
                     sem="pe" if (h == 7 and ci == 2) else None)
                first = False
        p.wait("vector", "pe", None)
        p.op("vector", lambda e, po=po: e.tensor_scalar(
            out=dn[:, 0:8],
            in0=po.rearrange("p (h v) -> p h v", v=17)[:, :, 16:17]
            .rearrange("p h v -> p (h v)"),
            scalar1=1e-5, scalar2=None, op0=ALU.max), sem="vg")
        p.wait("vector", "vg", None)
        p.op("vector", lambda e: e.reciprocal(dn[:, 0:8], dn[:, 0:8]),
             sem="vg")
        p.wait("vector", "vg", None)
        p.op("vector", lambda e, po=po, t=t: e.tensor_tensor(
            o1[t % 2].rearrange("p (h v) -> p h v", h=8),
            po.rearrange("p (h v) -> p h v", v=17)[:, :, 0:16],
            dn[:, :, None].broadcast_to([P, 8, 16]), ALU.mult), sem="ve")
        ev_hist[t] = p.cnt["ve"]
        p.wait("tensor", "ve", None)
        p.op("tensor", lambda e, t=t: e.transpose(
            ps_tr, o1[t % 2], id_sb), sem="pe")
        p.wait("vector", "pe", None)
        p.op("vector", lambda e, t=t: e.tensor_copy(o1t[t % 2], ps_tr),
             sem="ve")
        p.wait("tensor", "ve", None)
        for half in range(2):
            p.op("tensor", lambda e, t=t, half=half: e.matmul(
                ps_w[:, half * 512:(half + 1) * 512], o1t[t % 2],
                wo_sb[:, half * 512:(half + 1) * 512], start=True, stop=True,
                skip_group_check=True), sem="pe" if half == 1 else None)
        p.wait("vector", "pe", None)
        if t >= 2:
            p.wait("vector", "st", None)
        p.op("vector", lambda e, t=t: e.tensor_copy(atstg[t % 2], ps_w),
             sem="ve")
        p.wait("gpsimd", "ve", None)
        p.dma("gpsimd", o_at[t], atstg[t % 2], "st")
    p.wait("gpsimd", "st", None)
    p.free_prefix("f:")


def emit_g(p, xn2, x1, at1, wg, wu, wd, o_out, ident):
    xn2_sb = p.sb("g:xn2_sb", [P, 8, HT], BF16)
    wdT = p.sb("g:wdT", [P, 24, D], BF16)
    hfm = p.sb("g:hfm", [P, 24, D], BF16)
    wgb = [p.sb(f"g:wgb{i}", [P, D], BF16) for i in range(2)]
    wub = [p.sb(f"g:wub{i}", [P, D], BF16) for i in range(2)]
    wdb = [p.sb(f"g:wdb{i}", [P, 3072], BF16) for i in range(2)]
    wgT = p.sb("g:wgT", [P, 8, P], BF16)
    wuT = p.sb("g:wuT", [P, 8, P], BF16)
    sg = p.sb("g:sg", [P, D], BF16)
    id_sb = p.sb("g:id_sb_g", [P, P], BF16)
    x1b = [p.sb(f"g:x1b{i}", [P, D], F32) for i in range(2)]
    atb = [p.sb(f"g:atb{i}", [P, D], BF16) for i in range(2)]
    ostg = [p.sb(f"g:ostg{i}", [P, D], F32) for i in range(2)]
    tmp = p.sb("g:tmp_g", [P, D], F32)
    ps_g = p.psA[:, 0:1024]
    ps_u = p.psA[:, 1024:2048]
    ps_tr = p.psT
    ps_d = p.psB

    p.dma("sync", id_sb, ident, "lx")
    p.dma("sync", xn2_sb, xn2.rearrange("a p t -> p a t"), "lx")
    xnld = p.cnt["lx"]

    tr_hist = {}
    for dc in range(8):
        bb = wdb[dc % 2]
        if dc >= 2:
            p.wait("gpsimd", "ve", tr_hist[dc - 2])
        p.dma("gpsimd", bb, wd[dc * P:(dc + 1) * P, :], "lg")
        ldv = p.cnt["lg"]
        p.wait("tensor", "lg", ldv)
        p.wait("tensor", "lx", 16)
        for r in range(3):
            if not (dc == 0 and r == 0):
                p.wait("tensor", "ve", None)
            for j in range(8):
                p.op("tensor", lambda e, bb=bb, r=r, j=j: e.transpose(
                    ps_tr[:, j * P:(j + 1) * P],
                    bb[:, (r * 8 + j) * P:(r * 8 + j + 1) * P], id_sb),
                    sem="pe" if j == 7 else None)
            p.wait("vector", "pe", None)
            p.op("vector", lambda e, dc=dc, r=r: e.tensor_copy(
                wdT[:, r * 8:(r + 1) * 8, dc * P:(dc + 1) * P],
                ps_tr.rearrange("p (a b) -> p a b", a=8)), sem="ve")
        tr_hist[dc] = p.cnt["ve"]

    dn_hist = {}
    for half in range(2):
        if half == 1:
            p.wait("vector", "pe", None)
        for hc in range(24):
            bg, bu = wgb[hc % 2], wub[hc % 2]
            if hc >= 2 or half > 0:
                p.wait("gpsimd", "se", None)
            p.dma("gpsimd", bg, wg[hc * P:(hc + 1) * P, :], "lg")
            p.dma("gpsimd", bu, wu[hc * P:(hc + 1) * P, :], "lg")
            ldv = p.cnt["lg"]
            p.wait("tensor", "lg", ldv)
            p.wait("tensor", "ve", None)
            if hc > 0 or half > 0:
                p.wait("tensor", "se", None)
            for j in range(8):
                p.op("tensor", lambda e, bg=bg, j=j: e.transpose(
                    ps_tr[:, j * P:(j + 1) * P], bg[:, j * P:(j + 1) * P],
                    id_sb), sem="pe" if j == 7 else None)
            p.wait("vector", "pe", None)
            p.op("vector", lambda e: e.tensor_copy(
                wgT, ps_tr.rearrange("p (a b) -> p a b", a=8)), sem="ve")
            p.wait("tensor", "ve", None)
            for j in range(8):
                p.op("tensor", lambda e, bu=bu, j=j: e.transpose(
                    ps_tr[:, j * P:(j + 1) * P], bu[:, j * P:(j + 1) * P],
                    id_sb), sem="pe" if j == 7 else None)
            p.wait("vector", "pe", None)
            p.op("vector", lambda e: e.tensor_copy(
                wuT, ps_tr.rearrange("p (a b) -> p a b", a=8)), sem="ve")
            p.wait("tensor", "ve", None)
            p.wait("tensor", "lx", xnld)
            for dj in range(8):
                for sl in range(2):
                    p.op("tensor", lambda e, dj=dj, sl=sl, half=half: e.matmul(
                        ps_g[:, sl * 512:(sl + 1) * 512], wgT[:, dj, :],
                        xn2_sb[:, dj, half * 1024 + sl * 512:
                               half * 1024 + (sl + 1) * 512],
                        start=(dj == 0), stop=(dj == 7),
                        skip_group_check=True), sem=None)
            for dj in range(8):
                for sl in range(2):
                    p.op("tensor", lambda e, dj=dj, sl=sl, half=half: e.matmul(
                        ps_u[:, sl * 512:(sl + 1) * 512], wuT[:, dj, :],
                        xn2_sb[:, dj, half * 1024 + sl * 512:
                               half * 1024 + (sl + 1) * 512],
                        start=(dj == 0), stop=(dj == 7),
                        skip_group_check=True),
                        sem="pe" if (dj == 7 and sl == 1) else None)
            p.wait("scalar", "pe", None)
            p.op("scalar", lambda e: e.activation(
                out=sg, in_=ps_g, func=AF.Silu, scale=1.0 / 127.0), sem="se")
            p.wait("vector", "se", None)
            p.op("vector", lambda e, hc=hc: e.scalar_tensor_tensor(
                out=hfm[:, hc, :], in0=ps_u, scalar=1.0 / 127.0, in1=sg,
                op0=ALU.mult, op1=ALU.mult), sem="ve")
        hdone = p.cnt["ve"]
        p.wait("tensor", "ve", hdone)
        for tt in range(8):
            t = half * 8 + tt
            if t >= 2:
                p.wait("sync", "ve", dn_hist[t - 2])
            p.dma("sync", x1b[t % 2], x1[t], "lx")
            p.dma("sync", atb[t % 2], at1[t], "lx")
            ldv = p.cnt["lx"]
            if t >= 1:
                p.wait("tensor", "ve", dn_hist[t - 1])
            for hc in range(24):
                for sl in range(2):
                    p.op("tensor", lambda e, hc=hc, sl=sl, tt=tt: e.matmul(
                        ps_d[:, sl * 512:(sl + 1) * 512],
                        hfm[:, hc, tt * P:(tt + 1) * P],
                        wdT[:, hc, sl * 512:(sl + 1) * 512],
                        start=(hc == 0), stop=(hc == 23),
                        skip_group_check=True),
                        sem="pe" if (hc == 23 and sl == 1) else None)
            p.wait("vector", "pe", None)
            p.wait("vector", "lx", ldv)
            if t >= 2:
                p.wait("vector", "st", None)
            p.op("vector", lambda e, t=t: e.scalar_tensor_tensor(
                out=tmp, in0=ps_d, scalar=1.0 / 127.0, in1=x1b[t % 2],
                op0=ALU.mult, op1=ALU.add), sem="ve")
            p.wait("vector", "ve", None)
            p.op("vector", lambda e, t=t: e.tensor_add(
                ostg[t % 2], tmp, atb[t % 2]), sem="ve")
            dn_hist[t] = p.cnt["ve"]
            p.wait("gpsimd", "ve", None)
            p.dma("gpsimd", o_out[t], ostg[t % 2], "st")
    p.wait("gpsimd", "st", None)
    p.free_prefix("g:")


def build_l1():
    nc = new_nc()
    xf = nc.dram_tensor("xf", [S, D], F32, kind="ExternalInput").ap()
    wq = nc.dram_tensor("wq", [D, 512], F32, kind="ExternalInput").ap()
    wkvd = nc.dram_tensor("wkvd", [D, 256], F32, kind="ExternalInput").ap()
    wku = nc.dram_tensor("wku", [256, 512], F32, kind="ExternalInput").ap()
    wvu = nc.dram_tensor("wvu", [256, 512], F32, kind="ExternalInput").ap()
    o_xnfm = nc.dram_tensor("xnfm", [8, P, S], BF16, kind="ExternalOutput").ap()
    kt = nc.dram_tensor("kt", [4, P, S], BF16, kind="Internal").ap()
    qt = nc.dram_tensor("qt", [4, P, S], BF16, kind="Internal").ap()
    va = nc.dram_tensor("vaug", [32, P, 520], BF16, kind="Internal").ap()
    o_y = nc.dram_tensor("yfm", [4, P, S], BF16, kind="ExternalOutput").ap()
    ident = nc.inline_tensor(_ident_np(), name="ident").ap()
    maskc = nc.inline_tensor(_mask_np(), name="maskc").ap()
    p = Prog(nc)
    emit_a(p, xf, wq, wkvd, wku, wvu, o_xnfm, kt, qt, va, ident)
    p.barrier()
    emit_b(p, kt, qt, va, o_y, maskc)
    return p.build()


def build_l2():
    nc = new_nc()
    y_all = nc.dram_tensor("y_all", [8, P, HT], BF16, kind="ExternalInput").ap()
    wout = nc.dram_tensor("wout", [D, D], F32, kind="ExternalInput").ap()
    x_own = nc.dram_tensor("x_own", [HT, D], F32, kind="ExternalInput").ap()
    xr = nc.dram_tensor("xr", [NT, P, D], F32, kind="Internal").ap()
    xnfm = nc.dram_tensor("xnfm_own", [8, P, HT], BF16, kind="ExternalInput").ap()
    wfc = nc.dram_tensor("wfc", [4096, D], F32, kind="ExternalInput").ap()
    lnw = nc.dram_tensor("lnw", [1, 4096], F32, kind="ExternalInput").ap()
    lnb = nc.dram_tensor("lnb", [1, 4096], F32, kind="ExternalInput").ap()
    wproj = nc.dram_tensor("wproj", [D, 4096], F32, kind="ExternalInput").ap()
    o_x1 = nc.dram_tensor("x1", [NT, P, D], F32, kind="ExternalOutput").ap()
    gamt = nc.dram_tensor("gamt", [P, 8], F32, kind="ExternalInput").ap()
    wq1 = nc.dram_tensor("wq1", [D, P], F32, kind="ExternalInput").ap()
    wkv1 = nc.dram_tensor("wkv1", [D, 256], F32, kind="ExternalInput").ap()
    o_xn2 = nc.dram_tensor("xn2fm", [8, P, HT], BF16, kind="ExternalOutput").ap()
    o_qx = nc.dram_tensor("qexp", [8, 3, P, HT], BF16, kind="ExternalOutput").ap()
    o_kvp = nc.dram_tensor("kvp", [8, 384, 17], F32, kind="ExternalOutput").ap()
    ident = nc.inline_tensor(_ident_np(), name="ident").ap()
    id32 = nc.inline_tensor(np.eye(P, dtype=np.float32), name="id32").ap()
    pa0, pa1, pb = _pa_np()
    c_pa0 = nc.inline_tensor(pa0, name="pa0").ap()
    c_pa1 = nc.inline_tensor(pa1, name="pa1").ap()
    c_pb = nc.inline_tensor(pb, name="pb").ap()
    p = Prog(nc)
    emit_c(p, y_all, wout, x_own, xr)
    p.barrier()
    emit_d(p, xr, xnfm, wfc, lnw, lnb, wproj, o_x1, ident, id32)
    p.barrier()
    emit_e(p, o_x1, gamt, wq1, wkv1, o_xn2, o_qx, o_kvp, ident, c_pa0, c_pa1,
           c_pb)
    return p.build()


def build_l3():
    nc = new_nc()
    qx = nc.dram_tensor("qexp", [8, 3, P, HT], BF16, kind="ExternalInput").ap()
    kvp2 = nc.dram_tensor("kvp2", [2, 8, 384, 17], F32, kind="ExternalInput").ap()
    wo1 = nc.dram_tensor("wout1", [P, D], F32, kind="ExternalInput").ap()
    at1 = nc.dram_tensor("at1", [NT, P, D], BF16, kind="Internal").ap()
    xn2 = nc.dram_tensor("xn2fm", [8, P, HT], BF16, kind="ExternalInput").ap()
    x1 = nc.dram_tensor("x1", [NT, P, D], F32, kind="ExternalInput").ap()
    wg = nc.dram_tensor("wg", [3072, D], I32, kind="ExternalInput").ap()
    wu = nc.dram_tensor("wu", [3072, D], I32, kind="ExternalInput").ap()
    wd = nc.dram_tensor("wd", [D, 3072], I32, kind="ExternalInput").ap()
    o_out = nc.dram_tensor("out", [NT, P, D], F32, kind="ExternalOutput").ap()
    ident = nc.inline_tensor(_ident_np(), name="ident").ap()
    p = Prog(nc)
    emit_f(p, qx, kvp2, wo1, at1, ident)
    p.barrier()
    emit_g(p, xn2, x1, at1, wg, wu, wd, o_out, ident)
    return p.build()


def _bf16(x):
    return np.ascontiguousarray(x, dtype=BF16NP)


def _f32(x):
    return np.ascontiguousarray(x, dtype=np.float32)


def kernel(**inputs):
    x = _f32(inputs["L0_attn_x_bld"])
    wq_full = _f32(inputs["L0_attn_Wq"]).reshape(D, 16 * 64)
    wkvd = _f32(inputs["L0_attn_Wkv_down"])
    wku_full = _f32(inputs["L0_attn_Wk_up"]).reshape(256, 16 * 64)
    wvu_full = _f32(inputs["L0_attn_Wv_up"]).reshape(256, 16 * 64)

    nc1 = build_l1()
    in1 = []
    for core in range(8):
        b, g = core // 2, core % 2
        hsl = slice(g * 512, (g + 1) * 512)
        in1.append({
            "xf": np.ascontiguousarray(x[b]),
            "wq": np.ascontiguousarray(wq_full[:, hsl]),
            "wkvd": wkvd,
            "wku": np.ascontiguousarray(wku_full[:, hsl]),
            "wvu": np.ascontiguousarray(wvu_full[:, hsl]),
        })
    r1 = run_bass_kernel_spmd(nc1, in1, core_ids=list(range(8))).results

    wout = _f32(inputs["L0_attn_Wout"]).reshape(D, D)
    wfc = _f32(inputs["L0_ffn_W_fc"])
    lnw = _f32(inputs["L0_ffn_ln_w"]).reshape(1, 4096)
    lnb = _f32(inputs["L0_ffn_ln_b"]).reshape(1, 4096)
    wproj = _f32(inputs["L0_ffn_W_proj"])
    gamt = _f32(np.asarray(inputs["L1_attn_gamma"], np.float32)
                .reshape(8, P).T)
    wq1 = _f32(inputs["L1_attn_w_q"])
    wkv1 = _f32(inputs["L1_attn_w_kv"])
    nc2 = build_l2()
    in2 = []
    for core in range(8):
        b, g = core // 2, core % 2
        ysl = slice(g * HT, (g + 1) * HT)
        ya = np.concatenate([np.asarray(r1[2 * b]["yfm"])[:, :, ysl],
                             np.asarray(r1[2 * b + 1]["yfm"])[:, :, ysl]], 0)
        in2.append({
            "y_all": np.ascontiguousarray(ya),
            "wout": wout,
            "x_own": np.ascontiguousarray(x[b, g * HT:(g + 1) * HT, :]),
            "xnfm_own": np.ascontiguousarray(
                np.asarray(r1[core]["xnfm"])[:, :, ysl]),
            "wfc": wfc, "lnw": lnw, "lnb": lnb, "wproj": wproj,
            "gamt": gamt, "wq1": wq1, "wkv1": wkv1,
        })
    r2 = run_bass_kernel_spmd(nc2, in2, core_ids=list(range(8))).results

    wo1 = _f32(inputs["L1_attn_w_out"])
    wg = np.ascontiguousarray(np.asarray(inputs["L1_ffn_w_gate"], np.int32))
    wu = np.ascontiguousarray(np.asarray(inputs["L1_ffn_w_up"], np.int32))
    wd = np.ascontiguousarray(np.asarray(inputs["L1_ffn_w_down"], np.int32))
    nc3 = build_l3()
    in3 = []
    for core in range(8):
        kvp2 = np.stack([np.asarray(r2[core]["kvp"]),
                         np.asarray(r2[core ^ 1]["kvp"])])
        in3.append({"qexp": np.asarray(r2[core]["qexp"]), "kvp2": kvp2,
                    "wout1": wo1, "xn2fm": np.asarray(r2[core]["xn2fm"]),
                    "x1": np.asarray(r2[core]["x1"]),
                    "wg": wg, "wu": wu, "wd": wd})
    r3 = run_bass_kernel_spmd(nc3, in3, core_ids=list(range(8))).results

    out = np.empty((B, S, D), np.float32)
    for core in range(8):
        b, g = core // 2, core % 2
        out[b, g * HT:(g + 1) * HT, :] = np.asarray(
            r3[core]["out"], np.float32).reshape(HT, D)
    return out


# version 09
# speedup vs baseline: 1.2917x; 1.1231x over previous
import math
from contextlib import ExitStack

import numpy as np
import ml_dtypes

import concourse.bass as bass
import concourse.mybir as mybir
from concourse.bass_utils import run_bass_kernel_spmd

F32 = mybir.dt.float32
BF16 = mybir.dt.bfloat16
I32 = mybir.dt.int32
AF = mybir.ActivationFunctionType
ALU = mybir.AluOpType
AX = mybir.AxisListType

P = 128
D = 1024
S = 4096
B = 4
HT = 2048
NT = HT // P
NTS = S // P
EXPB = 25.0

BF16NP = ml_dtypes.bfloat16


ARENA_F32 = 53136
SEMS = ("lx", "lg", "ld", "pe", "ve", "se", "vg", "ge", "st")


class Prog:

    ENGINES = ("sync", "tensor", "vector", "scalar", "gpsimd")

    def __init__(self, nc):
        self.nc = nc
        self.ctx = ExitStack()
        self.q = {e: [] for e in self.ENGINES}
        self.cnt = {}
        self._sems = {}
        self._ar = self.ctx.enter_context(
            nc.sbuf_tensor("arena", [P, ARENA_F32], F32)).ap()
        self._ar16 = self._ar.bitcast(BF16)
        self._frees = [(0, ARENA_F32 * 4)]
        self._amap = {}
        self.psA = self.ctx.enter_context(
            nc.psum_tensor("psA", [P, 2048], F32)).ap()
        self.psB = self.ctx.enter_context(
            nc.psum_tensor("psB", [P, 1024], F32)).ap()
        self.psC = self.ctx.enter_context(
            nc.psum_tensor("psC", [P, 512], F32)).ap()
        self.psT = self.ctx.enter_context(
            nc.psum_tensor("psT", [P, 1024], BF16)).ap()
        for s_ in SEMS:
            self.sem(s_)

    def carve(self, name, nbytes):
        nb = (nbytes + 63) // 64 * 64
        for i, (s, e) in enumerate(self._frees):
            if e - s >= nb:
                if s + nb == e:
                    self._frees.pop(i)
                else:
                    self._frees[i] = (s + nb, e)
                self._amap[name] = (s, nb)
                return s
        raise MemoryError(f"arena full: {name} needs {nb}B; free={self._frees}")

    def vf32(self, off, shape):
        assert off % 4 == 0
        n = int(np.prod(shape[1:]))
        v = self._ar[0:shape[0], off // 4:off // 4 + n]
        return self._shape(v, shape)

    def vbf16(self, off, shape):
        assert off % 2 == 0
        n = int(np.prod(shape[1:]))
        v = self._ar16[0:shape[0], off // 2:off // 2 + n]
        return self._shape(v, shape)

    @staticmethod
    def _shape(v, shape):
        if len(shape) <= 2:
            return v
        dims = {f"d{i}": shape[1 + i] for i in range(len(shape) - 1)}
        names = " ".join(f"d{i}" for i in range(len(shape) - 1))
        return v.rearrange(f"p ({names}) -> p {names}", **dims)

    def sb(self, name, shape, dtype):
        es = 2 if dtype == BF16 else 4
        off = self.carve(name, int(np.prod(shape[1:])) * es)
        return self.vbf16(off, shape) if dtype == BF16 else self.vf32(off, shape)

    def free(self, *names):
        for n in names:
            s, nb = self._amap.pop(n)
            self._frees.append((s, s + nb))
        self._frees.sort()
        merged = []
        for s, e in self._frees:
            if merged and merged[-1][1] == s:
                merged[-1] = (merged[-1][0], e)
            else:
                merged.append((s, e))
        self._frees = merged

    def free_prefix(self, pfx):
        self.free(*[n for n in list(self._amap) if n.startswith(pfx)])

    def barrier(self):
        for e in self.ENGINES:
            for s_, h in self._sems.items():
                v = self.cnt[s_]
                if v:
                    self.on(e, lambda eng, h=h, v=v: eng.wait_ge(h, v))

    def sem(self, name):
        h = self.ctx.enter_context(self.nc.semaphore(name))
        self.cnt[name] = 0
        self._sems[name] = h
        return h

    def _h(self, sem):
        return self._sems[sem]

    def on(self, eng, fn):
        self.q[eng].append(fn)

    def inc(self, sem, n=1):
        self.cnt[sem] += n
        return self.cnt[sem]

    def wait(self, eng, sem, val=None):
        v = self.cnt[sem] if val is None else val
        h = self._h(sem)
        self.on(eng, lambda e, h=h, v=v: e.wait_ge(h, v))

    def op(self, eng, fn, sem=None, n=1):
        if sem is None:
            self.on(eng, fn)
            return None
        h = self._h(sem)
        v = self.inc(sem, n)
        self.on(eng, lambda e, fn=fn, h=h, n=n: fn(e).then_inc(h, n))
        return v

    def dma(self, eng, out, in_, sem, **kw):
        return self.op(eng, lambda e, o=out, i=in_: e.dma_start(out=o, in_=i, **kw),
                       sem=sem, n=16)

    def build(self, guard_store_sem=None):
        with self.ctx:
            with self.nc.Block() as block:
                for eng in self.ENGINES:
                    cmds = self.q[eng]
                    if not cmds:
                        continue
                    deco = getattr(block, eng)

                    def body(e, cmds=cmds):
                        for fn in cmds:
                            fn(e)

                    deco(body)
        return self.nc


def new_nc():
    return bass.Bass("TRN2", target_bir_lowering=False, debug=False)


def _ident_np():
    return np.eye(P, dtype=BF16NP)


def emit_a(p, xf, wq, wkvd, wku, wvu, o_xnfm, o_kt, o_qt, o_va, ident):
    xb = [p.sb(f"a:xb{i}", [P, D], F32) for i in range(2)]
    xn16 = [p.sb(f"a:xn16{i}", [P, D], BF16) for i in range(2)]
    xnfm = p.sb("a:xnfm_sb", [P, 8, S], BF16)
    lat = p.sb("a:lat_sb", [P, 2, S], BF16)
    qt_sb = p.sb("a:qt_sb", [P, 4, S], BF16)
    kstg = [p.sb(f"a:kstg{i}", [P, S], BF16) for i in range(2)]
    vstg = [p.sb(f"a:vstg{i}", [P, 520], BF16) for i in range(2)]
    wq_sb = p.sb("a:wq_sb", [P, 8, 512], BF16)
    wkvd_sb = p.sb("a:wkvd_sb", [P, 8, 256], BF16)
    wku_sb = p.sb("a:wku_sb", [P, 2, 512], BF16)
    wvu_sb = p.sb("a:wvu_sb", [P, 2, 512], BF16)
    id_sb = p.sb("a:id_sb", [P, P], BF16)
    stt = p.sb("a:stt_sb", [P, 8], F32)
    sqscr = p.sb("a:sqscr", [P, D], F32)
    epsc = p.sb("a:epsc", [P, 1], F32)
    ps_tr = p.psT
    ps_mm = [p.psA[:, i * 512:(i + 1) * 512] for i in range(2)]

    p.dma("gpsimd", wq_sb, wq.rearrange("(a p) k -> p a k", p=P), "lg")
    p.dma("gpsimd", wkvd_sb, wkvd.rearrange("(a p) k -> p a k", p=P), "lg")
    p.dma("gpsimd", wku_sb, wku.rearrange("(a p) k -> p a k", p=P), "lg")
    p.dma("gpsimd", wvu_sb, wvu.rearrange("(a p) k -> p a k", p=P), "lg")
    p.dma("sync", id_sb, ident, "lx")
    p.op("gpsimd", lambda e: e.memset(epsc, 1e-5), sem="ve")
    p.op("gpsimd", lambda e: e.memset(vstg[0][:, :], 1.0), sem="ve")
    p.op("gpsimd", lambda e: e.memset(vstg[1][:, :], 1.0), sem="ve")
    wload = p.cnt["lg"]

    c = lambda i: stt[:, i:i + 1]
    for i in range(NTS):
        buf = xb[i % 2]
        xn = xn16[i % 2]
        if i >= 2:
            p.wait("sync", "se", None)
        p.dma("sync", buf, xf[i * P:(i + 1) * P, :], "lx")
        ldv = p.cnt["lx"]
        p.wait("vector", "lx", ldv)
        if i > 0:
            p.wait("vector", "se", None)
        p.op("vector", lambda e, b=buf, o=c(0): e.reduce_sum(o, b, axis=AX.X),
             sem="vg")
        p.wait("scalar", "lx", ldv)
        p.wait("scalar", "vg", None)
        p.op("scalar", lambda e, b=buf, o=c(1): e.activation(
            out=sqscr, in_=b, func=AF.Square, accum_out=o), sem="se")
        p.wait("vector", "se", None)
        p.op("vector", lambda e, o=c(2), a=c(0): e.tensor_scalar(
            out=o, in0=a, scalar1=1.0 / D, scalar2=None, op0=ALU.mult), sem="vg")
        p.wait("vector", "vg", None)
        p.op("vector", lambda e, o=c(3), a=c(1): e.tensor_scalar(
            out=o, in0=a, scalar1=1.0 / D, scalar2=None, op0=ALU.mult), sem="vg")
        p.wait("vector", "vg", None)
        p.op("vector", lambda e, o=c(4), a=c(2): e.tensor_tensor(
            o, a, a, ALU.mult), sem="vg")
        p.wait("vector", "vg", None)
        p.op("vector", lambda e, o=c(5), a=c(3), b2=c(4): e.tensor_tensor(
            o, a, b2, ALU.subtract), sem="vg")
        p.wait("scalar", "vg", None)
        p.op("scalar", lambda e, o=c(6), a=c(5): e.activation(
            out=o, in_=a, func=AF.Sqrt, bias=epsc[:, 0:1]), sem="se")
        p.wait("vector", "se", None)
        p.op("vector", lambda e, o=c(6), a=c(6): e.reciprocal(o, a), sem="vg")
        p.wait("vector", "vg", None)
        p.op("vector", lambda e, o=c(7), a=c(2), b2=c(6): e.scalar_tensor_tensor(
            out=o, in0=a, scalar=-1.0, in1=b2, op0=ALU.mult, op1=ALU.mult),
            sem="vg")
        p.wait("scalar", "vg", None)
        if i >= 2:
            p.wait("scalar", "pe", p.cnt["pe"] - 1)
        p.op("scalar", lambda e, b=buf, o=xn: e.activation(
            out=o, in_=b, func=AF.Identity, scale=c(6), bias=c(7)), sem="se")
        p.wait("tensor", "se", None)
        if i > 0:
            p.wait("tensor", "ve", None)
        for j in range(8):
            p.op("tensor", lambda e, xnj=xn[:, j * P:(j + 1) * P],
                 d=ps_tr[:, j * P:(j + 1) * P]: e.transpose(d, xnj, id_sb),
                 sem="pe" if j == 7 else None)
        p.wait("vector", "pe", None)
        p.op("vector", lambda e, i=i: e.tensor_copy(
            xnfm[:, :, i * P:(i + 1) * P],
            ps_tr.rearrange("p (a b) -> p a b", a=8)), sem="ve")
    p.wait("gpsimd", "ve", None)
    for a in range(8):
        p.dma("gpsimd", o_xnfm[a], xnfm[:, a, :], "st")

    mmctr = {"n": 0}

    def mm_block(chunks, dest_fn, evac_eng="vector"):
        k = mmctr["n"]
        buf = ps_mm[k % 2]
        if k >= 2:
            p.wait("tensor", "ve", mmctr[f"evac{k - 2}"])
        for ci, (lh, rh) in enumerate(chunks):
            p.op("tensor", lambda e, lh=lh, rh=rh, st=(ci == 0),
                 sp=(ci == len(chunks) - 1), b=buf: e.matmul(
                     b, lh, rh, start=st, stop=sp),
                 sem="pe" if ci == len(chunks) - 1 else None)
        p.wait(evac_eng, "pe", None)
        p.op(evac_eng, lambda e, b=buf: dest_fn(e, b), sem="ve")
        mmctr[f"evac{k}"] = p.cnt["ve"]
        mmctr["n"] += 1

    p.wait("tensor", "lg", wload)
    p.wait("tensor", "ve", None)
    for c2 in range(2):
        for sl in range(8):
            chunks = [(wkvd_sb[:, dk, c2 * P:(c2 + 1) * P],
                       xnfm[:, dk, sl * 512:(sl + 1) * 512]) for dk in range(8)]
            mm_block(chunks, lambda e, b, c2=c2, sl=sl: e.tensor_copy(
                lat[:, c2, sl * 512:(sl + 1) * 512], b))

    for qc in range(4):
        for sl in range(8):
            chunks = [(wq_sb[:, dk, qc * P:(qc + 1) * P],
                       xnfm[:, dk, sl * 512:(sl + 1) * 512]) for dk in range(8)]
            mm_block(chunks, lambda e, b, qc=qc, sl=sl: e.tensor_copy(
                qt_sb[:, qc, sl * 512:(sl + 1) * 512], b))
    p.wait("gpsimd", "ve", None)
    for qc in range(4):
        p.dma("gpsimd", o_qt[qc], qt_sb[:, qc, :], "st")

    for kc in range(4):
        stg = kstg[kc % 2]
        if kc >= 2:
            p.wait("tensor", "st", None)
        for sl in range(8):
            chunks = [(wku_sb[:, c2, kc * P:(kc + 1) * P],
                       lat[:, c2, sl * 512:(sl + 1) * 512]) for c2 in range(2)]
            mm_block(chunks, lambda e, b, sl=sl, stg=stg: e.tensor_copy(
                stg[:, sl * 512:(sl + 1) * 512], b))
        p.wait("gpsimd", "ve", None)
        p.dma("gpsimd", o_kt[kc], stg, "st")

    for kb in range(32):
        stg = vstg[kb % 2]
        if kb >= 2:
            p.wait("vector", "st", None)
        chunks = [(lat[:, c2, kb * P:(kb + 1) * P], wvu_sb[:, c2, :])
                  for c2 in range(2)]
        mm_block(chunks, lambda e, b, stg=stg: e.tensor_copy(
            stg.rearrange("p (h v) -> p h v", h=8)[:, :, 0:64],
            b.rearrange("p (h v) -> p h v", h=8)))
        p.wait("gpsimd", "ve", None)
        p.dma("gpsimd", o_va[kb], stg, "st")

    p.wait("gpsimd", "st", None)
    p.free_prefix("a:")


def _mask_np():
    u = np.arange(896)[None, :] - 384
    r = np.arange(P)[:, None]
    return (u >= r).astype(BF16NP)


def emit_b(p, kt, qt, va, o_y, maskc, bden):
    kt_sb = p.sb("b:kt_sb", [P, 4, S], BF16)
    qt_sb = p.sb("b:qt_sb", [P, 4, S], BF16)
    va_sb = p.sb("b:va_sb", [P, 32, 8, 65], BF16)
    mk_sb = p.sb("b:mk_sb", [P, 896], BF16)
    ptb = [p.sb(f"b:ptb{i}", [P, 1024], BF16) for i in range(3)]
    ystg = [p.sb(f"b:ystg{i}", [P, S], BF16) for i in range(2)]
    rcp16 = [p.sb(f"b:rcp16_{i}", [1, 512], BF16) for i in range(2)]
    den16 = [p.sb(f"b:den16_{i}", [64, 512], BF16) for i in range(2)]
    expb = p.sb("b:expb", [P, 1], F32)
    ps_sg = [p.psA[:, 0:1024], p.psA[:, 1024:2048], p.psB]
    ps_y = [p.psC, p.psT.bitcast(F32)]

    p.dma("sync", kt_sb, kt.rearrange("a p t -> p a t"), "ld")
    p.dma("sync", qt_sb, qt.rearrange("a p t -> p a t"), "ld")
    p.dma("sync", va_sb.rearrange("p b h v -> p b (h v)"),
          va.rearrange("b p x -> p b x"), "ld")
    p.dma("sync", mk_sb, maskc, "ld")
    p.op("gpsimd", lambda e: e.memset(expb, -EXPB), sem="ve")
    ldall = p.cnt["ld"]
    p.wait("tensor", "ld", ldall)
    p.wait("tensor", "ve", None)
    p.wait("scalar", "ld", ldall)
    p.wait("scalar", "ve", None)
    p.wait("vector", "ld", ldall)

    ynorm_hist = {}
    dma_hist = {}
    exp_hist = {}
    avc_hist = {}

    misc = {}

    def emit_st(cc, rb, q0, g):
        if exp_hist.get(g % 3) is not None:
            p.wait("tensor", "se", exp_hist[g % 3])
        for j in range(2):
            kb = 2 * g + j
            p.op("tensor", lambda e, cc=cc, rb=rb, kb=kb, q0=q0, g=g, j=j:
                 e.matmul(ps_sg[g % 3][:, j * 512:(j + 1) * 512],
                          kt_sb[rb:rb + 64, cc, kb * P:(kb + 1) * P],
                          qt_sb[rb:rb + 64, cc, q0:q0 + 512],
                          start=True, stop=True, skip_group_check=True),
                 sem="pe" if j == 1 else None)
        return p.cnt["pe"]

    def emit_evac(rec):
        h, sl, cc, rb, q0 = rec["h"], rec["sl"], rec["cc"], rec["rb"], rec["q0"]
        idx = rec["idx"]
        par = idx % 2
        yb = ps_y[sl % 2]
        p.wait("vector", "pe", rec["av"])
        if ("rcpdma", idx - 2) in misc:
            p.wait("vector", "st", misc[("rcpdma", idx - 2)])

        def _recip_lp(e, yb=yb, par=par):
            with e.bass.allow_low_precision(
                    reason="softmax denom reciprocal; common scale cancels"):
                return e.reciprocal(rcp16[par], yb[64:65, :])

        p.op("vector", _recip_lp, sem="ve")
        rc = p.cnt["ve"]
        p.wait("gpsimd", "ve", rc)
        if idx >= 2:
            p.wait("gpsimd", "ve", ynorm_hist[idx - 2])
        p.dma("gpsimd", bden[par], rcp16[par], "st")
        misc[("rcpdma", idx)] = p.cnt["st"]
        p.wait("gpsimd", "st", p.cnt["st"])
        p.dma("gpsimd", den16[par], bden[par].broadcast_to([64, 512]), "st")
        bcst = p.cnt["st"]
        if cc >= 2 and (cc - 2) in dma_hist:
            p.wait("vector", "st", dma_hist[cc - 2])
        p.wait("vector", "st", bcst)
        p.op("vector", lambda e, yb=yb, rb=rb, q0=q0, cc=cc, par=par:
             e.tensor_tensor(ystg[cc % 2][rb:rb + 64, q0:q0 + 512],
                             yb[0:64, :], den16[par], ALU.mult), sem="ve")
        ynorm_hist[idx] = p.cnt["ve"]
        if h % 2 == 1 and sl == 7:
            p.wait("gpsimd", "ve", None)
            p.dma("gpsimd", o_y[cc], ystg[cc % 2], "st")
            dma_hist[cc] = p.cnt["st"]

    prev_rec = None
    for idx, (h, sl) in enumerate([(h, sl) for h in range(8)
                                   for sl in range(8)]):
        cc, rb, q0 = h // 2, (h % 2) * 64, sl * 512
        ngr = 2 * (sl + 1)
        st_pe = {0: emit_st(cc, rb, q0, 0)}
        if ngr > 1:
            st_pe[1] = emit_st(cc, rb, q0, 1)
        if prev_rec is not None:
            emit_evac(prev_rec)
        for g in range(ngr):
            if g + 2 < ngr:
                st_pe[g + 2] = emit_st(cc, rb, q0, g + 2)
            w = st_pe[g]
            if avc_hist.get(g % 3) is not None:
                w = max(w, avc_hist[g % 3])
            p.wait("scalar", "pe", w)
            p.op("scalar", lambda e, g=g: e.activation(
                out=ptb[g % 3], in_=ps_sg[g % 3], func=AF.Exp,
                scale=0.125, bias=expb[:, 0:1]), sem="se")
            exp_hist[g % 3] = p.cnt["se"]
            nmask = 0
            for j in range(2):
                kb = 2 * g + j
                delta = kb * P - q0
                if 0 <= delta <= 384:
                    off = 384 - delta
                    p.wait("vector", "se", exp_hist[g % 3])
                    p.op("vector", lambda e, g=g, j=j, off=off:
                         e.tensor_tensor(
                             ptb[g % 3][:, j * 512:(j + 1) * 512],
                             ptb[g % 3][:, j * 512:(j + 1) * 512],
                             mk_sb[:, off:off + 512], ALU.mult), sem="ve")
                    nmask += 1
            if nmask:
                p.wait("tensor", "ve", p.cnt["ve"])
            else:
                p.wait("tensor", "se", exp_hist[g % 3])
            if g == 0 and idx >= 2:
                p.wait("tensor", "ve", ynorm_hist[idx - 2])
            for j in range(2):
                kb = 2 * g + j
                p.op("tensor", lambda e, kb=kb, h=h, g=g, j=j, sl=sl,
                     st=(kb == 0), sp=(kb == ngr * 2 - 1): e.matmul(
                         ps_y[sl % 2][0:65, :], va_sb[:, kb, h, :],
                         ptb[g % 3][:, j * 512:(j + 1) * 512],
                         start=st, stop=sp, skip_group_check=True),
                     sem="pe" if j == 1 else None)
            avc_hist[g % 3] = p.cnt["pe"]
        prev_rec = dict(h=h, sl=sl, cc=cc, rb=rb, q0=q0, av=p.cnt["pe"],
                        idx=idx)
    emit_evac(prev_rec)

    p.wait("gpsimd", "st", None)
    p.free_prefix("b:")


def emit_c(p, y_all, wout, x_own, o_xr):
    y_sb = p.sb("c:y_sb", [P, 8, HT], BF16)
    wo_sb = p.sb("c:wo_sb", [P, 8, D], BF16)
    xb = [p.sb(f"c:xb{i}", [P, D], F32) for i in range(2)]
    stg = [p.sb(f"c:stg{i}", [P, D], F32) for i in range(2)]
    ps = [p.psA[:, i * 1024:(i + 1) * 1024] for i in range(2)]

    p.dma("sync", y_sb, y_all.rearrange("a p t -> p a t"), "lx")
    p.dma("gpsimd", wo_sb, wout.rearrange("(a p) k -> p a k", p=P), "lg")
    evac_hist = {}
    for t in range(NT):
        xbuf = xb[t % 2]
        if t >= 2:
            p.wait("sync", "ve", evac_hist[t - 2])
        p.dma("sync", xbuf, x_own[t * P:(t + 1) * P, :], "lx")
        ldv = p.cnt["lx"]
        pb = ps[t % 2]
        p.wait("tensor", "lx", 16)
        p.wait("tensor", "lg", 16)
        if t >= 2:
            p.wait("tensor", "ve", evac_hist[t - 2])
        for ic in range(8):
            for half in range(2):
                p.op("tensor", lambda e, ic=ic, half=half, t=t, pb=pb: e.matmul(
                    pb[:, half * 512:(half + 1) * 512],
                    y_sb[:, ic, t * P:(t + 1) * P],
                    wo_sb[:, ic, half * 512:(half + 1) * 512],
                    start=(ic == 0), stop=(ic == 7), skip_group_check=True),
                    sem="pe" if (ic == 7 and half == 1) else None)
        p.wait("vector", "lx", ldv)
        p.wait("vector", "pe", None)
        if t >= 2:
            p.wait("vector", "st", None)
        p.op("vector", lambda e, t=t, pb=pb, xbuf=xbuf: e.tensor_add(
            stg[t % 2], pb, xbuf), sem="ve")
        evac_hist[t] = p.cnt["ve"]
        p.wait("gpsimd", "ve", None)
        p.dma("gpsimd", o_xr[t], stg[t % 2], "st")
    p.wait("gpsimd", "st", None)
    p.free_prefix("c:")


def emit_d(p, xr, xnfm, wfc, lnw, lnb, wproj, o_x1, ident, id32):
    wfcT = p.sb("d:wfcT", [P, 8, 4096], BF16)
    wprT = p.sb("d:wprT", [P, 32, D], BF16)
    dof = p.carve("d:arena", 53248)
    arena = p.vf32(dof, [P, 13312])
    w3s = [arena[:, 0:4096], arena[:, 4096:8192]]
    sgna = p.vbf16(dof + 32768, [P, 4096])
    sgnb = p.vbf16(dof + 40960, [P, 4096])
    xnf = [p.vbf16(dof, [P, 8, P]),
           p.vbf16(dof + 2048, [P, 8, P])]
    h_sb = p.vbf16(dof + 4096, [P, 4096])
    hT = p.vbf16(dof + 12288, [P, 32, P])
    lnwbc = p.vbf16(dof + 20480, [P, 4096])
    lnbbc = p.vbf16(dof + 28672, [P, 4096])
    xrb = [arena[:, 9216:10240], arena[:, 10240:11264]]
    x1s = [arena[:, 11264:12288], arena[:, 12288:13312]]
    id_sb = p.sb("d:id_sb", [P, P], BF16)
    id32_sb = p.sb("d:id32_sb", [P, P], F32)
    onesf = p.sb("d:onesf", [1, P], F32)
    ones16 = p.sb("d:ones16", [1, P], BF16)
    lnrow = p.sb("d:lnrow", [1, 8192], BF16)
    stt = p.sb("d:stt_d", [P, 12], F32)
    g11 = p.sb("d:g11", [1, 4], F32)
    thr = p.sb("d:thr_d", [P, 4], F32)
    epsc = p.sb("d:epsc_d", [P, 1], F32)
    sqscr = p.sb("d:sqscr", [P, 1024], F32)
    ps_fc = p.psA
    ps_tr = p.psT
    ps_pr = p.psB
    ps_sc = p.psC[:, 0:128]

    p.dma("sync", id_sb, ident, "lx")
    p.dma("sync", id32_sb, id32, "lx")
    p.op("gpsimd", lambda e: e.memset(onesf, 1.0), sem="ve")
    p.op("gpsimd", lambda e: e.memset(ones16, 1.0), sem="ve")
    p.op("gpsimd", lambda e: e.memset(epsc, 1e-5), sem="ve")
    p.dma("gpsimd", lnrow[0:1, 0:4096], lnw, "lg")
    p.dma("gpsimd", lnrow[0:1, 4096:8192], lnb, "lg")
    cload = p.cnt["lg"]

    def gamma_phase(src_fn, ngr, gcols, tcols):
        for gidx in range(ngr):
            bb = w3s[gidx % 2][:, 0:gcols]
            if gidx >= 2:
                p.wait("sync", "vg", None)
            p.dma("sync", bb.rearrange("p (k d) -> p k d", d=1024),
                  src_fn(gidx), "lx")
            p.wait("vector", "lx", p.cnt["lx"])
            dst = stt[:, 8 + gidx % 2:9 + gidx % 2]
            p.op("vector", lambda e, bb=bb, dst=dst: e.tensor_reduce(
                dst, bb, axis=AX.X, op=ALU.add,
                apply_absolute_value=True), sem="vg")
            p.wait("vector", "vg", None)
            if gidx == 0:
                p.op("vector", lambda e, dst=dst: e.tensor_copy(
                    stt[:, 10:11], dst), sem="vg")
            else:
                p.op("vector", lambda e, dst=dst: e.tensor_add(
                    stt[:, 10:11], stt[:, 10:11], dst), sem="vg")
            p.wait("vector", "vg", None)
        p.wait("tensor", "vg", None)
        p.op("tensor", lambda e: e.transpose(
            ps_sc[0:1, 0:P], stt[:, 10:11], id32_sb), sem="pe")
        p.wait("vector", "pe", None)
        p.op("vector", lambda e: e.reduce_sum(
            g11[0:1, 0:1], ps_sc[0:1, 0:P], axis=AX.X), sem="vg")
        p.wait("vector", "vg", None)
        p.op("vector", lambda e: e.tensor_scalar(
            out=g11[0:1, 1:2], in0=g11[0:1, 0:1], scalar1=1.0 / (4096.0 * D),
            scalar2=1e-6, op0=ALU.mult, op1=ALU.add), sem="vg")
        p.wait("vector", "vg", None)
        p.op("vector", lambda e: e.tensor_scalar(
            out=g11[0:1, 2:3], in0=g11[0:1, 1:2], scalar1=0.5, scalar2=None,
            op0=ALU.mult), sem="vg")
        p.wait("vector", "vg", None)
        p.op("vector", lambda e: e.tensor_scalar(
            out=g11[0:1, 3:4], in0=g11[0:1, 2:3], scalar1=-1.0, scalar2=None,
            op0=ALU.mult), sem="vg")
        p.wait("vector", "vg", None)
        p.wait("tensor", "vg", None)
        p.op("tensor", lambda e: e.matmul(
            ps_sc[:, 0:2], onesf, g11[0:1, 2:4], start=True, stop=True,
            skip_group_check=True), sem="pe")
        p.wait("vector", "pe", None)
        p.op("vector", lambda e, tcols=tcols: e.tensor_copy(
            thr[:, tcols[0]:tcols[1] + 1], ps_sc[:, 0:2]), sem="ve")

    def quant_phase(src_fn, ntile, cols, tpos, tneg, dst_fn):
        rounds = cols // 1024
        for t in range(ntile):
            bb = w3s[t % 2][:, 0:cols]
            if t >= 2:
                p.wait("sync", "se", None)
            p.dma("sync", bb.rearrange("p (k d) -> p k d", d=1024),
                  src_fn(t), "lx")
            p.wait("scalar", "lx", p.cnt["lx"])
            p.wait("scalar", "ve", None)
            if t >= 1:
                p.wait("scalar", "ge", None)
                p.wait("scalar", "pe", None)
            sa_, sb_ = sgna[:, 0:cols], sgnb[:, 0:cols]
            p.op("scalar", lambda e, bb=bb, sa_=sa_, tn=tneg: e.activation(
                out=sa_, in_=bb, func=AF.Sign, bias=thr[:, tn:tn + 1]), sem="se")
            p.op("scalar", lambda e, bb=bb, sb_=sb_, tp=tpos: e.activation(
                out=sb_, in_=bb, func=AF.Sign, bias=thr[:, tp:tp + 1]), sem="se")
            p.wait("vector", "se", None)
            if t >= 1:
                p.wait("vector", "pe", None)
            p.op("vector", lambda e, sa_=sa_, sb_=sb_: e.tensor_tensor(
                sa_, sa_, sb_, ALU.add), sem="ge")
            for r in range(rounds):
                p.wait("tensor", "ge" if r == 0 else "ve", None)
                for j in range(8):
                    p.op("tensor", lambda e, r=r, j=j, sa_=sa_: e.transpose(
                        ps_tr[:, j * P:(j + 1) * P],
                        sa_[:, (r * 8 + j) * P:(r * 8 + j + 1) * P], id_sb),
                        sem="pe" if j == 7 else None)
                p.wait("vector", "pe", None)
                p.op("vector", lambda e, t=t, r=r: e.tensor_copy(
                    dst_fn(t, r), ps_tr.rearrange("p (a b) -> p a b", a=8)),
                    sem="ve")

    wfc_r = wfc.rearrange("(a p) d -> p a d", p=P)
    gamma_phase(lambda g: wfc_r[:, 4 * g:4 * (g + 1), :], 8, 4096, (0, 1))
    quant_phase(lambda t: wfc_r[:, t:t + 1, :], 32, 1024, 0, 1,
                lambda t, r: wfcT[:, :, t * P:(t + 1) * P])
    wpr_r = wproj.rearrange("(a p) h -> p a h", p=P)
    gamma_phase(lambda g: wpr_r[:, g:g + 1, :].rearrange(
        "p a (k d) -> p (a k) d", d=1024), 8, 4096, (2, 3))
    quant_phase(lambda t: wpr_r[:, t:t + 1, :].rearrange(
        "p a (k d) -> p (a k) d", d=1024), 8, 4096, 2, 3,
                lambda t, r: wprT[:, r * 8:(r + 1) * 8, t * P:(t + 1) * P])

    p.wait("vector", "lg", cload)
    p.wait("vector", "se", None)
    p.wait("tensor", "se", None)
    for piece in range(16):
        src = lnrow[0:1, piece * 512:(piece + 1) * 512]
        dst = (lnwbc if piece < 8 else lnbbc)
        doff = (piece % 8) * 512
        p.wait("tensor", "ve", None)
        p.op("tensor", lambda e, src=src: e.matmul(
            ps_sc.bitcast(BF16)[:, 0:512] if False else ps_fc[:, 0:512],
            ones16, src, start=True, stop=True, skip_group_check=True),
            sem="pe")
        p.wait("vector", "pe", None)
        p.op("vector", lambda e, dst=dst, doff=doff: e.tensor_copy(
            dst[:, doff:doff + 512], ps_fc[:, 0:512]), sem="ve")

    c = lambda i: stt[:, i:i + 1]
    f_hist = {}
    p.wait("sync", "pe", None)
    p.wait("sync", "se", None)
    p.wait("sync", "ge", None)
    p.wait("sync", "ve", None)
    def emit_fc_half(t, half, xnb):
        for dj in range(8):
            for sub in range(4):
                p.op("tensor", lambda e, dj=dj, sub=sub, half=half, xnb=xnb:
                     e.matmul(ps_fc[:, sub * 512:(sub + 1) * 512],
                              xnb[:, dj, :],
                              wfcT[:, dj, half * 2048 + sub * 512:
                                   half * 2048 + (sub + 1) * 512],
                              start=(dj == 0), stop=(dj == 7),
                              skip_group_check=True),
                     sem="pe" if (dj == 7 and sub == 3) else None)
        fcv = p.cnt["pe"]
        p.wait("scalar", "pe", fcv)
        for gg in range(2):
            p.op("scalar", lambda e, half=half, gg=gg: e.activation(
                out=h_sb[:, half * 2048 + gg * 1024:
                         half * 2048 + (gg + 1) * 1024],
                in_=ps_fc[:, gg * 1024:(gg + 1) * 1024], func=AF.Gelu,
                scale=0.5,
                accum_out=stt[:, 8 + half * 2 + gg:9 + half * 2 + gg]),
                sem="se")
        return fcv, p.cnt["se"]

    def emit_proj(t):
        p.wait("tensor", "ve", f_hist[(t, "hT")])
        if t >= 2:
            p.wait("tensor", "ve", f_hist[(t - 2, "evac")])
        for hc in range(32):
            for half in range(2):
                p.op("tensor", lambda e, hc=hc, half=half: e.matmul(
                    ps_pr[:, half * 512:(half + 1) * 512], hT[:, hc, :],
                    wprT[:, hc, half * 512:(half + 1) * 512],
                    start=(hc == 0), stop=(hc == 31), skip_group_check=True),
                    sem="pe" if (hc == 31 and half == 1) else None)
        f_hist[(t, "lastmm")] = p.cnt["pe"]
        p.wait("vector", "pe", f_hist[(t, "lastmm")])
        p.wait("vector", "lx", f_hist[(t, "ldv")])
        if t >= 2:
            p.wait("vector", "st", None)
        p.op("vector", lambda e, t=t: e.scalar_tensor_tensor(
            out=x1s[t % 2], in0=ps_pr, scalar=0.5, in1=xrb[t % 2],
            op0=ALU.mult, op1=ALU.add), sem="ve")
        f_hist[(t, "evac")] = p.cnt["ve"]
        p.wait("gpsimd", "ve", None)
        p.dma("gpsimd", o_x1[t], x1s[t % 2], "st")

    for t in range(NT):
        xnb = xnf[t % 2]
        xrbuf = xrb[t % 2]
        if t >= 2:
            p.wait("sync", "pe", f_hist[(t - 2, "fcmm")])
            p.wait("sync", "ve", f_hist[(t - 2, "evac")])
        p.dma("sync", xnb,
              xnfm.rearrange("a p t -> p a t")[:, :, t * P:(t + 1) * P], "lx")
        p.dma("sync", xrbuf, xr[t], "lx")
        ldv = p.cnt["lx"]
        f_hist[(t, "ldv")] = ldv
        p.wait("tensor", "lx", ldv)
        if t >= 1:
            p.wait("tensor", "se", f_hist[(t - 1, "gel1")])
        _, gel0 = emit_fc_half(t, 0, xnb)
        p.wait("tensor", "se", gel0)
        fcmm, gel1 = emit_fc_half(t, 1, xnb)
        f_hist[(t, "fcmm")] = fcmm
        f_hist[(t, "gel1")] = gel1
        if t >= 1:
            emit_proj(t - 1)
        p.wait("vector", "se", None)
        p.op("vector", lambda e: e.reduce_sum(c(0), stt[:, 8:12], axis=AX.X),
             sem="vg")
        p.wait("scalar", "vg", None)
        for gg in range(4):
            p.op("scalar", lambda e, gg=gg: e.activation(
                out=sqscr,
                in_=h_sb[:, gg * 1024:(gg + 1) * 1024], func=AF.Square,
                accum_out=stt[:, 8 + gg:9 + gg]), sem="se")
        f_hist[(t, "sq")] = p.cnt["se"]
        p.wait("vector", "se", None)
        p.op("vector", lambda e: e.reduce_sum(c(1), stt[:, 8:12], axis=AX.X),
             sem="vg")
        p.wait("vector", "vg", None)
        p.op("vector", lambda e: e.tensor_scalar(
            out=c(2), in0=c(0), scalar1=1.0 / 4096, scalar2=None,
            op0=ALU.mult), sem="vg")
        p.wait("vector", "vg", None)
        p.op("vector", lambda e: e.tensor_scalar(
            out=c(3), in0=c(1), scalar1=1.0 / 4096, scalar2=None,
            op0=ALU.mult), sem="vg")
        p.wait("vector", "vg", None)
        p.op("vector", lambda e: e.tensor_tensor(c(4), c(2), c(2), ALU.mult),
             sem="vg")
        p.wait("vector", "vg", None)
        p.op("vector", lambda e: e.tensor_tensor(c(5), c(3), c(4), ALU.subtract),
             sem="vg")
        p.wait("scalar", "vg", None)
        p.op("scalar", lambda e: e.activation(
            out=c(6), in_=c(5), func=AF.Sqrt, bias=epsc[:, 0:1]), sem="se")
        p.wait("vector", "se", None)
        p.op("vector", lambda e: e.reciprocal(c(6), c(6)), sem="vg")
        p.wait("vector", "vg", None)
        p.op("vector", lambda e: e.scalar_tensor_tensor(
            out=c(7), in0=c(2), scalar=-1.0, in1=c(6), op0=ALU.mult,
            op1=ALU.mult), sem="vg")
        p.wait("scalar", "vg", None)
        p.op("scalar", lambda e: e.activation(
            out=h_sb, in_=h_sb, func=AF.Identity, scale=c(6), bias=c(7)),
            sem="se")
        p.wait("vector", "se", None)
        p.op("vector", lambda e: e.tensor_tensor(h_sb, h_sb, lnwbc, ALU.mult),
             sem="ve")
        p.op("vector", lambda e: e.tensor_add(h_sb, h_sb, lnbbc), sem="ve")
        for r in range(4):
            p.wait("tensor", "ve", None)
            for j in range(8):
                p.op("tensor", lambda e, r=r, j=j: e.transpose(
                    ps_tr[:, j * P:(j + 1) * P],
                    h_sb[:, (r * 8 + j) * P:(r * 8 + j + 1) * P], id_sb),
                    sem="pe" if j == 7 else None)
            p.wait("vector", "pe", None)
            p.op("vector", lambda e, r=r: e.tensor_copy(
                hT[:, r * 8:(r + 1) * 8, :],
                ps_tr.rearrange("p (a b) -> p a b", a=8)), sem="ve")
        f_hist[(t, "hT")] = p.cnt["ve"]
    emit_proj(NT - 1)
    p.wait("gpsimd", "st", None)
    p.free_prefix("d:")


def _pa_np():
    pa0 = np.zeros((8, P, P), BF16NP)
    pa1 = np.zeros((8, P, P), BF16NP)
    pb = np.zeros((8, P, P), BF16NP)
    for h in range(8):
        for a in range(P):
            pa0[h, 16 * h + a // 16, a] = 1
            pa1[h, 16 * h + 8 + a // 16, a] = 1
            pb[h, 16 * h + a % 16, a] = 1
    return pa0, pa1, pb


def emit_e(p, x1, gamt, wq1, wkv1, o_xn2, o_qx, o_kvp, ident, c_pa0, c_pa1,
           c_pb):
    xb = [p.sb(f"e:xb{i}", [P, D], F32) for i in range(2)]
    xn16 = [p.sb(f"e:xn16{i}", [P, D], BF16) for i in range(2)]
    nr16 = [p.sb(f"e:nr16{i}", [P, D], BF16) for i in range(2)]
    xn2fm = p.sb("e:xn2fm_sb", [P, 8, HT], BF16)
    nrmfm = p.sb("e:nrmfm_sb", [P, 8, HT], BF16)
    q_sb = p.sb("e:q_sb", [P, HT], BF16)
    k_sb = p.sb("e:k_sb", [P, HT], BF16)
    v_sb = p.sb("e:v_sb", [P, HT], BF16)
    k_tm = p.sb("e:k_tm", [P, NT, P], BF16)
    vaug = p.sb("e:vaug_e", [P, NT, 8, 17], BF16)
    kexp = p.sb("e:kexp", [P, NT, 8, 273], BF16)
    qc0 = [p.sb(f"e:qc0_{i}", [P, HT], BF16) for i in range(2)]
    qx = [p.sb(f"e:qx_{i}", [P, HT], BF16) for i in range(2)]
    kvp_sb = p.sb("e:kvp_sb", [P, 8, 3, 17], F32)
    ab_scr = p.sb("e:ab_scr", [P, 512], BF16)
    wq1_sb = p.sb("e:wq1_sb", [P, 8, P], BF16)
    wkv1_sb = p.sb("e:wkv1_sb", [P, 8, 256], BF16)
    gam_sb = p.sb("e:gam_sb", [P, 8], F32)
    pa0_sb = p.sb("e:pa0_sb", [P, 8, P], BF16)
    pa1_sb = p.sb("e:pa1_sb", [P, 8, P], BF16)
    pb_sb = p.sb("e:pb_sb", [P, 8, P], BF16)
    id_sb = p.sb("e:id_sb_e", [P, P], BF16)
    stt = p.sb("e:stt_e", [P, 16], F32)
    sqscr = p.sb("e:sqscr_e", [P, D], F32)
    epsc = p.sb("e:epsc_e", [P, 1], F32)
    ps_tr = p.psT
    ps_mm = [p.psA[:, i * 512:(i + 1) * 512] for i in range(2)]
    ps_ab = [p.psA[:, 1024 + i * 512:1024 + (i + 1) * 512] for i in range(2)]

    p.dma("sync", id_sb, ident, "lx")
    p.dma("gpsimd", wq1_sb, wq1.rearrange("(a p) k -> p a k", p=P), "lg")
    p.dma("gpsimd", wkv1_sb, wkv1.rearrange("(a p) k -> p a k", p=P), "lg")
    p.dma("gpsimd", gam_sb, gamt, "lg")
    p.dma("gpsimd", pa0_sb, c_pa0.rearrange("a p k -> p a k"), "lg")
    p.dma("gpsimd", pa1_sb, c_pa1.rearrange("a p k -> p a k"), "lg")
    p.dma("gpsimd", pb_sb, c_pb.rearrange("a p k -> p a k"), "lg")
    p.op("gpsimd", lambda e: e.memset(epsc, 1e-5), sem="ve")
    p.op("gpsimd", lambda e: e.memset(vaug[:, :, :, 16:17], 1.0), sem="ve")
    p.op("gpsimd", lambda e: e.memset(kexp[:, :, :, 0:1], 1.0), sem="ve")
    p.op("gpsimd", lambda e: e.memset(qc0[0], 0.0), sem="ve")
    p.op("gpsimd", lambda e: e.memset(qc0[1], 0.0), sem="ve")
    p.op("gpsimd", lambda e: e.memset(qc0[0][0:1, :], 1.0), sem="ve")
    p.op("gpsimd", lambda e: e.memset(qc0[1][0:1, :], 1.0), sem="ve")
    p.op("gpsimd", lambda e: e.memset(kvp_sb.rearrange("p a b c -> p (a b c)"),
                                      0.0), sem="ve")
    wload = p.cnt["lg"]
    p.wait("vector", "lg", wload)
    p.op("vector", lambda e: e.tensor_tensor(
        wq1_sb, wq1_sb, gam_sb[:, :, None].broadcast_to([P, 8, P]),
        ALU.mult), sem="vg")
    p.wait("vector", "vg", None)
    p.op("vector", lambda e: e.tensor_tensor(
        wkv1_sb, wkv1_sb, gam_sb[:, :, None].broadcast_to([P, 8, 256]),
        ALU.mult), sem="vg")
    gfold = p.cnt["vg"]

    c = lambda i: stt[:, i:i + 1]
    for i in range(NT):
        buf, xn, nr = xb[i % 2], xn16[i % 2], nr16[i % 2]
        if i >= 2:
            p.wait("sync", "se", None)
        p.dma("sync", buf, x1[i], "lx")
        ldv = p.cnt["lx"]
        p.wait("vector", "lx", ldv)
        if i > 0:
            p.wait("vector", "se", None)
        p.op("vector", lambda e, b=buf: e.reduce_sum(c(0), b, axis=AX.X),
             sem="vg")
        p.wait("scalar", "lx", ldv)
        p.wait("scalar", "vg", None)
        p.op("scalar", lambda e, b=buf: e.activation(
            out=sqscr, in_=b, func=AF.Square, accum_out=c(1)), sem="se")
        p.wait("vector", "se", None)
        p.op("vector", lambda e: e.tensor_scalar(
            out=c(2), in0=c(0), scalar1=1.0 / D, scalar2=None, op0=ALU.mult),
            sem="vg")
        p.wait("vector", "vg", None)
        p.op("vector", lambda e: e.tensor_scalar(
            out=c(3), in0=c(1), scalar1=1.0 / D, scalar2=None, op0=ALU.mult),
            sem="vg")
        p.wait("vector", "vg", None)
        p.op("vector", lambda e: e.tensor_tensor(c(4), c(2), c(2), ALU.mult),
             sem="vg")
        p.wait("vector", "vg", None)
        p.op("vector", lambda e: e.tensor_tensor(c(5), c(3), c(4), ALU.subtract),
             sem="vg")
        p.wait("scalar", "vg", None)
        p.op("scalar", lambda e: e.activation(
            out=c(6), in_=c(5), func=AF.Sqrt, bias=epsc[:, 0:1]), sem="se")
        p.wait("vector", "se", None)
        p.op("vector", lambda e: e.reciprocal(c(6), c(6)), sem="vg")
        p.wait("vector", "vg", None)
        p.op("vector", lambda e: e.scalar_tensor_tensor(
            out=c(7), in0=c(2), scalar=-1.0, in1=c(6), op0=ALU.mult,
            op1=ALU.mult), sem="vg")
        p.wait("vector", "vg", None)
        p.op("vector", lambda e: e.tensor_scalar(
            out=c(8), in0=c(5), scalar1=1e-5, scalar2=None, op0=ALU.add),
            sem="vg")
        p.wait("vector", "vg", None)
        p.op("vector", lambda e: e.reciprocal(c(8), c(8)), sem="vg")
        p.wait("vector", "vg", None)
        p.op("vector", lambda e: e.tensor_tensor(c(9), c(5), c(8), ALU.mult),
             sem="vg")
        p.wait("vector", "vg", None)
        p.op("vector", lambda e: e.tensor_scalar(
            out=c(9), in0=c(9), scalar1=9.765625e-28, scalar2=None,
            op0=ALU.max), sem="vg")
        p.wait("scalar", "vg", None)
        p.op("scalar", lambda e: e.activation(
            out=c(10), in_=c(9), func=AF.Sqrt), sem="se")
        p.wait("vector", "se", None)
        p.op("vector", lambda e: e.reciprocal(c(10), c(10)), sem="vg")
        p.wait("vector", "vg", None)
        p.op("vector", lambda e: e.tensor_tensor(c(11), c(6), c(10), ALU.mult),
             sem="vg")
        p.wait("vector", "vg", None)
        p.op("vector", lambda e: e.tensor_tensor(c(12), c(7), c(10), ALU.mult),
             sem="vg")
        p.wait("scalar", "vg", None)
        if i >= 2:
            p.wait("scalar", "pe", p.cnt["pe"] - 2)
        p.op("scalar", lambda e, b=buf, o=xn: e.activation(
            out=o, in_=b, func=AF.Identity, scale=c(6), bias=c(7)), sem="se")
        p.op("scalar", lambda e, b=buf, o=nr: e.activation(
            out=o, in_=b, func=AF.Identity, scale=c(11), bias=c(12)), sem="se")
        p.wait("tensor", "se", None)
        if i > 0:
            p.wait("tensor", "ve", None)
        for j in range(8):
            p.op("tensor", lambda e, xn=xn, j=j: e.transpose(
                ps_tr[:, j * P:(j + 1) * P], xn[:, j * P:(j + 1) * P], id_sb),
                sem="pe" if j == 7 else None)
        p.wait("vector", "pe", None)
        p.op("vector", lambda e, i=i: e.tensor_copy(
            xn2fm[:, :, i * P:(i + 1) * P],
            ps_tr.rearrange("p (a b) -> p a b", a=8)), sem="ve")
        p.wait("tensor", "ve", None)
        for j in range(8):
            p.op("tensor", lambda e, nr=nr, j=j: e.transpose(
                ps_tr[:, j * P:(j + 1) * P], nr[:, j * P:(j + 1) * P], id_sb),
                sem="pe" if j == 7 else None)
        p.wait("vector", "pe", None)
        p.op("vector", lambda e, i=i: e.tensor_copy(
            nrmfm[:, :, i * P:(i + 1) * P],
            ps_tr.rearrange("p (a b) -> p a b", a=8)), sem="ve")
    p.wait("gpsimd", "ve", None)
    for a in range(8):
        p.dma("gpsimd", o_xn2[a], xn2fm[:, a, :], "st")

    mmctr = {"n": 0}

    def mm_block(chunks, dest_fn, evac_eng="vector"):
        k = mmctr["n"]
        buf = ps_mm[k % 2]
        if k >= 2:
            p.wait("tensor", "ve", mmctr[f"evac{k - 2}"])
        for ci, (lh, rh) in enumerate(chunks):
            p.op("tensor", lambda e, lh=lh, rh=rh, st=(ci == 0),
                 sp=(ci == len(chunks) - 1), b=buf: e.matmul(
                     b, lh, rh, start=st, stop=sp, skip_group_check=True),
                 sem="pe" if ci == len(chunks) - 1 else None)
        p.wait(evac_eng, "pe", None)
        p.op(evac_eng, lambda e, b=buf: dest_fn(e, b), sem="ve")
        mmctr[f"evac{k}"] = p.cnt["ve"]
        mmctr["n"] += 1

    p.wait("tensor", "vg", gfold)
    p.wait("tensor", "ve", None)
    for sl in range(4):
        chunks = [(wq1_sb[:, dj, :], nrmfm[:, dj, sl * 512:(sl + 1) * 512])
                  for dj in range(8)]
        mm_block(chunks, lambda e, b, sl=sl: e.activation(
            out=q_sb[:, sl * 512:(sl + 1) * 512], in_=b, func=AF.Copy,
            scale=0.25), evac_eng="scalar")
    for c2 in range(2):
        dst = k_sb if c2 == 0 else v_sb
        for sl in range(4):
            chunks = [(wkv1_sb[:, dj, c2 * P:(c2 + 1) * P],
                       nrmfm[:, dj, sl * 512:(sl + 1) * 512])
                      for dj in range(8)]
            mm_block(chunks, lambda e, b, dst=dst, sl=sl: e.tensor_copy(
                dst[:, sl * 512:(sl + 1) * 512], b))

    p.wait("tensor", "ve", None)
    p.wait("tensor", "se", None)
    trcp_hist = {}
    for t in range(NT):
        if t >= 1:
            p.wait("tensor", "ve", trcp_hist[t - 1])
        p.op("tensor", lambda e, t=t: e.transpose(
            ps_tr[:, 0:P], k_sb[:, t * P:(t + 1) * P], id_sb), sem=None)
        p.op("tensor", lambda e, t=t: e.transpose(
            ps_tr[:, P:2 * P], v_sb[:, t * P:(t + 1) * P], id_sb), sem="pe")
        p.wait("vector", "pe", None)
        p.op("vector", lambda e, t=t: e.tensor_copy(
            k_tm[:, t, :], ps_tr[:, 0:P]), sem="ve")
        p.op("vector", lambda e, t=t: e.tensor_copy(
            vaug[:, t, :, 0:16],
            ps_tr[:, P:2 * P].rearrange("p (h v) -> p h v", h=8)), sem="ve")
        trcp_hist[t] = p.cnt["ve"]
        p.wait("vector", "ve", None)
        p.op("vector", lambda e, t=t: e.tensor_copy(
            kexp[:, t, :, 1:17],
            k_tm[:, t, :].rearrange("p (h v) -> p h v", h=8)), sem="ve")
        for h in range(8):
            p.op("vector", lambda e, t=t, h=h: e.scalar_tensor_tensor(
                out=kexp[:, t, h, 17:273].rearrange("p (a b) -> p a b", a=16),
                in0=k_tm[:, t, h * 16:(h + 1) * 16][:, :, None]
                .broadcast_to([P, 16, 16]),
                scalar=0.7071067811865476,
                in1=k_tm[:, t, None, h * 16:(h + 1) * 16]
                .broadcast_to([P, 16, 16]),
                op0=ALU.mult, op1=ALU.mult),
                sem="ve" if h == 7 else None)
    kexp_done = p.cnt["ve"]

    csl = [(0, 17), (17, 145), (145, 273)]
    p.wait("tensor", "ve", kexp_done)
    grp = 0
    for h in range(8):
        for ci, (c0_, c1_) in enumerate(csl):
            buf = ps_mm[grp % 2]
            if grp >= 2:
                p.wait("tensor", "ve", mmctr[f"kv{grp - 2}"])
            for t in range(NT):
                p.op("tensor", lambda e, t=t, h=h, c0_=c0_, c1_=c1_, buf=buf:
                     e.matmul(buf[0:c1_ - c0_, 0:17],
                              kexp[:, t, h, c0_:c1_], vaug[:, t, h, :],
                              start=(t == 0), stop=(t == NT - 1),
                              skip_group_check=True),
                     sem="pe" if t == NT - 1 else None)
            p.wait("vector", "pe", None)
            p.op("vector", lambda e, h=h, ci=ci, buf=buf: e.tensor_copy(
                kvp_sb[:, h, ci, :], buf[:, 0:17]), sem="ve")
            mmctr[f"kv{grp}"] = p.cnt["ve"]
            grp += 1
    p.wait("gpsimd", "ve", None)
    p.dma("gpsimd", o_kvp.rearrange("h (c p) e -> p h c e", p=P), kvp_sb, "st")

    qdma_hist = {}
    for h in range(8):
        qb = qc0[h % 2]
        if h >= 2:
            p.wait("gpsimd", "st", qdma_hist[h - 2])
        p.wait("gpsimd", "ve", mmctr["evac3"])
        p.dma("gpsimd", qb[1:17, :], q_sb[16 * h:16 * h + 16, :], "lg")
        p.wait("gpsimd", "lg", None)
        p.dma("gpsimd", o_qx[h, 0], qb, "st")
        qdma_hist[h] = p.cnt["st"]
    for h in range(8):
        for tile in range(2):
            pa = pa0_sb if tile == 0 else pa1_sb
            qxb = qx[(h * 2 + tile) % 2]
            if h * 2 + tile >= 2:
                p.wait("vector", "st", qdma_hist[(h * 2 + tile - 2, "qx")])
            for sl in range(4):
                p.wait("tensor", "ve", None)
                p.op("tensor", lambda e, pa=pa, h=h, sl=sl: e.matmul(
                    ps_ab[0], pa[:, h, :], q_sb[:, sl * 512:(sl + 1) * 512],
                    start=True, stop=True, skip_group_check=True), sem=None)
                p.op("tensor", lambda e, h=h, sl=sl: e.matmul(
                    ps_ab[1], pb_sb[:, h, :], q_sb[:, sl * 512:(sl + 1) * 512],
                    start=True, stop=True, skip_group_check=True), sem="pe")
                p.wait("vector", "pe", None)
                p.op("vector", lambda e: e.tensor_copy(
                    ab_scr, ps_ab[0]), sem="ve")
                p.op("vector", lambda e, qxb=qxb, sl=sl: e.scalar_tensor_tensor(
                    out=qxb[:, sl * 512:(sl + 1) * 512], in0=ab_scr,
                    scalar=0.7071067811865476, in1=ps_ab[1],
                    op0=ALU.mult, op1=ALU.mult), sem="ve")
            p.wait("gpsimd", "ve", None)
            p.dma("gpsimd", o_qx[h, 1 + tile], qxb, "st")
            qdma_hist[(h * 2 + tile, "qx")] = p.cnt["st"]

    p.wait("gpsimd", "st", None)
    p.free_prefix("e:")


def emit_f(p, qx, kvp2, wo1, o_at, ident):
    qx_sb = p.sb("f:qx_sb", [P, 8, 3, HT], BF16)
    kva = p.sb("f:kva", [P, 8, 3, 17], F32)
    kvb = p.sb("f:kvb", [P, 8, 3, 17], F32)
    kv16 = p.sb("f:kv16", [P, 8, 3, 17], BF16)
    wo_sb = p.sb("f:wo_sb_f", [P, D], BF16)
    id_sb = p.sb("f:id_sb_f", [P, P], BF16)
    o1 = [p.sb(f"f:o1_{i}", [P, P], BF16) for i in range(2)]
    o1t = [p.sb(f"f:o1t_{i}", [P, P], BF16) for i in range(2)]
    dn = p.sb("f:dn", [P, 8], F32)
    atstg = [p.sb(f"f:atstg{i}", [P, D], BF16) for i in range(2)]
    ps_o = [p.psA[:, 0:136], p.psA[:, 512:648]]
    ps_tr = p.psT[:, 0:P]
    ps_w = p.psB

    p.dma("sync", qx_sb, qx.rearrange("h c p t -> p h c t"), "lx")
    p.dma("sync", id_sb, ident, "lx")
    p.dma("sync", kva, kvp2[0].rearrange("h (c p) e -> p h c e", p=P), "lx")
    p.dma("sync", kvb, kvp2[1].rearrange("h (c p) e -> p h c e", p=P), "lx")
    p.dma("gpsimd", wo_sb, wo1, "lg")
    lall = p.cnt["lx"]
    p.wait("vector", "lx", lall)
    p.op("vector", lambda e: e.tensor_add(kv16, kva, kvb), sem="vg")
    p.wait("tensor", "vg", None)
    p.wait("tensor", "lg", 16)

    csl = [(0, 17), (17, 145), (145, 273)]
    ev_hist = {}
    for t in range(NT):
        po = ps_o[t % 2]
        if t >= 2:
            p.wait("tensor", "ve", ev_hist[t - 2])
        first = True
        for h in range(8):
            for ci, (c0_, c1_) in enumerate(csl):
                p.op("tensor", lambda e, h=h, ci=ci, c0_=c0_, c1_=c1_, t=t,
                     po=po, st=first, sp=(h == 7 and ci == 2): e.matmul(
                         po[:, h * 17:(h + 1) * 17],
                         qx_sb[0:c1_ - c0_, h, ci, t * P:(t + 1) * P],
                         kv16[0:c1_ - c0_, h, ci, :],
                         start=st, stop=sp, skip_group_check=True),
                     sem="pe" if (h == 7 and ci == 2) else None)
                first = False
        p.wait("vector", "pe", None)
        p.op("vector", lambda e, po=po: e.tensor_scalar(
            out=dn[:, 0:8],
            in0=po.rearrange("p (h v) -> p h v", v=17)[:, :, 16:17]
            .rearrange("p h v -> p (h v)"),
            scalar1=1e-5, scalar2=None, op0=ALU.max), sem="vg")
        p.wait("vector", "vg", None)
        p.op("vector", lambda e: e.reciprocal(dn[:, 0:8], dn[:, 0:8]),
             sem="vg")
        p.wait("vector", "vg", None)
        p.op("vector", lambda e, po=po, t=t: e.tensor_tensor(
            o1[t % 2].rearrange("p (h v) -> p h v", h=8),
            po.rearrange("p (h v) -> p h v", v=17)[:, :, 0:16],
            dn[:, :, None].broadcast_to([P, 8, 16]), ALU.mult), sem="ve")
        ev_hist[t] = p.cnt["ve"]
        p.wait("tensor", "ve", None)
        p.op("tensor", lambda e, t=t: e.transpose(
            ps_tr, o1[t % 2], id_sb), sem="pe")
        p.wait("vector", "pe", None)
        p.op("vector", lambda e, t=t: e.tensor_copy(o1t[t % 2], ps_tr),
             sem="ve")
        p.wait("tensor", "ve", None)
        for half in range(2):
            p.op("tensor", lambda e, t=t, half=half: e.matmul(
                ps_w[:, half * 512:(half + 1) * 512], o1t[t % 2],
                wo_sb[:, half * 512:(half + 1) * 512], start=True, stop=True,
                skip_group_check=True), sem="pe" if half == 1 else None)
        p.wait("vector", "pe", None)
        if t >= 2:
            p.wait("vector", "st", None)
        p.op("vector", lambda e, t=t: e.tensor_copy(atstg[t % 2], ps_w),
             sem="ve")
        p.wait("gpsimd", "ve", None)
        p.dma("gpsimd", o_at[t], atstg[t % 2], "st")
    p.wait("gpsimd", "st", None)
    p.free_prefix("f:")


def emit_g(p, xn2, x1, at1, wg, wu, wd, o_out, ident):
    xn2_sb = p.sb("g:xn2_sb", [P, 8, HT], BF16)
    wdT = p.sb("g:wdT", [P, 24, D], BF16)
    hfm = p.sb("g:hfm", [P, 24, D], BF16)
    wgb = [p.sb(f"g:wgb{i}", [P, D], BF16) for i in range(2)]
    wub = [p.sb(f"g:wub{i}", [P, D], BF16) for i in range(2)]
    wdb = [p.sb(f"g:wdb{i}", [P, 3072], BF16) for i in range(2)]
    wgT = p.sb("g:wgT", [P, 8, P], BF16)
    wuT = p.sb("g:wuT", [P, 8, P], BF16)
    sg = p.sb("g:sg", [P, D], BF16)
    id_sb = p.sb("g:id_sb_g", [P, P], BF16)
    x1b = [p.sb(f"g:x1b{i}", [P, D], F32) for i in range(2)]
    atb = [p.sb(f"g:atb{i}", [P, D], BF16) for i in range(2)]
    ostg = [p.sb(f"g:ostg{i}", [P, D], F32) for i in range(2)]
    tmp = p.sb("g:tmp_g", [P, D], F32)
    ps_g = p.psA[:, 0:1024]
    ps_u = p.psA[:, 1024:2048]
    ps_tr = p.psT
    ps_d = p.psB

    p.dma("sync", id_sb, ident, "lx")
    p.dma("sync", xn2_sb, xn2.rearrange("a p t -> p a t"), "lx")
    xnld = p.cnt["lx"]

    tr_hist = {}
    for dc in range(8):
        bb = wdb[dc % 2]
        if dc >= 2:
            p.wait("gpsimd", "ve", tr_hist[dc - 2])
        p.dma("gpsimd", bb, wd[dc * P:(dc + 1) * P, :], "lg")
        ldv = p.cnt["lg"]
        p.wait("tensor", "lg", ldv)
        p.wait("tensor", "lx", 16)
        for r in range(3):
            if not (dc == 0 and r == 0):
                p.wait("tensor", "ve", None)
            for j in range(8):
                p.op("tensor", lambda e, bb=bb, r=r, j=j: e.transpose(
                    ps_tr[:, j * P:(j + 1) * P],
                    bb[:, (r * 8 + j) * P:(r * 8 + j + 1) * P], id_sb),
                    sem="pe" if j == 7 else None)
            p.wait("vector", "pe", None)
            p.op("vector", lambda e, dc=dc, r=r: e.tensor_copy(
                wdT[:, r * 8:(r + 1) * 8, dc * P:(dc + 1) * P],
                ps_tr.rearrange("p (a b) -> p a b", a=8)), sem="ve")
        tr_hist[dc] = p.cnt["ve"]

    dn_hist = {}
    trpe = {}
    for half in range(2):
        if half == 1:
            p.wait("vector", "pe", None)
        for hc in range(24):
            bg, bu = wgb[hc % 2], wub[hc % 2]
            prevk = (half, hc - 2) if hc >= 2 else (half - 1, hc + 22)
            if prevk in trpe:
                p.wait("gpsimd", "pe", trpe[prevk])
            p.dma("gpsimd", bg, wg[hc * P:(hc + 1) * P, :], "lg")
            p.dma("gpsimd", bu, wu[hc * P:(hc + 1) * P, :], "lg")
            ldv = p.cnt["lg"]
            p.wait("tensor", "lg", ldv)
            p.wait("tensor", "ve", None)
            for j in range(8):
                p.op("tensor", lambda e, bg=bg, j=j: e.transpose(
                    ps_tr[:, j * P:(j + 1) * P], bg[:, j * P:(j + 1) * P],
                    id_sb), sem="pe" if j == 7 else None)
            p.wait("vector", "pe", None)
            p.op("vector", lambda e: e.tensor_copy(
                wgT, ps_tr.rearrange("p (a b) -> p a b", a=8)), sem="ve")
            p.wait("tensor", "ve", None)
            for j in range(8):
                p.op("tensor", lambda e, bu=bu, j=j: e.transpose(
                    ps_tr[:, j * P:(j + 1) * P], bu[:, j * P:(j + 1) * P],
                    id_sb), sem="pe" if j == 7 else None)
            trpe[(half, hc)] = p.cnt["pe"]
            p.wait("vector", "pe", None)
            p.op("vector", lambda e: e.tensor_copy(
                wuT, ps_tr.rearrange("p (a b) -> p a b", a=8)), sem="ve")
            p.wait("tensor", "ve", None)
            if hc > 0 or half > 0:
                p.wait("tensor", "se", None)
            p.wait("tensor", "lx", xnld)
            for dj in range(8):
                for sl in range(2):
                    p.op("tensor", lambda e, dj=dj, sl=sl, half=half: e.matmul(
                        ps_g[:, sl * 512:(sl + 1) * 512], wgT[:, dj, :],
                        xn2_sb[:, dj, half * 1024 + sl * 512:
                               half * 1024 + (sl + 1) * 512],
                        start=(dj == 0), stop=(dj == 7),
                        skip_group_check=True), sem=None)
            for dj in range(8):
                for sl in range(2):
                    p.op("tensor", lambda e, dj=dj, sl=sl, half=half: e.matmul(
                        ps_u[:, sl * 512:(sl + 1) * 512], wuT[:, dj, :],
                        xn2_sb[:, dj, half * 1024 + sl * 512:
                               half * 1024 + (sl + 1) * 512],
                        start=(dj == 0), stop=(dj == 7),
                        skip_group_check=True),
                        sem="pe" if (dj == 7 and sl == 1) else None)
            p.wait("scalar", "pe", None)
            p.op("scalar", lambda e: e.activation(
                out=sg, in_=ps_g, func=AF.Silu, scale=1.0 / 127.0), sem="se")
            p.wait("vector", "se", None)
            p.op("vector", lambda e, hc=hc: e.scalar_tensor_tensor(
                out=hfm[:, hc, :], in0=ps_u, scalar=1.0 / 127.0, in1=sg,
                op0=ALU.mult, op1=ALU.mult), sem="ve")
        hdone = p.cnt["ve"]
        p.wait("tensor", "ve", hdone)
        for tt in range(8):
            t = half * 8 + tt
            if t >= 2:
                p.wait("sync", "ve", dn_hist[t - 2])
            p.dma("sync", x1b[t % 2], x1[t], "lx")
            p.dma("sync", atb[t % 2], at1[t], "lx")
            ldv = p.cnt["lx"]
            if t >= 1:
                p.wait("tensor", "ve", dn_hist[t - 1])
            for hc in range(24):
                for sl in range(2):
                    p.op("tensor", lambda e, hc=hc, sl=sl, tt=tt: e.matmul(
                        ps_d[:, sl * 512:(sl + 1) * 512],
                        hfm[:, hc, tt * P:(tt + 1) * P],
                        wdT[:, hc, sl * 512:(sl + 1) * 512],
                        start=(hc == 0), stop=(hc == 23),
                        skip_group_check=True),
                        sem="pe" if (hc == 23 and sl == 1) else None)
            p.wait("vector", "pe", None)
            p.wait("vector", "lx", ldv)
            if t >= 2:
                p.wait("vector", "st", None)
            p.op("vector", lambda e, t=t: e.scalar_tensor_tensor(
                out=tmp, in0=ps_d, scalar=1.0 / 127.0, in1=x1b[t % 2],
                op0=ALU.mult, op1=ALU.add), sem="ve")
            p.wait("vector", "ve", None)
            p.op("vector", lambda e, t=t: e.tensor_add(
                ostg[t % 2], tmp, atb[t % 2]), sem="ve")
            dn_hist[t] = p.cnt["ve"]
            p.wait("gpsimd", "ve", None)
            p.dma("gpsimd", o_out[t], ostg[t % 2], "st")
    p.wait("gpsimd", "st", None)
    p.free_prefix("g:")


def build_l1():
    nc = new_nc()
    xf = nc.dram_tensor("xf", [S, D], F32, kind="ExternalInput").ap()
    wq = nc.dram_tensor("wq", [D, 512], F32, kind="ExternalInput").ap()
    wkvd = nc.dram_tensor("wkvd", [D, 256], F32, kind="ExternalInput").ap()
    wku = nc.dram_tensor("wku", [256, 512], F32, kind="ExternalInput").ap()
    wvu = nc.dram_tensor("wvu", [256, 512], F32, kind="ExternalInput").ap()
    o_xnfm = nc.dram_tensor("xnfm", [8, P, S], BF16, kind="ExternalOutput").ap()
    kt = nc.dram_tensor("kt", [4, P, S], BF16, kind="Internal").ap()
    qt = nc.dram_tensor("qt", [4, P, S], BF16, kind="Internal").ap()
    va = nc.dram_tensor("vaug", [32, P, 520], BF16, kind="Internal").ap()
    o_y = nc.dram_tensor("yfm", [4, P, S], BF16, kind="ExternalOutput").ap()
    bden = nc.dram_tensor("bden", [2, 1, 512], BF16, kind="Internal").ap()
    ident = nc.inline_tensor(_ident_np(), name="ident").ap()
    maskc = nc.inline_tensor(_mask_np(), name="maskc").ap()
    p = Prog(nc)
    emit_a(p, xf, wq, wkvd, wku, wvu, o_xnfm, kt, qt, va, ident)
    p.barrier()
    emit_b(p, kt, qt, va, o_y, maskc, bden)
    return p.build()


def build_l2():
    nc = new_nc()
    y_all = nc.dram_tensor("y_all", [8, P, HT], BF16, kind="ExternalInput").ap()
    wout = nc.dram_tensor("wout", [D, D], F32, kind="ExternalInput").ap()
    x_own = nc.dram_tensor("x_own", [HT, D], F32, kind="ExternalInput").ap()
    xr = nc.dram_tensor("xr", [NT, P, D], F32, kind="Internal").ap()
    xnfm = nc.dram_tensor("xnfm_own", [8, P, HT], BF16, kind="ExternalInput").ap()
    wfc = nc.dram_tensor("wfc", [4096, D], F32, kind="ExternalInput").ap()
    lnw = nc.dram_tensor("lnw", [1, 4096], F32, kind="ExternalInput").ap()
    lnb = nc.dram_tensor("lnb", [1, 4096], F32, kind="ExternalInput").ap()
    wproj = nc.dram_tensor("wproj", [D, 4096], F32, kind="ExternalInput").ap()
    o_x1 = nc.dram_tensor("x1", [NT, P, D], F32, kind="ExternalOutput").ap()
    gamt = nc.dram_tensor("gamt", [P, 8], F32, kind="ExternalInput").ap()
    wq1 = nc.dram_tensor("wq1", [D, P], F32, kind="ExternalInput").ap()
    wkv1 = nc.dram_tensor("wkv1", [D, 256], F32, kind="ExternalInput").ap()
    o_xn2 = nc.dram_tensor("xn2fm", [8, P, HT], BF16, kind="ExternalOutput").ap()
    o_qx = nc.dram_tensor("qexp", [8, 3, P, HT], BF16, kind="ExternalOutput").ap()
    o_kvp = nc.dram_tensor("kvp", [8, 384, 17], F32, kind="ExternalOutput").ap()
    ident = nc.inline_tensor(_ident_np(), name="ident").ap()
    id32 = nc.inline_tensor(np.eye(P, dtype=np.float32), name="id32").ap()
    pa0, pa1, pb = _pa_np()
    c_pa0 = nc.inline_tensor(pa0, name="pa0").ap()
    c_pa1 = nc.inline_tensor(pa1, name="pa1").ap()
    c_pb = nc.inline_tensor(pb, name="pb").ap()
    p = Prog(nc)
    emit_c(p, y_all, wout, x_own, xr)
    p.barrier()
    emit_d(p, xr, xnfm, wfc, lnw, lnb, wproj, o_x1, ident, id32)
    p.barrier()
    emit_e(p, o_x1, gamt, wq1, wkv1, o_xn2, o_qx, o_kvp, ident, c_pa0, c_pa1,
           c_pb)
    return p.build()


def build_l3():
    nc = new_nc()
    qx = nc.dram_tensor("qexp", [8, 3, P, HT], BF16, kind="ExternalInput").ap()
    kvp2 = nc.dram_tensor("kvp2", [2, 8, 384, 17], F32, kind="ExternalInput").ap()
    wo1 = nc.dram_tensor("wout1", [P, D], F32, kind="ExternalInput").ap()
    at1 = nc.dram_tensor("at1", [NT, P, D], BF16, kind="Internal").ap()
    xn2 = nc.dram_tensor("xn2fm", [8, P, HT], BF16, kind="ExternalInput").ap()
    x1 = nc.dram_tensor("x1", [NT, P, D], F32, kind="ExternalInput").ap()
    wg = nc.dram_tensor("wg", [3072, D], I32, kind="ExternalInput").ap()
    wu = nc.dram_tensor("wu", [3072, D], I32, kind="ExternalInput").ap()
    wd = nc.dram_tensor("wd", [D, 3072], I32, kind="ExternalInput").ap()
    o_out = nc.dram_tensor("out", [NT, P, D], F32, kind="ExternalOutput").ap()
    ident = nc.inline_tensor(_ident_np(), name="ident").ap()
    p = Prog(nc)
    emit_f(p, qx, kvp2, wo1, at1, ident)
    p.barrier()
    emit_g(p, xn2, x1, at1, wg, wu, wd, o_out, ident)
    return p.build()


def _bf16(x):
    return np.ascontiguousarray(x, dtype=BF16NP)


def _f32(x):
    return np.ascontiguousarray(x, dtype=np.float32)


def kernel(**inputs):
    x = _f32(inputs["L0_attn_x_bld"])
    wq_full = _f32(inputs["L0_attn_Wq"]).reshape(D, 16 * 64)
    wkvd = _f32(inputs["L0_attn_Wkv_down"])
    wku_full = _f32(inputs["L0_attn_Wk_up"]).reshape(256, 16 * 64)
    wvu_full = _f32(inputs["L0_attn_Wv_up"]).reshape(256, 16 * 64)

    nc1 = build_l1()
    in1 = []
    for core in range(8):
        b, g = core // 2, core % 2
        hsl = slice(g * 512, (g + 1) * 512)
        in1.append({
            "xf": np.ascontiguousarray(x[b]),
            "wq": np.ascontiguousarray(wq_full[:, hsl]),
            "wkvd": wkvd,
            "wku": np.ascontiguousarray(wku_full[:, hsl]),
            "wvu": np.ascontiguousarray(wvu_full[:, hsl]),
        })
    r1 = run_bass_kernel_spmd(nc1, in1, core_ids=list(range(8))).results

    wout = _f32(inputs["L0_attn_Wout"]).reshape(D, D)
    wfc = _f32(inputs["L0_ffn_W_fc"])
    lnw = _f32(inputs["L0_ffn_ln_w"]).reshape(1, 4096)
    lnb = _f32(inputs["L0_ffn_ln_b"]).reshape(1, 4096)
    wproj = _f32(inputs["L0_ffn_W_proj"])
    gamt = _f32(np.asarray(inputs["L1_attn_gamma"], np.float32)
                .reshape(8, P).T)
    wq1 = _f32(inputs["L1_attn_w_q"])
    wkv1 = _f32(inputs["L1_attn_w_kv"])
    nc2 = build_l2()
    in2 = []
    for core in range(8):
        b, g = core // 2, core % 2
        ysl = slice(g * HT, (g + 1) * HT)
        ya = np.concatenate([np.asarray(r1[2 * b]["yfm"])[:, :, ysl],
                             np.asarray(r1[2 * b + 1]["yfm"])[:, :, ysl]], 0)
        in2.append({
            "y_all": np.ascontiguousarray(ya),
            "wout": wout,
            "x_own": np.ascontiguousarray(x[b, g * HT:(g + 1) * HT, :]),
            "xnfm_own": np.ascontiguousarray(
                np.asarray(r1[core]["xnfm"])[:, :, ysl]),
            "wfc": wfc, "lnw": lnw, "lnb": lnb, "wproj": wproj,
            "gamt": gamt, "wq1": wq1, "wkv1": wkv1,
        })
    r2 = run_bass_kernel_spmd(nc2, in2, core_ids=list(range(8))).results

    wo1 = _f32(inputs["L1_attn_w_out"])
    wg = np.ascontiguousarray(np.asarray(inputs["L1_ffn_w_gate"], np.int32))
    wu = np.ascontiguousarray(np.asarray(inputs["L1_ffn_w_up"], np.int32))
    wd = np.ascontiguousarray(np.asarray(inputs["L1_ffn_w_down"], np.int32))
    nc3 = build_l3()
    in3 = []
    for core in range(8):
        kvp2 = np.stack([np.asarray(r2[core]["kvp"]),
                         np.asarray(r2[core ^ 1]["kvp"])])
        in3.append({"qexp": np.asarray(r2[core]["qexp"]), "kvp2": kvp2,
                    "wout1": wo1, "xn2fm": np.asarray(r2[core]["xn2fm"]),
                    "x1": np.asarray(r2[core]["x1"]),
                    "wg": wg, "wu": wu, "wd": wd})
    r3 = run_bass_kernel_spmd(nc3, in3, core_ids=list(range(8))).results

    out = np.empty((B, S, D), np.float32)
    for core in range(8):
        b, g = core // 2, core % 2
        out[b, g * HT:(g + 1) * HT, :] = np.asarray(
            r3[core]["out"], np.float32).reshape(HT, D)
    return out


# version 10
# speedup vs baseline: 1.3423x; 1.0391x over previous
import math
from contextlib import ExitStack

import numpy as np
import ml_dtypes

import concourse.bass as bass
import concourse.mybir as mybir
from concourse.bass_utils import run_bass_kernel_spmd

F32 = mybir.dt.float32
BF16 = mybir.dt.bfloat16
I32 = mybir.dt.int32
AF = mybir.ActivationFunctionType
ALU = mybir.AluOpType
AX = mybir.AxisListType

P = 128
D = 1024
S = 4096
B = 4
HT = 2048
NT = HT // P
NTS = S // P
EXPB = 25.0

BF16NP = ml_dtypes.bfloat16


ARENA_F32 = 53136
SEMS = ("lx", "lg", "ld", "pe", "ve", "se", "vg", "ge", "st")


class Prog:

    ENGINES = ("sync", "tensor", "vector", "scalar", "gpsimd")

    def __init__(self, nc):
        self.nc = nc
        self.ctx = ExitStack()
        self.q = {e: [] for e in self.ENGINES}
        self.cnt = {}
        self._sems = {}
        self._ar = self.ctx.enter_context(
            nc.sbuf_tensor("arena", [P, ARENA_F32], F32)).ap()
        self._ar16 = self._ar.bitcast(BF16)
        self._frees = [(0, ARENA_F32 * 4)]
        self._amap = {}
        self.psA = self.ctx.enter_context(
            nc.psum_tensor("psA", [P, 2048], F32)).ap()
        self.psB = self.ctx.enter_context(
            nc.psum_tensor("psB", [P, 1024], F32)).ap()
        self.psC = self.ctx.enter_context(
            nc.psum_tensor("psC", [P, 512], F32)).ap()
        self.psT = self.ctx.enter_context(
            nc.psum_tensor("psT", [P, 1024], BF16)).ap()
        for s_ in SEMS:
            self.sem(s_)

    def carve(self, name, nbytes):
        nb = (nbytes + 63) // 64 * 64
        for i, (s, e) in enumerate(self._frees):
            if e - s >= nb:
                if s + nb == e:
                    self._frees.pop(i)
                else:
                    self._frees[i] = (s + nb, e)
                self._amap[name] = (s, nb)
                return s
        raise MemoryError(f"arena full: {name} needs {nb}B; free={self._frees}")

    def vf32(self, off, shape):
        assert off % 4 == 0
        n = int(np.prod(shape[1:]))
        v = self._ar[0:shape[0], off // 4:off // 4 + n]
        return self._shape(v, shape)

    def vbf16(self, off, shape):
        assert off % 2 == 0
        n = int(np.prod(shape[1:]))
        v = self._ar16[0:shape[0], off // 2:off // 2 + n]
        return self._shape(v, shape)

    @staticmethod
    def _shape(v, shape):
        if len(shape) <= 2:
            return v
        dims = {f"d{i}": shape[1 + i] for i in range(len(shape) - 1)}
        names = " ".join(f"d{i}" for i in range(len(shape) - 1))
        return v.rearrange(f"p ({names}) -> p {names}", **dims)

    def sb(self, name, shape, dtype):
        es = 2 if dtype == BF16 else 4
        off = self.carve(name, int(np.prod(shape[1:])) * es)
        return self.vbf16(off, shape) if dtype == BF16 else self.vf32(off, shape)

    def free(self, *names):
        for n in names:
            s, nb = self._amap.pop(n)
            self._frees.append((s, s + nb))
        self._frees.sort()
        merged = []
        for s, e in self._frees:
            if merged and merged[-1][1] == s:
                merged[-1] = (merged[-1][0], e)
            else:
                merged.append((s, e))
        self._frees = merged

    def free_prefix(self, pfx):
        self.free(*[n for n in list(self._amap) if n.startswith(pfx)])

    def barrier(self):
        for e in self.ENGINES:
            for s_, h in self._sems.items():
                v = self.cnt[s_]
                if v:
                    self.on(e, lambda eng, h=h, v=v: eng.wait_ge(h, v))

    def sem(self, name):
        h = self.ctx.enter_context(self.nc.semaphore(name))
        self.cnt[name] = 0
        self._sems[name] = h
        return h

    def _h(self, sem):
        return self._sems[sem]

    def on(self, eng, fn):
        self.q[eng].append(fn)

    def inc(self, sem, n=1):
        self.cnt[sem] += n
        return self.cnt[sem]

    def wait(self, eng, sem, val=None):
        v = self.cnt[sem] if val is None else val
        h = self._h(sem)
        self.on(eng, lambda e, h=h, v=v: e.wait_ge(h, v))

    def op(self, eng, fn, sem=None, n=1):
        if sem is None:
            self.on(eng, fn)
            return None
        h = self._h(sem)
        v = self.inc(sem, n)
        self.on(eng, lambda e, fn=fn, h=h, n=n: fn(e).then_inc(h, n))
        return v

    def dma(self, eng, out, in_, sem, **kw):
        return self.op(eng, lambda e, o=out, i=in_: e.dma_start(out=o, in_=i, **kw),
                       sem=sem, n=16)

    def build(self, guard_store_sem=None):
        with self.ctx:
            with self.nc.Block() as block:
                for eng in self.ENGINES:
                    cmds = self.q[eng]
                    if not cmds:
                        continue
                    deco = getattr(block, eng)

                    def body(e, cmds=cmds):
                        for fn in cmds:
                            fn(e)

                    deco(body)
        return self.nc


def new_nc():
    return bass.Bass("TRN2", target_bir_lowering=False, debug=False)


def _ident_np():
    return np.eye(P, dtype=BF16NP)


def emit_a(p, xf, wq, wkvd, wku, wvu, o_xnfm, o_kt, o_qt, o_va, ident):
    xb = [p.sb(f"a:xb{i}", [P, D], F32) for i in range(2)]
    xn16 = [p.sb(f"a:xn16{i}", [P, D], BF16) for i in range(2)]
    xnfm = p.sb("a:xnfm_sb", [P, 8, S], BF16)
    lat = p.sb("a:lat_sb", [P, 2, S], BF16)
    qt_sb = p.sb("a:qt_sb", [P, 4, S], BF16)
    kstg = [p.sb(f"a:kstg{i}", [P, S], BF16) for i in range(2)]
    vstg = [p.sb(f"a:vstg{i}", [P, 520], BF16) for i in range(2)]
    wq_sb = p.sb("a:wq_sb", [P, 8, 512], BF16)
    wkvd_sb = p.sb("a:wkvd_sb", [P, 8, 256], BF16)
    wku_sb = p.sb("a:wku_sb", [P, 2, 512], BF16)
    wvu_sb = p.sb("a:wvu_sb", [P, 2, 512], BF16)
    id_sb = p.sb("a:id_sb", [P, P], BF16)
    stt = p.sb("a:stt_sb", [P, 8], F32)
    sqscr = p.sb("a:sqscr", [P, D], F32)
    epsc = p.sb("a:epsc", [P, 1], F32)
    ps_tr = p.psT
    ps_mm = [p.psA[:, i * 512:(i + 1) * 512] for i in range(2)]

    p.dma("gpsimd", wq_sb, wq.rearrange("(a p) k -> p a k", p=P), "lg")
    p.dma("gpsimd", wkvd_sb, wkvd.rearrange("(a p) k -> p a k", p=P), "lg")
    p.dma("gpsimd", wku_sb, wku.rearrange("(a p) k -> p a k", p=P), "lg")
    p.dma("gpsimd", wvu_sb, wvu.rearrange("(a p) k -> p a k", p=P), "lg")
    p.dma("sync", id_sb, ident, "lx")
    p.op("gpsimd", lambda e: e.memset(epsc, 1e-5), sem="ve")
    p.op("gpsimd", lambda e: e.memset(vstg[0][:, :], 1.0), sem="ve")
    p.op("gpsimd", lambda e: e.memset(vstg[1][:, :], 1.0), sem="ve")
    wload = p.cnt["lg"]

    c = lambda i: stt[:, i:i + 1]
    for i in range(NTS):
        buf = xb[i % 2]
        xn = xn16[i % 2]
        if i >= 2:
            p.wait("sync", "se", None)
        p.dma("sync", buf, xf[i * P:(i + 1) * P, :], "lx")
        ldv = p.cnt["lx"]
        p.wait("vector", "lx", ldv)
        if i > 0:
            p.wait("vector", "se", None)
        p.op("vector", lambda e, b=buf, o=c(0): e.reduce_sum(o, b, axis=AX.X),
             sem="vg")
        p.wait("scalar", "lx", ldv)
        p.wait("scalar", "vg", None)
        p.op("scalar", lambda e, b=buf, o=c(1): e.activation(
            out=sqscr, in_=b, func=AF.Square, accum_out=o), sem="se")
        p.wait("vector", "se", None)
        p.op("vector", lambda e, o=c(2), a=c(0): e.tensor_scalar(
            out=o, in0=a, scalar1=1.0 / D, scalar2=None, op0=ALU.mult), sem="vg")
        p.wait("vector", "vg", None)
        p.op("vector", lambda e, o=c(3), a=c(1): e.tensor_scalar(
            out=o, in0=a, scalar1=1.0 / D, scalar2=None, op0=ALU.mult), sem="vg")
        p.wait("vector", "vg", None)
        p.op("vector", lambda e, o=c(4), a=c(2): e.tensor_tensor(
            o, a, a, ALU.mult), sem="vg")
        p.wait("vector", "vg", None)
        p.op("vector", lambda e, o=c(5), a=c(3), b2=c(4): e.tensor_tensor(
            o, a, b2, ALU.subtract), sem="vg")
        p.wait("scalar", "vg", None)
        p.op("scalar", lambda e, o=c(6), a=c(5): e.activation(
            out=o, in_=a, func=AF.Sqrt, bias=epsc[:, 0:1]), sem="se")
        p.wait("vector", "se", None)
        p.op("vector", lambda e, o=c(6), a=c(6): e.reciprocal(o, a), sem="vg")
        p.wait("vector", "vg", None)
        p.op("vector", lambda e, o=c(7), a=c(2), b2=c(6): e.scalar_tensor_tensor(
            out=o, in0=a, scalar=-1.0, in1=b2, op0=ALU.mult, op1=ALU.mult),
            sem="vg")
        p.wait("scalar", "vg", None)
        if i >= 2:
            p.wait("scalar", "pe", p.cnt["pe"] - 1)
        p.op("scalar", lambda e, b=buf, o=xn: e.activation(
            out=o, in_=b, func=AF.Identity, scale=c(6), bias=c(7)), sem="se")
        p.wait("tensor", "se", None)
        if i > 0:
            p.wait("tensor", "ve", None)
        for j in range(8):
            p.op("tensor", lambda e, xnj=xn[:, j * P:(j + 1) * P],
                 d=ps_tr[:, j * P:(j + 1) * P]: e.transpose(d, xnj, id_sb),
                 sem="pe" if j == 7 else None)
        p.wait("vector", "pe", None)
        p.op("vector", lambda e, i=i: e.tensor_copy(
            xnfm[:, :, i * P:(i + 1) * P],
            ps_tr.rearrange("p (a b) -> p a b", a=8)), sem="ve")
    p.wait("gpsimd", "ve", None)
    for a in range(8):
        p.dma("gpsimd", o_xnfm[a], xnfm[:, a, :], "st")

    mmctr = {"n": 0}

    def mm_block(chunks, dest_fn, evac_eng="vector"):
        k = mmctr["n"]
        buf = ps_mm[k % 2]
        if k >= 2:
            p.wait("tensor", "ve", mmctr[f"evac{k - 2}"])
        for ci, (lh, rh) in enumerate(chunks):
            p.op("tensor", lambda e, lh=lh, rh=rh, st=(ci == 0),
                 sp=(ci == len(chunks) - 1), b=buf: e.matmul(
                     b, lh, rh, start=st, stop=sp),
                 sem="pe" if ci == len(chunks) - 1 else None)
        p.wait(evac_eng, "pe", None)
        p.op(evac_eng, lambda e, b=buf: dest_fn(e, b), sem="ve")
        mmctr[f"evac{k}"] = p.cnt["ve"]
        mmctr["n"] += 1

    p.wait("tensor", "lg", wload)
    p.wait("tensor", "ve", None)
    for c2 in range(2):
        for sl in range(8):
            chunks = [(wkvd_sb[:, dk, c2 * P:(c2 + 1) * P],
                       xnfm[:, dk, sl * 512:(sl + 1) * 512]) for dk in range(8)]
            mm_block(chunks, lambda e, b, c2=c2, sl=sl: e.tensor_copy(
                lat[:, c2, sl * 512:(sl + 1) * 512], b))

    for qc in range(4):
        for sl in range(8):
            chunks = [(wq_sb[:, dk, qc * P:(qc + 1) * P],
                       xnfm[:, dk, sl * 512:(sl + 1) * 512]) for dk in range(8)]
            mm_block(chunks, lambda e, b, qc=qc, sl=sl: e.tensor_copy(
                qt_sb[:, qc, sl * 512:(sl + 1) * 512], b))
    p.wait("gpsimd", "ve", None)
    for qc in range(4):
        p.dma("gpsimd", o_qt[qc], qt_sb[:, qc, :], "st")

    for kc in range(4):
        stg = kstg[kc % 2]
        if kc >= 2:
            p.wait("tensor", "st", None)
        for sl in range(8):
            chunks = [(wku_sb[:, c2, kc * P:(kc + 1) * P],
                       lat[:, c2, sl * 512:(sl + 1) * 512]) for c2 in range(2)]
            mm_block(chunks, lambda e, b, sl=sl, stg=stg: e.tensor_copy(
                stg[:, sl * 512:(sl + 1) * 512], b))
        p.wait("gpsimd", "ve", None)
        p.dma("gpsimd", o_kt[kc], stg, "st")

    for kb in range(32):
        stg = vstg[kb % 2]
        if kb >= 2:
            p.wait("vector", "st", None)
        chunks = [(lat[:, c2, kb * P:(kb + 1) * P], wvu_sb[:, c2, :])
                  for c2 in range(2)]
        mm_block(chunks, lambda e, b, stg=stg: e.tensor_copy(
            stg.rearrange("p (h v) -> p h v", h=8)[:, :, 0:64],
            b.rearrange("p (h v) -> p h v", h=8)))
        p.wait("gpsimd", "ve", None)
        p.dma("gpsimd", o_va[kb], stg, "st")

    p.wait("gpsimd", "st", None)
    p.free_prefix("a:")


def _mask_np():
    u = np.arange(896)[None, :] - 384
    r = np.arange(P)[:, None]
    return (u >= r).astype(BF16NP)


def emit_b(p, kt, qt, va, o_y, maskc, bden):
    kt_sb = p.sb("b:kt_sb", [P, 4, S], BF16)
    qt_sb = p.sb("b:qt_sb", [P, 4, S], BF16)
    va_sb = p.sb("b:va_sb", [P, 32, 8, 65], BF16)
    mk_sb = p.sb("b:mk_sb", [P, 896], BF16)
    ptb = [p.sb(f"b:ptb{i}", [P, 1024], BF16) for i in range(3)]
    ystg = [p.sb(f"b:ystg{i}", [P, S], BF16) for i in range(2)]
    rcp16 = [p.sb(f"b:rcp16_{i}", [1, 512], BF16) for i in range(2)]
    den16 = [p.sb(f"b:den16_{i}", [64, 512], BF16) for i in range(2)]
    expb = p.sb("b:expb", [P, 1], F32)
    ps_sg = [p.psA[:, 0:1024], p.psA[:, 1024:2048], p.psB]
    ps_y = [p.psC, p.psT.bitcast(F32)]

    p.dma("sync", kt_sb, kt.rearrange("a p t -> p a t"), "ld")
    p.dma("sync", qt_sb, qt.rearrange("a p t -> p a t"), "ld")
    p.dma("sync", va_sb.rearrange("p b h v -> p b (h v)"),
          va.rearrange("b p x -> p b x"), "ld")
    p.dma("sync", mk_sb, maskc, "ld")
    p.op("gpsimd", lambda e: e.memset(expb, -EXPB), sem="ve")
    ldall = p.cnt["ld"]
    p.wait("tensor", "ld", ldall)
    p.wait("tensor", "ve", None)
    p.wait("scalar", "ld", ldall)
    p.wait("scalar", "ve", None)
    p.wait("vector", "ld", ldall)

    ynorm_hist = {}
    dma_hist = {}
    exp_hist = {}
    avc_hist = {}

    misc = {}

    def emit_st(cc, rb, q0, g):
        if exp_hist.get(g % 3) is not None:
            p.wait("tensor", "se", exp_hist[g % 3])
        for j in range(2):
            kb = 2 * g + j
            p.op("tensor", lambda e, cc=cc, rb=rb, kb=kb, q0=q0, g=g, j=j:
                 e.matmul(ps_sg[g % 3][:, j * 512:(j + 1) * 512],
                          kt_sb[rb:rb + 64, cc, kb * P:(kb + 1) * P],
                          qt_sb[rb:rb + 64, cc, q0:q0 + 512],
                          start=True, stop=True, skip_group_check=True),
                 sem="pe" if j == 1 else None)
        return p.cnt["pe"]

    def emit_evac(rec):
        h, sl, cc, rb, q0 = rec["h"], rec["sl"], rec["cc"], rec["rb"], rec["q0"]
        idx = rec["idx"]
        par = idx % 2
        yb = ps_y[sl % 2]
        p.wait("vector", "pe", rec["av"])
        if ("rcpdma", idx - 2) in misc:
            p.wait("vector", "st", misc[("rcpdma", idx - 2)])

        def _recip_lp(e, yb=yb, par=par):
            with e.bass.allow_low_precision(
                    reason="softmax denom reciprocal; common scale cancels"):
                return e.reciprocal(rcp16[par], yb[64:65, :])

        p.op("vector", _recip_lp, sem="ve")
        rc = p.cnt["ve"]
        p.wait("gpsimd", "ve", rc)
        if idx >= 2:
            p.wait("gpsimd", "ve", ynorm_hist[idx - 2])
        p.dma("gpsimd", bden[par], rcp16[par], "st")
        misc[("rcpdma", idx)] = p.cnt["st"]
        p.wait("gpsimd", "st", p.cnt["st"])
        p.dma("gpsimd", den16[par], bden[par].broadcast_to([64, 512]), "st")
        bcst = p.cnt["st"]
        if cc >= 2 and (cc - 2) in dma_hist:
            p.wait("vector", "st", dma_hist[cc - 2])
        p.wait("vector", "st", bcst)
        p.op("vector", lambda e, yb=yb, rb=rb, q0=q0, cc=cc, par=par:
             e.tensor_tensor(ystg[cc % 2][rb:rb + 64, q0:q0 + 512],
                             yb[0:64, :], den16[par], ALU.mult), sem="ve")
        ynorm_hist[idx] = p.cnt["ve"]
        if h % 2 == 1 and sl == 7:
            p.wait("gpsimd", "ve", None)
            p.dma("gpsimd", o_y[cc], ystg[cc % 2], "st")
            dma_hist[cc] = p.cnt["st"]

    prev_rec = None
    for idx, (h, sl) in enumerate([(h, sl) for h in range(8)
                                   for sl in range(8)]):
        cc, rb, q0 = h // 2, (h % 2) * 64, sl * 512
        ngr = 2 * (sl + 1)
        st_pe = {0: emit_st(cc, rb, q0, 0)}
        if ngr > 1:
            st_pe[1] = emit_st(cc, rb, q0, 1)
        if prev_rec is not None:
            emit_evac(prev_rec)
        for g in range(ngr):
            if g + 2 < ngr:
                st_pe[g + 2] = emit_st(cc, rb, q0, g + 2)
            w = st_pe[g]
            if avc_hist.get(g % 3) is not None:
                w = max(w, avc_hist[g % 3])
            p.wait("scalar", "pe", w)
            p.op("scalar", lambda e, g=g: e.activation(
                out=ptb[g % 3], in_=ps_sg[g % 3], func=AF.Exp,
                scale=0.125, bias=expb[:, 0:1]), sem="se")
            exp_hist[g % 3] = p.cnt["se"]
            nmask = 0
            for j in range(2):
                kb = 2 * g + j
                delta = kb * P - q0
                if 0 <= delta <= 384:
                    off = 384 - delta
                    p.wait("vector", "se", exp_hist[g % 3])
                    p.op("vector", lambda e, g=g, j=j, off=off:
                         e.tensor_tensor(
                             ptb[g % 3][:, j * 512:(j + 1) * 512],
                             ptb[g % 3][:, j * 512:(j + 1) * 512],
                             mk_sb[:, off:off + 512], ALU.mult), sem="ve")
                    nmask += 1
            if nmask:
                p.wait("tensor", "ve", p.cnt["ve"])
            else:
                p.wait("tensor", "se", exp_hist[g % 3])
            if g == 0 and idx >= 2:
                p.wait("tensor", "ve", ynorm_hist[idx - 2])
            for j in range(2):
                kb = 2 * g + j
                p.op("tensor", lambda e, kb=kb, h=h, g=g, j=j, sl=sl,
                     st=(kb == 0), sp=(kb == ngr * 2 - 1): e.matmul(
                         ps_y[sl % 2][0:65, :], va_sb[:, kb, h, :],
                         ptb[g % 3][:, j * 512:(j + 1) * 512],
                         start=st, stop=sp, skip_group_check=True),
                     sem="pe" if j == 1 else None)
            avc_hist[g % 3] = p.cnt["pe"]
        prev_rec = dict(h=h, sl=sl, cc=cc, rb=rb, q0=q0, av=p.cnt["pe"],
                        idx=idx)
    emit_evac(prev_rec)

    p.wait("gpsimd", "st", None)
    p.free_prefix("b:")


def emit_c(p, y_all, wout, x_own, o_xr):
    y_sb = p.sb("c:y_sb", [P, 8, HT], BF16)
    wo_sb = p.sb("c:wo_sb", [P, 8, D], BF16)
    xb = [p.sb(f"c:xb{i}", [P, D], F32) for i in range(2)]
    stg = [p.sb(f"c:stg{i}", [P, D], F32) for i in range(2)]
    ps = [p.psA[:, i * 1024:(i + 1) * 1024] for i in range(2)]

    p.dma("sync", y_sb, y_all.rearrange("a p t -> p a t"), "lx")
    p.dma("gpsimd", wo_sb, wout.rearrange("(a p) k -> p a k", p=P), "lg")
    evac_hist = {}
    for t in range(NT):
        xbuf = xb[t % 2]
        if t >= 2:
            p.wait("sync", "ve", evac_hist[t - 2])
        p.dma("sync", xbuf, x_own[t * P:(t + 1) * P, :], "lx")
        ldv = p.cnt["lx"]
        pb = ps[t % 2]
        p.wait("tensor", "lx", 16)
        p.wait("tensor", "lg", 16)
        if t >= 2:
            p.wait("tensor", "ve", evac_hist[t - 2])
        for ic in range(8):
            for half in range(2):
                p.op("tensor", lambda e, ic=ic, half=half, t=t, pb=pb: e.matmul(
                    pb[:, half * 512:(half + 1) * 512],
                    y_sb[:, ic, t * P:(t + 1) * P],
                    wo_sb[:, ic, half * 512:(half + 1) * 512],
                    start=(ic == 0), stop=(ic == 7), skip_group_check=True),
                    sem="pe" if (ic == 7 and half == 1) else None)
        p.wait("vector", "lx", ldv)
        p.wait("vector", "pe", None)
        if t >= 2:
            p.wait("vector", "st", None)
        p.op("vector", lambda e, t=t, pb=pb, xbuf=xbuf: e.tensor_add(
            stg[t % 2], pb, xbuf), sem="ve")
        evac_hist[t] = p.cnt["ve"]
        p.wait("gpsimd", "ve", None)
        p.dma("gpsimd", o_xr[t], stg[t % 2], "st")
    p.wait("gpsimd", "st", None)
    p.free_prefix("c:")


def emit_d(p, xr, xnfm, wfc, lnw, lnb, wproj, o_x1, ident, id32):
    wfcT = p.sb("d:wfcT", [P, 8, 4096], BF16)
    wprT = p.sb("d:wprT", [P, 32, D], BF16)
    dof = p.carve("d:arena", 53248)
    arena = p.vf32(dof, [P, 13312])
    w3s = [arena[:, 0:4096], arena[:, 4096:8192]]
    sgna = p.vbf16(dof + 32768, [P, 4096])
    sgnb = p.vbf16(dof + 40960, [P, 4096])
    xnf = [p.vbf16(dof, [P, 8, P]),
           p.vbf16(dof + 2048, [P, 8, P])]
    h_sb = p.vbf16(dof + 4096, [P, 4096])
    hT = p.vbf16(dof + 12288, [P, 32, P])
    lnwbc = p.vbf16(dof + 20480, [P, 4096])
    lnbbc = p.vbf16(dof + 28672, [P, 4096])
    xrb = [arena[:, 9216:10240], arena[:, 10240:11264]]
    x1s = [arena[:, 11264:12288], arena[:, 12288:13312]]
    id_sb = p.sb("d:id_sb", [P, P], BF16)
    id32_sb = p.sb("d:id32_sb", [P, P], F32)
    onesf = p.sb("d:onesf", [1, P], F32)
    ones16 = p.sb("d:ones16", [1, P], BF16)
    lnrow = p.sb("d:lnrow", [1, 8192], BF16)
    stt = p.sb("d:stt_d", [P, 12], F32)
    g11 = p.sb("d:g11", [1, 4], F32)
    thr = p.sb("d:thr_d", [P, 4], F32)
    epsc = p.sb("d:epsc_d", [P, 1], F32)
    sqscr = p.sb("d:sqscr", [P, 1024], F32)
    ps_fc = p.psA
    ps_tr = p.psT
    ps_pr = p.psB
    ps_sc = p.psC[:, 0:128]

    p.dma("sync", id_sb, ident, "lx")
    p.dma("sync", id32_sb, id32, "lx")
    p.op("gpsimd", lambda e: e.memset(onesf, 1.0), sem="ve")
    p.op("gpsimd", lambda e: e.memset(ones16, 1.0), sem="ve")
    p.op("gpsimd", lambda e: e.memset(epsc, 1e-5), sem="ve")
    p.dma("gpsimd", lnrow[0:1, 0:4096], lnw, "lg")
    p.dma("gpsimd", lnrow[0:1, 4096:8192], lnb, "lg")
    cload = p.cnt["lg"]

    def gamma_phase(src_fn, ngr, gcols, tcols):
        for gidx in range(ngr):
            bb = w3s[gidx % 2][:, 0:gcols]
            if gidx >= 2:
                p.wait("sync", "vg", None)
            p.dma("sync", bb.rearrange("p (k d) -> p k d", d=1024),
                  src_fn(gidx), "lx")
            p.wait("vector", "lx", p.cnt["lx"])
            dst = stt[:, 8 + gidx % 2:9 + gidx % 2]
            p.op("vector", lambda e, bb=bb, dst=dst: e.tensor_reduce(
                dst, bb, axis=AX.X, op=ALU.add,
                apply_absolute_value=True), sem="vg")
            p.wait("vector", "vg", None)
            if gidx == 0:
                p.op("vector", lambda e, dst=dst: e.tensor_copy(
                    stt[:, 10:11], dst), sem="vg")
            else:
                p.op("vector", lambda e, dst=dst: e.tensor_add(
                    stt[:, 10:11], stt[:, 10:11], dst), sem="vg")
            p.wait("vector", "vg", None)
        p.wait("tensor", "vg", None)
        p.op("tensor", lambda e: e.transpose(
            ps_sc[0:1, 0:P], stt[:, 10:11], id32_sb), sem="pe")
        p.wait("vector", "pe", None)
        p.op("vector", lambda e: e.reduce_sum(
            g11[0:1, 0:1], ps_sc[0:1, 0:P], axis=AX.X), sem="vg")
        p.wait("vector", "vg", None)
        p.op("vector", lambda e: e.tensor_scalar(
            out=g11[0:1, 1:2], in0=g11[0:1, 0:1], scalar1=1.0 / (4096.0 * D),
            scalar2=1e-6, op0=ALU.mult, op1=ALU.add), sem="vg")
        p.wait("vector", "vg", None)
        p.op("vector", lambda e: e.tensor_scalar(
            out=g11[0:1, 2:3], in0=g11[0:1, 1:2], scalar1=0.5, scalar2=None,
            op0=ALU.mult), sem="vg")
        p.wait("vector", "vg", None)
        p.op("vector", lambda e: e.tensor_scalar(
            out=g11[0:1, 3:4], in0=g11[0:1, 2:3], scalar1=-1.0, scalar2=None,
            op0=ALU.mult), sem="vg")
        p.wait("vector", "vg", None)
        p.wait("tensor", "vg", None)
        p.op("tensor", lambda e: e.matmul(
            ps_sc[:, 0:2], onesf, g11[0:1, 2:4], start=True, stop=True,
            skip_group_check=True), sem="pe")
        p.wait("vector", "pe", None)
        p.op("vector", lambda e, tcols=tcols: e.tensor_copy(
            thr[:, tcols[0]:tcols[1] + 1], ps_sc[:, 0:2]), sem="ve")

    def quant_phase(src_fn, ntile, cols, tpos, tneg, dst_fn):
        rounds = cols // 1024
        for t in range(ntile):
            bb = w3s[t % 2][:, 0:cols]
            if t >= 2:
                p.wait("sync", "se", None)
            p.dma("sync", bb.rearrange("p (k d) -> p k d", d=1024),
                  src_fn(t), "lx")
            p.wait("scalar", "lx", p.cnt["lx"])
            p.wait("scalar", "ve", None)
            if t >= 1:
                p.wait("scalar", "ge", None)
                p.wait("scalar", "pe", None)
            sa_, sb_ = sgna[:, 0:cols], sgnb[:, 0:cols]
            p.op("scalar", lambda e, bb=bb, sa_=sa_, tn=tneg: e.activation(
                out=sa_, in_=bb, func=AF.Sign, bias=thr[:, tn:tn + 1]), sem="se")
            p.op("scalar", lambda e, bb=bb, sb_=sb_, tp=tpos: e.activation(
                out=sb_, in_=bb, func=AF.Sign, bias=thr[:, tp:tp + 1]), sem="se")
            p.wait("vector", "se", None)
            if t >= 1:
                p.wait("vector", "pe", None)
            p.op("vector", lambda e, sa_=sa_, sb_=sb_: e.tensor_tensor(
                sa_, sa_, sb_, ALU.add), sem="ge")
            for r in range(rounds):
                p.wait("tensor", "ge" if r == 0 else "ve", None)
                for j in range(8):
                    p.op("tensor", lambda e, r=r, j=j, sa_=sa_: e.transpose(
                        ps_tr[:, j * P:(j + 1) * P],
                        sa_[:, (r * 8 + j) * P:(r * 8 + j + 1) * P], id_sb),
                        sem="pe" if j == 7 else None)
                p.wait("vector", "pe", None)
                p.op("vector", lambda e, t=t, r=r: e.tensor_copy(
                    dst_fn(t, r), ps_tr.rearrange("p (a b) -> p a b", a=8)),
                    sem="ve")

    wfc_r = wfc.rearrange("(a p) d -> p a d", p=P)
    gamma_phase(lambda g: wfc_r[:, 4 * g:4 * (g + 1), :], 8, 4096, (0, 1))
    quant_phase(lambda t: wfc_r[:, t:t + 1, :], 32, 1024, 0, 1,
                lambda t, r: wfcT[:, :, t * P:(t + 1) * P])
    wpr_r = wproj.rearrange("(a p) h -> p a h", p=P)
    gamma_phase(lambda g: wpr_r[:, g:g + 1, :].rearrange(
        "p a (k d) -> p (a k) d", d=1024), 8, 4096, (2, 3))
    quant_phase(lambda t: wpr_r[:, t:t + 1, :].rearrange(
        "p a (k d) -> p (a k) d", d=1024), 8, 4096, 2, 3,
                lambda t, r: wprT[:, r * 8:(r + 1) * 8, t * P:(t + 1) * P])

    p.wait("vector", "lg", cload)
    p.wait("vector", "se", None)
    p.wait("tensor", "se", None)
    for piece in range(16):
        src = lnrow[0:1, piece * 512:(piece + 1) * 512]
        dst = (lnwbc if piece < 8 else lnbbc)
        doff = (piece % 8) * 512
        p.wait("tensor", "ve", None)
        p.op("tensor", lambda e, src=src: e.matmul(
            ps_sc.bitcast(BF16)[:, 0:512] if False else ps_fc[:, 0:512],
            ones16, src, start=True, stop=True, skip_group_check=True),
            sem="pe")
        p.wait("vector", "pe", None)
        p.op("vector", lambda e, dst=dst, doff=doff: e.tensor_copy(
            dst[:, doff:doff + 512], ps_fc[:, 0:512]), sem="ve")

    c = lambda i: stt[:, i:i + 1]
    f_hist = {}
    p.wait("sync", "pe", None)
    p.wait("sync", "se", None)
    p.wait("sync", "ge", None)
    p.wait("sync", "ve", None)
    def emit_fc_mm(t, half, xnb):
        for dj in range(8):
            for sub in range(4):
                p.op("tensor", lambda e, dj=dj, sub=sub, half=half, xnb=xnb:
                     e.matmul(ps_fc[:, sub * 512:(sub + 1) * 512],
                              xnb[:, dj, :],
                              wfcT[:, dj, half * 2048 + sub * 512:
                                   half * 2048 + (sub + 1) * 512],
                              start=(dj == 0), stop=(dj == 7),
                              skip_group_check=True),
                     sem="pe" if (dj == 7 and sub == 3) else None)
        return p.cnt["pe"]

    def emit_gelu(t, half, pewait):
        p.wait("scalar", "pe", pewait)
        for gg in range(2):
            p.op("scalar", lambda e, half=half, gg=gg: e.activation(
                out=h_sb[:, half * 2048 + gg * 1024:
                         half * 2048 + (gg + 1) * 1024],
                in_=ps_fc[:, gg * 1024:(gg + 1) * 1024], func=AF.Gelu,
                scale=0.5,
                accum_out=stt[:, 8 + half * 2 + gg:9 + half * 2 + gg]),
                sem="se")
        return p.cnt["se"]

    def emit_proj(t):
        p.wait("tensor", "ve", f_hist[(t, "hT")])
        if t >= 2:
            p.wait("tensor", "ve", f_hist[(t - 2, "evac")])
        for hc in range(32):
            for half in range(2):
                p.op("tensor", lambda e, hc=hc, half=half: e.matmul(
                    ps_pr[:, half * 512:(half + 1) * 512], hT[:, hc, :],
                    wprT[:, hc, half * 512:(half + 1) * 512],
                    start=(hc == 0), stop=(hc == 31), skip_group_check=True),
                    sem="pe" if (hc == 31 and half == 1) else None)
        f_hist[(t, "lastmm")] = p.cnt["pe"]
        p.wait("vector", "pe", f_hist[(t, "lastmm")])
        p.wait("vector", "lx", f_hist[(t, "ldv")])
        if t >= 2:
            p.wait("vector", "st", None)
        p.op("vector", lambda e, t=t: e.scalar_tensor_tensor(
            out=x1s[t % 2], in0=ps_pr, scalar=0.5, in1=xrb[t % 2],
            op0=ALU.mult, op1=ALU.add), sem="ve")
        f_hist[(t, "evac")] = p.cnt["ve"]
        p.wait("gpsimd", "ve", None)
        p.dma("gpsimd", o_x1[t], x1s[t % 2], "st")

    def emit_loads(t):
        if t >= 2:
            p.wait("sync", "pe", f_hist[(t - 2, "fcmm")])
            p.wait("sync", "ve", f_hist[(t - 2, "evac")])
        p.dma("sync", xnf[t % 2],
              xnfm.rearrange("a p t -> p a t")[:, :, t * P:(t + 1) * P], "lx")
        p.dma("sync", xrb[t % 2], xr[t], "lx")
        f_hist[(t, "ldv")] = p.cnt["lx"]

    emit_loads(0)
    p.wait("tensor", "lx", f_hist[(0, "ldv")])
    f0c = emit_fc_mm(0, 0, xnf[0])
    gel0 = emit_gelu(0, 0, f0c)
    for t in range(NT):
        xnb = xnf[t % 2]
        p.wait("tensor", "se", gel0)
        fcmm = emit_fc_mm(t, 1, xnb)
        f_hist[(t, "fcmm")] = fcmm
        gel1 = emit_gelu(t, 1, fcmm)
        f_hist[(t, "gel1")] = gel1
        if t >= 1:
            emit_proj(t - 1)
        p.wait("vector", "se", None)
        p.op("vector", lambda e: e.reduce_sum(c(0), stt[:, 8:12], axis=AX.X),
             sem="vg")
        p.wait("scalar", "vg", None)
        for gg in range(4):
            p.op("scalar", lambda e, gg=gg: e.activation(
                out=sqscr,
                in_=h_sb[:, gg * 1024:(gg + 1) * 1024], func=AF.Square,
                accum_out=stt[:, 8 + gg:9 + gg]), sem="se")
        f_hist[(t, "sq")] = p.cnt["se"]
        p.wait("vector", "se", None)
        p.op("vector", lambda e: e.reduce_sum(c(1), stt[:, 8:12], axis=AX.X),
             sem="vg")
        p.wait("vector", "vg", None)
        p.op("vector", lambda e: e.tensor_scalar(
            out=c(2), in0=c(0), scalar1=1.0 / 4096, scalar2=None,
            op0=ALU.mult), sem="vg")
        p.wait("vector", "vg", None)
        p.op("vector", lambda e: e.tensor_scalar(
            out=c(3), in0=c(1), scalar1=1.0 / 4096, scalar2=None,
            op0=ALU.mult), sem="vg")
        p.wait("vector", "vg", None)
        p.op("vector", lambda e: e.tensor_tensor(c(4), c(2), c(2), ALU.mult),
             sem="vg")
        p.wait("vector", "vg", None)
        p.op("vector", lambda e: e.tensor_tensor(c(5), c(3), c(4), ALU.subtract),
             sem="vg")
        p.wait("scalar", "vg", None)
        p.op("scalar", lambda e: e.activation(
            out=c(6), in_=c(5), func=AF.Sqrt, bias=epsc[:, 0:1]), sem="se")
        p.wait("vector", "se", None)
        p.op("vector", lambda e: e.reciprocal(c(6), c(6)), sem="vg")
        p.wait("vector", "vg", None)
        p.op("vector", lambda e: e.scalar_tensor_tensor(
            out=c(7), in0=c(2), scalar=-1.0, in1=c(6), op0=ALU.mult,
            op1=ALU.mult), sem="vg")
        p.wait("scalar", "vg", None)
        p.op("scalar", lambda e: e.activation(
            out=h_sb, in_=h_sb, func=AF.Identity, scale=c(6), bias=c(7)),
            sem="se")
        p.wait("vector", "se", None)
        p.op("vector", lambda e: e.tensor_tensor(h_sb, h_sb, lnwbc, ALU.mult),
             sem="ve")
        p.op("vector", lambda e: e.tensor_add(h_sb, h_sb, lnbbc), sem="ve")
        if t + 1 < NT:
            emit_loads(t + 1)
            p.wait("tensor", "lx", f_hist[(t + 1, "ldv")])
            p.wait("tensor", "se", gel1)
            f_hist[(t + 1, "f0c")] = emit_fc_mm(t + 1, 0, xnf[(t + 1) % 2])
        for r in range(4):
            p.wait("tensor", "ve", None)
            for j in range(8):
                p.op("tensor", lambda e, r=r, j=j: e.transpose(
                    ps_tr[:, j * P:(j + 1) * P],
                    h_sb[:, (r * 8 + j) * P:(r * 8 + j + 1) * P], id_sb),
                    sem="pe" if j == 7 else None)
            p.wait("vector", "pe", None)
            p.op("vector", lambda e, r=r: e.tensor_copy(
                hT[:, r * 8:(r + 1) * 8, :],
                ps_tr.rearrange("p (a b) -> p a b", a=8)), sem="ve")
        f_hist[(t, "hT")] = p.cnt["ve"]
        if t + 1 < NT:
            gel0 = emit_gelu(t + 1, 0, max(f_hist[(t + 1, "f0c")],
                                           p.cnt["pe"]))
    emit_proj(NT - 1)
    p.wait("gpsimd", "st", None)
    p.free_prefix("d:")


def _pa_np():
    pa0 = np.zeros((8, P, P), BF16NP)
    pa1 = np.zeros((8, P, P), BF16NP)
    pb = np.zeros((8, P, P), BF16NP)
    for h in range(8):
        for a in range(P):
            pa0[h, 16 * h + a // 16, a] = 1
            pa1[h, 16 * h + 8 + a // 16, a] = 1
            pb[h, 16 * h + a % 16, a] = 1
    return pa0, pa1, pb


def emit_e(p, x1, gamt, wq1, wkv1, o_xn2, o_qx, o_kvp, ident, c_pa0, c_pa1,
           c_pb):
    xb = [p.sb(f"e:xb{i}", [P, D], F32) for i in range(2)]
    xn16 = [p.sb(f"e:xn16{i}", [P, D], BF16) for i in range(2)]
    nr16 = [p.sb(f"e:nr16{i}", [P, D], BF16) for i in range(2)]
    xn2fm = p.sb("e:xn2fm_sb", [P, 8, HT], BF16)
    nrmfm = p.sb("e:nrmfm_sb", [P, 8, HT], BF16)
    q_sb = p.sb("e:q_sb", [P, HT], BF16)
    k_sb = p.sb("e:k_sb", [P, HT], BF16)
    v_sb = p.sb("e:v_sb", [P, HT], BF16)
    k_tm = p.sb("e:k_tm", [P, NT, P], BF16)
    vaug = p.sb("e:vaug_e", [P, NT, 8, 17], BF16)
    kexp = p.sb("e:kexp", [P, NT, 8, 273], BF16)
    qc0 = [p.sb(f"e:qc0_{i}", [P, HT], BF16) for i in range(2)]
    qx = [p.sb(f"e:qx_{i}", [P, HT], BF16) for i in range(2)]
    kvp_sb = p.sb("e:kvp_sb", [P, 8, 3, 17], F32)
    ab_scr = p.sb("e:ab_scr", [P, 512], BF16)
    wq1_sb = p.sb("e:wq1_sb", [P, 8, P], BF16)
    wkv1_sb = p.sb("e:wkv1_sb", [P, 8, 256], BF16)
    gam_sb = p.sb("e:gam_sb", [P, 8], F32)
    pa0_sb = p.sb("e:pa0_sb", [P, 8, P], BF16)
    pa1_sb = p.sb("e:pa1_sb", [P, 8, P], BF16)
    pb_sb = p.sb("e:pb_sb", [P, 8, P], BF16)
    id_sb = p.sb("e:id_sb_e", [P, P], BF16)
    stt = p.sb("e:stt_e", [P, 16], F32)
    sqscr = p.sb("e:sqscr_e", [P, D], F32)
    epsc = p.sb("e:epsc_e", [P, 1], F32)
    ps_tr = p.psT
    ps_mm = [p.psA[:, i * 512:(i + 1) * 512] for i in range(2)]
    ps_ab = [p.psA[:, 1024 + i * 512:1024 + (i + 1) * 512] for i in range(2)]

    p.dma("sync", id_sb, ident, "lx")
    p.dma("gpsimd", wq1_sb, wq1.rearrange("(a p) k -> p a k", p=P), "lg")
    p.dma("gpsimd", wkv1_sb, wkv1.rearrange("(a p) k -> p a k", p=P), "lg")
    p.dma("gpsimd", gam_sb, gamt, "lg")
    p.dma("gpsimd", pa0_sb, c_pa0.rearrange("a p k -> p a k"), "lg")
    p.dma("gpsimd", pa1_sb, c_pa1.rearrange("a p k -> p a k"), "lg")
    p.dma("gpsimd", pb_sb, c_pb.rearrange("a p k -> p a k"), "lg")
    p.op("gpsimd", lambda e: e.memset(epsc, 1e-5), sem="ve")
    p.op("gpsimd", lambda e: e.memset(vaug[:, :, :, 16:17], 1.0), sem="ve")
    p.op("gpsimd", lambda e: e.memset(kexp[:, :, :, 0:1], 1.0), sem="ve")
    p.op("gpsimd", lambda e: e.memset(qc0[0], 0.0), sem="ve")
    p.op("gpsimd", lambda e: e.memset(qc0[1], 0.0), sem="ve")
    p.op("gpsimd", lambda e: e.memset(qc0[0][0:1, :], 1.0), sem="ve")
    p.op("gpsimd", lambda e: e.memset(qc0[1][0:1, :], 1.0), sem="ve")
    p.op("gpsimd", lambda e: e.memset(kvp_sb.rearrange("p a b c -> p (a b c)"),
                                      0.0), sem="ve")
    wload = p.cnt["lg"]
    p.wait("vector", "lg", wload)
    p.op("vector", lambda e: e.tensor_tensor(
        wq1_sb, wq1_sb, gam_sb[:, :, None].broadcast_to([P, 8, P]),
        ALU.mult), sem="vg")
    p.wait("vector", "vg", None)
    p.op("vector", lambda e: e.tensor_tensor(
        wkv1_sb, wkv1_sb, gam_sb[:, :, None].broadcast_to([P, 8, 256]),
        ALU.mult), sem="vg")
    gfold = p.cnt["vg"]

    c = lambda i: stt[:, i:i + 1]
    for i in range(NT):
        buf, xn, nr = xb[i % 2], xn16[i % 2], nr16[i % 2]
        if i >= 2:
            p.wait("sync", "se", None)
        p.dma("sync", buf, x1[i], "lx")
        ldv = p.cnt["lx"]
        p.wait("vector", "lx", ldv)
        if i > 0:
            p.wait("vector", "se", None)
        p.op("vector", lambda e, b=buf: e.reduce_sum(c(0), b, axis=AX.X),
             sem="vg")
        p.wait("scalar", "lx", ldv)
        p.wait("scalar", "vg", None)
        p.op("scalar", lambda e, b=buf: e.activation(
            out=sqscr, in_=b, func=AF.Square, accum_out=c(1)), sem="se")
        p.wait("vector", "se", None)
        p.op("vector", lambda e: e.tensor_scalar(
            out=c(2), in0=c(0), scalar1=1.0 / D, scalar2=None, op0=ALU.mult),
            sem="vg")
        p.wait("vector", "vg", None)
        p.op("vector", lambda e: e.tensor_scalar(
            out=c(3), in0=c(1), scalar1=1.0 / D, scalar2=None, op0=ALU.mult),
            sem="vg")
        p.wait("vector", "vg", None)
        p.op("vector", lambda e: e.tensor_tensor(c(4), c(2), c(2), ALU.mult),
             sem="vg")
        p.wait("vector", "vg", None)
        p.op("vector", lambda e: e.tensor_tensor(c(5), c(3), c(4), ALU.subtract),
             sem="vg")
        p.wait("scalar", "vg", None)
        p.op("scalar", lambda e: e.activation(
            out=c(6), in_=c(5), func=AF.Sqrt, bias=epsc[:, 0:1]), sem="se")
        p.wait("vector", "se", None)
        p.op("vector", lambda e: e.reciprocal(c(6), c(6)), sem="vg")
        p.wait("vector", "vg", None)
        p.op("vector", lambda e: e.scalar_tensor_tensor(
            out=c(7), in0=c(2), scalar=-1.0, in1=c(6), op0=ALU.mult,
            op1=ALU.mult), sem="vg")
        p.wait("vector", "vg", None)
        p.op("vector", lambda e: e.tensor_scalar(
            out=c(8), in0=c(5), scalar1=1e-5, scalar2=None, op0=ALU.add),
            sem="vg")
        p.wait("vector", "vg", None)
        p.op("vector", lambda e: e.reciprocal(c(8), c(8)), sem="vg")
        p.wait("vector", "vg", None)
        p.op("vector", lambda e: e.tensor_tensor(c(9), c(5), c(8), ALU.mult),
             sem="vg")
        p.wait("vector", "vg", None)
        p.op("vector", lambda e: e.tensor_scalar(
            out=c(9), in0=c(9), scalar1=9.765625e-28, scalar2=None,
            op0=ALU.max), sem="vg")
        p.wait("scalar", "vg", None)
        p.op("scalar", lambda e: e.activation(
            out=c(10), in_=c(9), func=AF.Sqrt), sem="se")
        p.wait("vector", "se", None)
        p.op("vector", lambda e: e.reciprocal(c(10), c(10)), sem="vg")
        p.wait("vector", "vg", None)
        p.op("vector", lambda e: e.tensor_tensor(c(11), c(6), c(10), ALU.mult),
             sem="vg")
        p.wait("vector", "vg", None)
        p.op("vector", lambda e: e.tensor_tensor(c(12), c(7), c(10), ALU.mult),
             sem="vg")
        p.wait("scalar", "vg", None)
        if i >= 2:
            p.wait("scalar", "pe", p.cnt["pe"] - 2)
        p.op("scalar", lambda e, b=buf, o=xn: e.activation(
            out=o, in_=b, func=AF.Identity, scale=c(6), bias=c(7)), sem="se")
        p.op("scalar", lambda e, b=buf, o=nr: e.activation(
            out=o, in_=b, func=AF.Identity, scale=c(11), bias=c(12)), sem="se")
        p.wait("tensor", "se", None)
        if i > 0:
            p.wait("tensor", "ve", None)
        for j in range(8):
            p.op("tensor", lambda e, xn=xn, j=j: e.transpose(
                ps_tr[:, j * P:(j + 1) * P], xn[:, j * P:(j + 1) * P], id_sb),
                sem="pe" if j == 7 else None)
        p.wait("vector", "pe", None)
        p.op("vector", lambda e, i=i: e.tensor_copy(
            xn2fm[:, :, i * P:(i + 1) * P],
            ps_tr.rearrange("p (a b) -> p a b", a=8)), sem="ve")
        p.wait("tensor", "ve", None)
        for j in range(8):
            p.op("tensor", lambda e, nr=nr, j=j: e.transpose(
                ps_tr[:, j * P:(j + 1) * P], nr[:, j * P:(j + 1) * P], id_sb),
                sem="pe" if j == 7 else None)
        p.wait("vector", "pe", None)
        p.op("vector", lambda e, i=i: e.tensor_copy(
            nrmfm[:, :, i * P:(i + 1) * P],
            ps_tr.rearrange("p (a b) -> p a b", a=8)), sem="ve")
    p.wait("gpsimd", "ve", None)
    for a in range(8):
        p.dma("gpsimd", o_xn2[a], xn2fm[:, a, :], "st")

    mmctr = {"n": 0}

    def mm_block(chunks, dest_fn, evac_eng="vector"):
        k = mmctr["n"]
        buf = ps_mm[k % 2]
        if k >= 2:
            p.wait("tensor", "ve", mmctr[f"evac{k - 2}"])
        for ci, (lh, rh) in enumerate(chunks):
            p.op("tensor", lambda e, lh=lh, rh=rh, st=(ci == 0),
                 sp=(ci == len(chunks) - 1), b=buf: e.matmul(
                     b, lh, rh, start=st, stop=sp, skip_group_check=True),
                 sem="pe" if ci == len(chunks) - 1 else None)
        p.wait(evac_eng, "pe", None)
        p.op(evac_eng, lambda e, b=buf: dest_fn(e, b), sem="ve")
        mmctr[f"evac{k}"] = p.cnt["ve"]
        mmctr["n"] += 1

    p.wait("tensor", "vg", gfold)
    p.wait("tensor", "ve", None)
    for sl in range(4):
        chunks = [(wq1_sb[:, dj, :], nrmfm[:, dj, sl * 512:(sl + 1) * 512])
                  for dj in range(8)]
        mm_block(chunks, lambda e, b, sl=sl: e.activation(
            out=q_sb[:, sl * 512:(sl + 1) * 512], in_=b, func=AF.Copy,
            scale=0.25), evac_eng="scalar")
    for c2 in range(2):
        dst = k_sb if c2 == 0 else v_sb
        for sl in range(4):
            chunks = [(wkv1_sb[:, dj, c2 * P:(c2 + 1) * P],
                       nrmfm[:, dj, sl * 512:(sl + 1) * 512])
                      for dj in range(8)]
            mm_block(chunks, lambda e, b, dst=dst, sl=sl: e.tensor_copy(
                dst[:, sl * 512:(sl + 1) * 512], b))

    p.wait("tensor", "ve", None)
    p.wait("tensor", "se", None)
    trcp_hist = {}
    for t in range(NT):
        if t >= 1:
            p.wait("tensor", "ve", trcp_hist[t - 1])
        p.op("tensor", lambda e, t=t: e.transpose(
            ps_tr[:, 0:P], k_sb[:, t * P:(t + 1) * P], id_sb), sem=None)
        p.op("tensor", lambda e, t=t: e.transpose(
            ps_tr[:, P:2 * P], v_sb[:, t * P:(t + 1) * P], id_sb), sem="pe")
        p.wait("vector", "pe", None)
        p.op("vector", lambda e, t=t: e.tensor_copy(
            k_tm[:, t, :], ps_tr[:, 0:P]), sem="ve")
        p.op("vector", lambda e, t=t: e.tensor_copy(
            vaug[:, t, :, 0:16],
            ps_tr[:, P:2 * P].rearrange("p (h v) -> p h v", h=8)), sem="ve")
        trcp_hist[t] = p.cnt["ve"]
        p.wait("vector", "ve", None)
        p.op("vector", lambda e, t=t: e.tensor_copy(
            kexp[:, t, :, 1:17],
            k_tm[:, t, :].rearrange("p (h v) -> p h v", h=8)), sem="ve")
        for h in range(8):
            p.op("vector", lambda e, t=t, h=h: e.scalar_tensor_tensor(
                out=kexp[:, t, h, 17:273].rearrange("p (a b) -> p a b", a=16),
                in0=k_tm[:, t, h * 16:(h + 1) * 16][:, :, None]
                .broadcast_to([P, 16, 16]),
                scalar=0.7071067811865476,
                in1=k_tm[:, t, None, h * 16:(h + 1) * 16]
                .broadcast_to([P, 16, 16]),
                op0=ALU.mult, op1=ALU.mult),
                sem="ve" if h == 7 else None)
    kexp_done = p.cnt["ve"]

    csl = [(0, 17), (17, 145), (145, 273)]
    p.wait("tensor", "ve", kexp_done)
    grp = 0
    for h in range(8):
        for ci, (c0_, c1_) in enumerate(csl):
            buf = ps_mm[grp % 2]
            if grp >= 2:
                p.wait("tensor", "ve", mmctr[f"kv{grp - 2}"])
            for t in range(NT):
                p.op("tensor", lambda e, t=t, h=h, c0_=c0_, c1_=c1_, buf=buf:
                     e.matmul(buf[0:c1_ - c0_, 0:17],
                              kexp[:, t, h, c0_:c1_], vaug[:, t, h, :],
                              start=(t == 0), stop=(t == NT - 1),
                              skip_group_check=True),
                     sem="pe" if t == NT - 1 else None)
            p.wait("vector", "pe", None)
            p.op("vector", lambda e, h=h, ci=ci, buf=buf: e.tensor_copy(
                kvp_sb[:, h, ci, :], buf[:, 0:17]), sem="ve")
            mmctr[f"kv{grp}"] = p.cnt["ve"]
            grp += 1
    p.wait("gpsimd", "ve", None)
    p.dma("gpsimd", o_kvp.rearrange("h (c p) e -> p h c e", p=P), kvp_sb, "st")

    qdma_hist = {}
    for h in range(8):
        qb = qc0[h % 2]
        if h >= 2:
            p.wait("gpsimd", "st", qdma_hist[h - 2])
        p.wait("gpsimd", "ve", mmctr["evac3"])
        p.dma("gpsimd", qb[1:17, :], q_sb[16 * h:16 * h + 16, :], "lg")
        p.wait("gpsimd", "lg", None)
        p.dma("gpsimd", o_qx[h, 0], qb, "st")
        qdma_hist[h] = p.cnt["st"]
    for h in range(8):
        for tile in range(2):
            pa = pa0_sb if tile == 0 else pa1_sb
            qxb = qx[(h * 2 + tile) % 2]
            if h * 2 + tile >= 2:
                p.wait("vector", "st", qdma_hist[(h * 2 + tile - 2, "qx")])
            for sl in range(4):
                p.wait("tensor", "ve", None)
                p.op("tensor", lambda e, pa=pa, h=h, sl=sl: e.matmul(
                    ps_ab[0], pa[:, h, :], q_sb[:, sl * 512:(sl + 1) * 512],
                    start=True, stop=True, skip_group_check=True), sem=None)
                p.op("tensor", lambda e, h=h, sl=sl: e.matmul(
                    ps_ab[1], pb_sb[:, h, :], q_sb[:, sl * 512:(sl + 1) * 512],
                    start=True, stop=True, skip_group_check=True), sem="pe")
                p.wait("vector", "pe", None)
                p.op("vector", lambda e: e.tensor_copy(
                    ab_scr, ps_ab[0]), sem="ve")
                p.op("vector", lambda e, qxb=qxb, sl=sl: e.scalar_tensor_tensor(
                    out=qxb[:, sl * 512:(sl + 1) * 512], in0=ab_scr,
                    scalar=0.7071067811865476, in1=ps_ab[1],
                    op0=ALU.mult, op1=ALU.mult), sem="ve")
            p.wait("gpsimd", "ve", None)
            p.dma("gpsimd", o_qx[h, 1 + tile], qxb, "st")
            qdma_hist[(h * 2 + tile, "qx")] = p.cnt["st"]

    p.wait("gpsimd", "st", None)
    p.free_prefix("e:")


def emit_f(p, qx, kvp2, wo1, o_at, ident):
    qx_sb = p.sb("f:qx_sb", [P, 8, 3, HT], BF16)
    kva = p.sb("f:kva", [P, 8, 3, 17], F32)
    kvb = p.sb("f:kvb", [P, 8, 3, 17], F32)
    kv16 = p.sb("f:kv16", [P, 8, 3, 17], BF16)
    wo_sb = p.sb("f:wo_sb_f", [P, D], BF16)
    id_sb = p.sb("f:id_sb_f", [P, P], BF16)
    o1 = [p.sb(f"f:o1_{i}", [P, P], BF16) for i in range(2)]
    o1t = [p.sb(f"f:o1t_{i}", [P, P], BF16) for i in range(2)]
    dn = p.sb("f:dn", [P, 8], F32)
    atstg = [p.sb(f"f:atstg{i}", [P, D], BF16) for i in range(2)]
    ps_o = [p.psA[:, 0:136], p.psA[:, 512:648]]
    ps_tr = p.psT[:, 0:P]
    ps_w = p.psB

    p.dma("sync", qx_sb, qx.rearrange("h c p t -> p h c t"), "lx")
    p.dma("sync", id_sb, ident, "lx")
    p.dma("sync", kva, kvp2[0].rearrange("h (c p) e -> p h c e", p=P), "lx")
    p.dma("sync", kvb, kvp2[1].rearrange("h (c p) e -> p h c e", p=P), "lx")
    p.dma("gpsimd", wo_sb, wo1, "lg")
    lall = p.cnt["lx"]
    p.wait("vector", "lx", lall)
    p.op("vector", lambda e: e.tensor_add(kv16, kva, kvb), sem="vg")
    p.wait("tensor", "vg", None)
    p.wait("tensor", "lg", 16)

    csl = [(0, 17), (17, 145), (145, 273)]
    ev_hist = {}
    for t in range(NT):
        po = ps_o[t % 2]
        if t >= 2:
            p.wait("tensor", "ve", ev_hist[t - 2])
        first = True
        for h in range(8):
            for ci, (c0_, c1_) in enumerate(csl):
                p.op("tensor", lambda e, h=h, ci=ci, c0_=c0_, c1_=c1_, t=t,
                     po=po, st=first, sp=(h == 7 and ci == 2): e.matmul(
                         po[:, h * 17:(h + 1) * 17],
                         qx_sb[0:c1_ - c0_, h, ci, t * P:(t + 1) * P],
                         kv16[0:c1_ - c0_, h, ci, :],
                         start=st, stop=sp, skip_group_check=True),
                     sem="pe" if (h == 7 and ci == 2) else None)
                first = False
        p.wait("vector", "pe", None)
        p.op("vector", lambda e, po=po: e.tensor_scalar(
            out=dn[:, 0:8],
            in0=po.rearrange("p (h v) -> p h v", v=17)[:, :, 16:17]
            .rearrange("p h v -> p (h v)"),
            scalar1=1e-5, scalar2=None, op0=ALU.max), sem="vg")
        p.wait("vector", "vg", None)
        p.op("vector", lambda e: e.reciprocal(dn[:, 0:8], dn[:, 0:8]),
             sem="vg")
        p.wait("vector", "vg", None)
        p.op("vector", lambda e, po=po, t=t: e.tensor_tensor(
            o1[t % 2].rearrange("p (h v) -> p h v", h=8),
            po.rearrange("p (h v) -> p h v", v=17)[:, :, 0:16],
            dn[:, :, None].broadcast_to([P, 8, 16]), ALU.mult), sem="ve")
        ev_hist[t] = p.cnt["ve"]
        p.wait("tensor", "ve", None)
        p.op("tensor", lambda e, t=t: e.transpose(
            ps_tr, o1[t % 2], id_sb), sem="pe")
        p.wait("vector", "pe", None)
        p.op("vector", lambda e, t=t: e.tensor_copy(o1t[t % 2], ps_tr),
             sem="ve")
        p.wait("tensor", "ve", None)
        for half in range(2):
            p.op("tensor", lambda e, t=t, half=half: e.matmul(
                ps_w[:, half * 512:(half + 1) * 512], o1t[t % 2],
                wo_sb[:, half * 512:(half + 1) * 512], start=True, stop=True,
                skip_group_check=True), sem="pe" if half == 1 else None)
        p.wait("vector", "pe", None)
        if t >= 2:
            p.wait("vector", "st", None)
        p.op("vector", lambda e, t=t: e.tensor_copy(atstg[t % 2], ps_w),
             sem="ve")
        p.wait("gpsimd", "ve", None)
        p.dma("gpsimd", o_at[t], atstg[t % 2], "st")
    p.wait("gpsimd", "st", None)
    p.free_prefix("f:")


def emit_g(p, xn2, x1, at1, wg, wu, wd, o_out, ident):
    xn2_sb = p.sb("g:xn2_sb", [P, 8, HT], BF16)
    wdT = p.sb("g:wdT", [P, 24, D], BF16)
    hfm = p.sb("g:hfm", [P, 24, D], BF16)
    wgb = [p.sb(f"g:wgb{i}", [P, D], BF16) for i in range(2)]
    wub = [p.sb(f"g:wub{i}", [P, D], BF16) for i in range(2)]
    wdb = [p.sb(f"g:wdb{i}", [P, 3072], BF16) for i in range(2)]
    wgT = p.sb("g:wgT", [P, 8, P], BF16)
    wuT = p.sb("g:wuT", [P, 8, P], BF16)
    sg = p.sb("g:sg", [P, D], BF16)
    id_sb = p.sb("g:id_sb_g", [P, P], BF16)
    x1b = [p.sb(f"g:x1b{i}", [P, D], F32) for i in range(2)]
    atb = [p.sb(f"g:atb{i}", [P, D], BF16) for i in range(2)]
    ostg = [p.sb(f"g:ostg{i}", [P, D], F32) for i in range(2)]
    tmp = p.sb("g:tmp_g", [P, D], F32)
    ps_g = p.psA[:, 0:1024]
    ps_u = p.psA[:, 1024:2048]
    ps_tr = p.psT
    ps_d = p.psB

    p.dma("sync", id_sb, ident, "lx")
    p.dma("sync", xn2_sb, xn2.rearrange("a p t -> p a t"), "lx")
    xnld = p.cnt["lx"]

    tr_hist = {}
    for dc in range(8):
        bb = wdb[dc % 2]
        if dc >= 2:
            p.wait("gpsimd", "ve", tr_hist[dc - 2])
        p.dma("gpsimd", bb, wd[dc * P:(dc + 1) * P, :], "lg")
        ldv = p.cnt["lg"]
        p.wait("tensor", "lg", ldv)
        p.wait("tensor", "lx", 16)
        for r in range(3):
            if not (dc == 0 and r == 0):
                p.wait("tensor", "ve", None)
            for j in range(8):
                p.op("tensor", lambda e, bb=bb, r=r, j=j: e.transpose(
                    ps_tr[:, j * P:(j + 1) * P],
                    bb[:, (r * 8 + j) * P:(r * 8 + j + 1) * P], id_sb),
                    sem="pe" if j == 7 else None)
            p.wait("vector", "pe", None)
            p.op("vector", lambda e, dc=dc, r=r: e.tensor_copy(
                wdT[:, r * 8:(r + 1) * 8, dc * P:(dc + 1) * P],
                ps_tr.rearrange("p (a b) -> p a b", a=8)), sem="ve")
        tr_hist[dc] = p.cnt["ve"]

    dn_hist = {}
    trpe = {}
    for half in range(2):
        if half == 1:
            p.wait("vector", "pe", None)
        for hc in range(24):
            bg, bu = wgb[hc % 2], wub[hc % 2]
            prevk = (half, hc - 2) if hc >= 2 else (half - 1, hc + 22)
            if prevk in trpe:
                p.wait("gpsimd", "pe", trpe[prevk])
            p.dma("gpsimd", bg, wg[hc * P:(hc + 1) * P, :], "lg")
            p.dma("gpsimd", bu, wu[hc * P:(hc + 1) * P, :], "lg")
            ldv = p.cnt["lg"]
            p.wait("tensor", "lg", ldv)
            p.wait("tensor", "ve", None)
            for j in range(8):
                p.op("tensor", lambda e, bg=bg, j=j: e.transpose(
                    ps_tr[:, j * P:(j + 1) * P], bg[:, j * P:(j + 1) * P],
                    id_sb), sem="pe" if j == 7 else None)
            p.wait("vector", "pe", None)
            p.op("vector", lambda e: e.tensor_copy(
                wgT, ps_tr.rearrange("p (a b) -> p a b", a=8)), sem="ve")
            p.wait("tensor", "ve", None)
            for j in range(8):
                p.op("tensor", lambda e, bu=bu, j=j: e.transpose(
                    ps_tr[:, j * P:(j + 1) * P], bu[:, j * P:(j + 1) * P],
                    id_sb), sem="pe" if j == 7 else None)
            trpe[(half, hc)] = p.cnt["pe"]
            p.wait("vector", "pe", None)
            p.op("vector", lambda e: e.tensor_copy(
                wuT, ps_tr.rearrange("p (a b) -> p a b", a=8)), sem="ve")
            p.wait("tensor", "ve", None)
            if hc > 0 or half > 0:
                p.wait("tensor", "se", None)
            p.wait("tensor", "lx", xnld)
            for dj in range(8):
                for sl in range(2):
                    p.op("tensor", lambda e, dj=dj, sl=sl, half=half: e.matmul(
                        ps_g[:, sl * 512:(sl + 1) * 512], wgT[:, dj, :],
                        xn2_sb[:, dj, half * 1024 + sl * 512:
                               half * 1024 + (sl + 1) * 512],
                        start=(dj == 0), stop=(dj == 7),
                        skip_group_check=True), sem=None)
            for dj in range(8):
                for sl in range(2):
                    p.op("tensor", lambda e, dj=dj, sl=sl, half=half: e.matmul(
                        ps_u[:, sl * 512:(sl + 1) * 512], wuT[:, dj, :],
                        xn2_sb[:, dj, half * 1024 + sl * 512:
                               half * 1024 + (sl + 1) * 512],
                        start=(dj == 0), stop=(dj == 7),
                        skip_group_check=True),
                        sem="pe" if (dj == 7 and sl == 1) else None)
            p.wait("scalar", "pe", None)
            p.op("scalar", lambda e: e.activation(
                out=sg, in_=ps_g, func=AF.Silu, scale=1.0 / 127.0), sem="se")
            p.wait("vector", "se", None)
            p.op("vector", lambda e, hc=hc: e.scalar_tensor_tensor(
                out=hfm[:, hc, :], in0=ps_u, scalar=1.0 / 127.0, in1=sg,
                op0=ALU.mult, op1=ALU.mult), sem="ve")
        hdone = p.cnt["ve"]
        p.wait("tensor", "ve", hdone)
        for tt in range(8):
            t = half * 8 + tt
            if t >= 2:
                p.wait("sync", "ve", dn_hist[t - 2])
            p.dma("sync", x1b[t % 2], x1[t], "lx")
            p.dma("sync", atb[t % 2], at1[t], "lx")
            ldv = p.cnt["lx"]
            if t >= 1:
                p.wait("tensor", "ve", dn_hist[t - 1])
            for hc in range(24):
                for sl in range(2):
                    p.op("tensor", lambda e, hc=hc, sl=sl, tt=tt: e.matmul(
                        ps_d[:, sl * 512:(sl + 1) * 512],
                        hfm[:, hc, tt * P:(tt + 1) * P],
                        wdT[:, hc, sl * 512:(sl + 1) * 512],
                        start=(hc == 0), stop=(hc == 23),
                        skip_group_check=True),
                        sem="pe" if (hc == 23 and sl == 1) else None)
            p.wait("vector", "pe", None)
            p.wait("vector", "lx", ldv)
            if t >= 2:
                p.wait("vector", "st", None)
            p.op("vector", lambda e, t=t: e.scalar_tensor_tensor(
                out=tmp, in0=ps_d, scalar=1.0 / 127.0, in1=x1b[t % 2],
                op0=ALU.mult, op1=ALU.add), sem="ve")
            p.wait("vector", "ve", None)
            p.op("vector", lambda e, t=t: e.tensor_add(
                ostg[t % 2], tmp, atb[t % 2]), sem="ve")
            dn_hist[t] = p.cnt["ve"]
            p.wait("gpsimd", "ve", None)
            p.dma("gpsimd", o_out[t], ostg[t % 2], "st")
    p.wait("gpsimd", "st", None)
    p.free_prefix("g:")


def build_l1():
    nc = new_nc()
    xf = nc.dram_tensor("xf", [S, D], F32, kind="ExternalInput").ap()
    wq = nc.dram_tensor("wq", [D, 512], F32, kind="ExternalInput").ap()
    wkvd = nc.dram_tensor("wkvd", [D, 256], F32, kind="ExternalInput").ap()
    wku = nc.dram_tensor("wku", [256, 512], F32, kind="ExternalInput").ap()
    wvu = nc.dram_tensor("wvu", [256, 512], F32, kind="ExternalInput").ap()
    o_xnfm = nc.dram_tensor("xnfm", [8, P, S], BF16, kind="ExternalOutput").ap()
    kt = nc.dram_tensor("kt", [4, P, S], BF16, kind="Internal").ap()
    qt = nc.dram_tensor("qt", [4, P, S], BF16, kind="Internal").ap()
    va = nc.dram_tensor("vaug", [32, P, 520], BF16, kind="Internal").ap()
    o_y = nc.dram_tensor("yfm", [4, P, S], BF16, kind="ExternalOutput").ap()
    bden = nc.dram_tensor("bden", [2, 1, 512], BF16, kind="Internal").ap()
    ident = nc.inline_tensor(_ident_np(), name="ident").ap()
    maskc = nc.inline_tensor(_mask_np(), name="maskc").ap()
    p = Prog(nc)
    emit_a(p, xf, wq, wkvd, wku, wvu, o_xnfm, kt, qt, va, ident)
    p.barrier()
    emit_b(p, kt, qt, va, o_y, maskc, bden)
    return p.build()


def build_l2():
    nc = new_nc()
    y_all = nc.dram_tensor("y_all", [8, P, HT], BF16, kind="ExternalInput").ap()
    wout = nc.dram_tensor("wout", [D, D], F32, kind="ExternalInput").ap()
    x_own = nc.dram_tensor("x_own", [HT, D], F32, kind="ExternalInput").ap()
    xr = nc.dram_tensor("xr", [NT, P, D], F32, kind="Internal").ap()
    xnfm = nc.dram_tensor("xnfm_own", [8, P, HT], BF16, kind="ExternalInput").ap()
    wfc = nc.dram_tensor("wfc", [4096, D], F32, kind="ExternalInput").ap()
    lnw = nc.dram_tensor("lnw", [1, 4096], F32, kind="ExternalInput").ap()
    lnb = nc.dram_tensor("lnb", [1, 4096], F32, kind="ExternalInput").ap()
    wproj = nc.dram_tensor("wproj", [D, 4096], F32, kind="ExternalInput").ap()
    o_x1 = nc.dram_tensor("x1", [NT, P, D], F32, kind="ExternalOutput").ap()
    gamt = nc.dram_tensor("gamt", [P, 8], F32, kind="ExternalInput").ap()
    wq1 = nc.dram_tensor("wq1", [D, P], F32, kind="ExternalInput").ap()
    wkv1 = nc.dram_tensor("wkv1", [D, 256], F32, kind="ExternalInput").ap()
    o_xn2 = nc.dram_tensor("xn2fm", [8, P, HT], BF16, kind="ExternalOutput").ap()
    o_qx = nc.dram_tensor("qexp", [8, 3, P, HT], BF16, kind="ExternalOutput").ap()
    o_kvp = nc.dram_tensor("kvp", [8, 384, 17], F32, kind="ExternalOutput").ap()
    ident = nc.inline_tensor(_ident_np(), name="ident").ap()
    id32 = nc.inline_tensor(np.eye(P, dtype=np.float32), name="id32").ap()
    pa0, pa1, pb = _pa_np()
    c_pa0 = nc.inline_tensor(pa0, name="pa0").ap()
    c_pa1 = nc.inline_tensor(pa1, name="pa1").ap()
    c_pb = nc.inline_tensor(pb, name="pb").ap()
    p = Prog(nc)
    emit_c(p, y_all, wout, x_own, xr)
    p.barrier()
    emit_d(p, xr, xnfm, wfc, lnw, lnb, wproj, o_x1, ident, id32)
    p.barrier()
    emit_e(p, o_x1, gamt, wq1, wkv1, o_xn2, o_qx, o_kvp, ident, c_pa0, c_pa1,
           c_pb)
    return p.build()


def build_l3():
    nc = new_nc()
    qx = nc.dram_tensor("qexp", [8, 3, P, HT], BF16, kind="ExternalInput").ap()
    kvp2 = nc.dram_tensor("kvp2", [2, 8, 384, 17], F32, kind="ExternalInput").ap()
    wo1 = nc.dram_tensor("wout1", [P, D], F32, kind="ExternalInput").ap()
    at1 = nc.dram_tensor("at1", [NT, P, D], BF16, kind="Internal").ap()
    xn2 = nc.dram_tensor("xn2fm", [8, P, HT], BF16, kind="ExternalInput").ap()
    x1 = nc.dram_tensor("x1", [NT, P, D], F32, kind="ExternalInput").ap()
    wg = nc.dram_tensor("wg", [3072, D], I32, kind="ExternalInput").ap()
    wu = nc.dram_tensor("wu", [3072, D], I32, kind="ExternalInput").ap()
    wd = nc.dram_tensor("wd", [D, 3072], I32, kind="ExternalInput").ap()
    o_out = nc.dram_tensor("out", [NT, P, D], F32, kind="ExternalOutput").ap()
    ident = nc.inline_tensor(_ident_np(), name="ident").ap()
    p = Prog(nc)
    emit_f(p, qx, kvp2, wo1, at1, ident)
    p.barrier()
    emit_g(p, xn2, x1, at1, wg, wu, wd, o_out, ident)
    return p.build()


def _bf16(x):
    return np.ascontiguousarray(x, dtype=BF16NP)


def _f32(x):
    return np.ascontiguousarray(x, dtype=np.float32)


def kernel(**inputs):
    x = _f32(inputs["L0_attn_x_bld"])
    wq_full = _f32(inputs["L0_attn_Wq"]).reshape(D, 16 * 64)
    wkvd = _f32(inputs["L0_attn_Wkv_down"])
    wku_full = _f32(inputs["L0_attn_Wk_up"]).reshape(256, 16 * 64)
    wvu_full = _f32(inputs["L0_attn_Wv_up"]).reshape(256, 16 * 64)

    nc1 = build_l1()
    in1 = []
    for core in range(8):
        b, g = core // 2, core % 2
        hsl = slice(g * 512, (g + 1) * 512)
        in1.append({
            "xf": np.ascontiguousarray(x[b]),
            "wq": np.ascontiguousarray(wq_full[:, hsl]),
            "wkvd": wkvd,
            "wku": np.ascontiguousarray(wku_full[:, hsl]),
            "wvu": np.ascontiguousarray(wvu_full[:, hsl]),
        })
    r1 = run_bass_kernel_spmd(nc1, in1, core_ids=list(range(8))).results

    wout = _f32(inputs["L0_attn_Wout"]).reshape(D, D)
    wfc = _f32(inputs["L0_ffn_W_fc"])
    lnw = _f32(inputs["L0_ffn_ln_w"]).reshape(1, 4096)
    lnb = _f32(inputs["L0_ffn_ln_b"]).reshape(1, 4096)
    wproj = _f32(inputs["L0_ffn_W_proj"])
    gamt = _f32(np.asarray(inputs["L1_attn_gamma"], np.float32)
                .reshape(8, P).T)
    wq1 = _f32(inputs["L1_attn_w_q"])
    wkv1 = _f32(inputs["L1_attn_w_kv"])
    nc2 = build_l2()
    in2 = []
    for core in range(8):
        b, g = core // 2, core % 2
        ysl = slice(g * HT, (g + 1) * HT)
        ya = np.concatenate([np.asarray(r1[2 * b]["yfm"])[:, :, ysl],
                             np.asarray(r1[2 * b + 1]["yfm"])[:, :, ysl]], 0)
        in2.append({
            "y_all": np.ascontiguousarray(ya),
            "wout": wout,
            "x_own": np.ascontiguousarray(x[b, g * HT:(g + 1) * HT, :]),
            "xnfm_own": np.ascontiguousarray(
                np.asarray(r1[core]["xnfm"])[:, :, ysl]),
            "wfc": wfc, "lnw": lnw, "lnb": lnb, "wproj": wproj,
            "gamt": gamt, "wq1": wq1, "wkv1": wkv1,
        })
    r2 = run_bass_kernel_spmd(nc2, in2, core_ids=list(range(8))).results

    wo1 = _f32(inputs["L1_attn_w_out"])
    wg = np.ascontiguousarray(np.asarray(inputs["L1_ffn_w_gate"], np.int32))
    wu = np.ascontiguousarray(np.asarray(inputs["L1_ffn_w_up"], np.int32))
    wd = np.ascontiguousarray(np.asarray(inputs["L1_ffn_w_down"], np.int32))
    nc3 = build_l3()
    in3 = []
    for core in range(8):
        kvp2 = np.stack([np.asarray(r2[core]["kvp"]),
                         np.asarray(r2[core ^ 1]["kvp"])])
        in3.append({"qexp": np.asarray(r2[core]["qexp"]), "kvp2": kvp2,
                    "wout1": wo1, "xn2fm": np.asarray(r2[core]["xn2fm"]),
                    "x1": np.asarray(r2[core]["x1"]),
                    "wg": wg, "wu": wu, "wd": wd})
    r3 = run_bass_kernel_spmd(nc3, in3, core_ids=list(range(8))).results

    out = np.empty((B, S, D), np.float32)
    for core in range(8):
        b, g = core // 2, core % 2
        out[b, g * HT:(g + 1) * HT, :] = np.asarray(
            r3[core]["out"], np.float32).reshape(HT, D)
    return out
